# version 26
# baseline (speedup 1.0000x reference)
import contextlib
import numpy as np
import concourse.bass as bass
import concourse.mybir as mybir
from concourse.bass_utils import run_bass_kernel_spmd

F32 = mybir.dt.float32
I32 = mybir.dt.int32
AF = mybir.ActivationFunctionType
ALU = mybir.AluOpType
AX = mybir.AxisListType

NCORES = 8
L = 16384
DM = 1024
DI = 2048
NST = 16
DTR = 64
DCONV = 4
FF = 4096
GRP = 128
TSL = L // NCORES
DSL = DI // NCORES
HALO = DCONV - 1
EPS = 1e-5


class Sched:

    ENGINES = ("sync", "tensor", "vector", "scalar", "gpsimd")
    SEMS = ("sync", "tensor", "vector", "scalar", "gpsimd", "gpdma", "scdma")

    def __init__(self, nc):
        self.nc = nc
        self.stream = {e: [] for e in self.ENGINES}
        self.val = {e: 0 for e in self.SEMS}
        self.sems = {}
        self._hw = {}

    def make_sems(self, ctx):
        for e in self.SEMS:
            self.sems[e] = ctx.enter_context(self.nc.semaphore("sem_" + e))

    def step(self, eng, fn, waits=(), dma=0):
        wl = [w for w in waits if w is not None]
        sem_map = self.sems
        key = eng
        if dma and eng == "gpsimd":
            key = "gpdma"
        elif dma and eng == "scalar":
            key = "scdma"
        pre = self.val[key]
        if dma:
            wl.append((key, pre))
            self.val[key] += 16 * dma
        else:
            self.val[key] += 1

        def emit(e, wl=wl, fn=fn, eng=eng, key=key, dma=dma):
            hw = self._hw.setdefault(eng, {})
            best = {}
            for (we, wv) in wl:
                if wv > best.get(we, 0):
                    best[we] = wv
            for we, wv in best.items():
                if wv <= hw.get(we, 0):
                    continue
                hw[we] = wv
                e.wait_ge(sem_map[we], wv)
            if dma:
                fn(e, sem_map[key])
            else:
                inst = fn(e)
                inst.then_inc(sem_map[key], 1)

        self.stream[eng].append(emit)
        return (key, self.val[key])

    def finish(self, block, extra_final=()):
        final_gp = self.val["gpdma"]
        final_sync = self.val["sync"]
        final_sc = self.val["scdma"]
        for name in self.ENGINES:
            ops = self.stream[name]

            def body(e, ops=ops, name=name):
                for op in ops:
                    op(e)
                if name == "gpsimd":
                    if final_gp:
                        e.wait_ge(self.sems["gpdma"], final_gp)
                    if final_sync:
                        e.wait_ge(self.sems["sync"], final_sync)
                    if final_sc:
                        e.wait_ge(self.sems["scdma"], final_sc)
                    for (we, wv) in extra_final:
                        e.wait_ge(self.sems[we], wv)

            getattr(block, name)(body)



def build_L1(debug=False):
    TS = TSL + 2 * HALO
    NTT = (TS + 127) // 128
    NG = TS // 512
    MINI = TS - NG * 512
    NKT = DM // 128
    nc = bass.Bass("TRN2", target_bir_lowering=False, detect_race_conditions=False)

    x = nc.dram_tensor("x", [TS, DM], F32, kind="ExternalInput").ap()
    in_w = nc.dram_tensor("in_w", [2, DM, 2 * DI], F32, kind="ExternalInput").ap()
    conv_w = nc.dram_tensor("conv_w", [2, DI, DCONV], F32, kind="ExternalInput").ap()
    conv_b = nc.dram_tensor("conv_b", [2, DI], F32, kind="ExternalInput").ap()
    xp_w = nc.dram_tensor("xp_w", [2, DI, DTR + 2 * NST], F32, kind="ExternalInput").ap()
    dt_w = nc.dram_tensor("dt_w", [2, DTR, DI], F32, kind="ExternalInput").ap()
    dt_b = nc.dram_tensor("dt_b", [2, DI], F32, kind="ExternalInput").ap()

    o_dt = nc.dram_tensor("o_dt", [2, DI, TSL], F32, kind="ExternalOutput").ap()
    o_xc = nc.dram_tensor("o_xc", [2, DI, TSL], F32, kind="ExternalOutput").ap()
    o_sz = nc.dram_tensor("o_sz", [2, DI, TSL], F32, kind="ExternalOutput").ap()
    o_bc = nc.dram_tensor("o_bc", [2, 2 * NST, TSL], F32, kind="ExternalOutput").ap()
    if debug:
        d_st = nc.dram_tensor("d_st", [128, 8 + 2 * DM], F32, kind="ExternalOutput").ap()
        d_xnT = nc.dram_tensor("d_xnT", [128, 8 * (TSL + 2 * HALO)], F32, kind="ExternalOutput").ap()
        d_strip = nc.dram_tensor("d_strip", [128, TSL + 2 * HALO], F32, kind="ExternalOutput").ap()

    with contextlib.ExitStack() as ctx:
        sbt = lambda name, shape, dt=F32: ctx.enter_context(nc.sbuf_tensor(name, shape, dt))
        pst = lambda name, shape: ctx.enter_context(nc.psum_tensor(name, shape, F32))

        xnT = sbt("xnT", [128, NKT * TS])
        xtm = [sbt("xtm0", [128, DM]), sbt("xtm1", [128, DM])]
        xnb = [sbt("xnb0", [128, DM]), sbt("xnb1", [128, DM])]
        xsq = sbt("xsq", [128, DM])
        st = sbt("stats", [128, 8])
        ident = sbt("ident", [128, 128])
        io_i = sbt("io_i", [128, 128], F32)
        io_p = sbt("io_p", [128, 1], F32)
        wbuf = [sbt("wbuf0", [128, NKT * 128]), sbt("wbuf1", [128, NKT * 128])]
        strip = sbt("strip", [128, TS])
        cacc = [sbt("cacc0", [128, TSL]), sbt("cacc1", [128, TSL])]
        xc_strip = sbt("xc_strip", [128, TSL])
        cw = sbt("cw", [128, 2 * 16 * DCONV])
        cb = sbt("cb", [128, 2 * 16])
        xpw = sbt("xpw", [128, 2 * 16 * 96])
        dtw = sbt("dtw", [64, 2 * DI])
        dtb = sbt("dtb", [128, 2 * 16])
        dbc_sb = sbt("dbc_sb", [96, 2 * TSL])
        dt_sb = sbt("dt_sb", [128, TSL])
        spe = sbt("spe", [128, TSL])
        cst = sbt("cst", [128, 2])

        pz = [pst("pz0", [128, 512]), pst("pz1", [128, 512]), pst("pz2", [128, 512])]
        pzm = pst("pzm", [128, MINI])
        pdb = [pst("pdb0", [96, 512]), pst("pdb1", [96, 512]),
               pst("pdb2", [96, 512]), pst("pdb3", [96, 512])]

        s = Sched(nc)
        s.make_sems(ctx)
        block = ctx.enter_context(nc.Block())

        ttl = [min(128, TS - t * 128) for t in range(NTT)]
        bank_free = {}
        buf_free = {}

        def mkcst(e):
            e.memset(cst[:, 0:1], EPS)
            return e.memset(cst[:, 1:2], 1.0)
        m_cst = s.step("vector", mkcst)

        m_io1 = s.step("gpsimd", lambda e: e.iota(
            io_i[:, :], [[1, 128]], channel_multiplier=0,
            allow_small_or_imprecise_dtypes=True))
        m_io2 = s.step("gpsimd", lambda e: e.iota(
            io_p[:, 0:1], [[0, 1]], channel_multiplier=1,
            allow_small_or_imprecise_dtypes=True))
        m_id = s.step("vector", lambda e: e.tensor_scalar(
            ident[:, :], io_i[:, :], io_p[:, 0:1], None, ALU.is_equal),
            waits=[m_io1, m_io2])

        def load_small(e, sem):
            for d in range(2):
                e.dma_start(out=dtw[0:64, d * DI:(d + 1) * DI],
                            in_=dt_w[d, :, :]).then_inc(sem, 16)
                for m in range(16):
                    i = d * 16 + m
                    e.dma_start(out=cw[:, i * DCONV:(i + 1) * DCONV],
                                in_=conv_w[d, m * 128:(m + 1) * 128, :]).then_inc(sem, 16)
                    e.dma_start(out=cb[:, i:i + 1],
                                in_=conv_b[d, m * 128:(m + 1) * 128].unsqueeze(1)).then_inc(sem, 16)
                    e.dma_start(out=dtb[:, i:i + 1],
                                in_=dt_b[d, m * 128:(m + 1) * 128].unsqueeze(1)).then_inc(sem, 16)
                    e.dma_start(out=xpw[:, i * 96:(i + 1) * 96],
                                in_=xp_w[d, m * 128:(m + 1) * 128, :]).then_inc(sem, 16)
        m_smw = s.step("sync", load_small, dma=2 * (1 + 16 * 4))

        tr_i = 0
        for t in range(NTT):
            n = ttl[t]
            m_dma = s.step("sync", lambda e, sem, t=t, n=n: e.dma_start(
                out=xtm[t % 2][0:n, :],
                in_=x[t * 128:t * 128 + n, :]).then_inc(sem, 16),
                waits=[buf_free.get(("xtm", t % 2))], dma=1)
            m_s1 = s.step("vector", lambda e, t=t, n=n: e.reduce_sum(
                st[0:n, 0:1], xtm[t % 2][0:n, :], AX.X), waits=[m_dma])
            m_s2 = s.step("scalar", lambda e, t=t, n=n: e.activation(
                xsq[0:n, :], xtm[t % 2][0:n, :], AF.Square,
                accum_out=st[0:n, 1:2]), waits=[m_dma])

            def statmath1(e, n=n):
                e.tensor_scalar(st[0:n, 2:3], st[0:n, 0:1], 1.0 / DM, None, ALU.mult)
                return e.tensor_scalar(st[0:n, 3:4], st[0:n, 1:2], 1.0 / DM, None, ALU.mult)
            m_v1 = s.step("vector", statmath1, waits=[m_s1, m_s2])
            m_v2 = s.step("vector", lambda e, n=n: e.tensor_tensor(
                st[0:n, 4:5], st[0:n, 2:3], st[0:n, 2:3], ALU.mult), waits=[m_v1])
            m_vr = s.step("vector", lambda e, n=n: e.tensor_tensor(
                st[0:n, 5:6], st[0:n, 3:4], st[0:n, 4:5], ALU.subtract), waits=[m_v2])
            m_sq = s.step("scalar", lambda e, n=n: e.activation(
                st[0:n, 6:7], st[0:n, 5:6], AF.Sqrt, bias=cst[0:n, 0:1]),
                waits=[m_vr, m_cst])

            m_rc = s.step("vector", lambda e, n=n: e.reciprocal(
                st[0:n, 7:8], st[0:n, 6:7]), waits=[m_sq])
            m_xn = s.step("vector", lambda e, t=t, n=n: e.tensor_scalar(
                xnb[t % 2][0:n, :], xtm[t % 2][0:n, :],
                st[0:n, 2:3], st[0:n, 7:8], ALU.subtract, ALU.mult),
                waits=[m_rc, buf_free.get(("xnb", t % 2))])
            if debug and t == 0:
                def dumpst(e, sem):
                    e.dma_start(out=d_st[:, 0:8], in_=st[:, :]).then_inc(sem, 16)
                    e.dma_start(out=d_st[:, 8:8 + DM], in_=xnb[0][:, :]).then_inc(sem, 16)
                    e.dma_start(out=d_st[:, 8 + DM:8 + 2 * DM], in_=xtm[0][:, :]).then_inc(sem, 16)
                s.step("gpsimd", dumpst, waits=[m_xn], dma=3)
            buf_free[("xtm", t % 2)] = m_xn

            last_tr = None
            for dsub in range(NKT):
                bank = pz[tr_i % 2]
                m_tr = s.step("tensor", lambda e, t=t, n=n, dsub=dsub, bank=bank: e.transpose(
                    bank[:, 0:n], xnb[t % 2][0:n, dsub * 128:(dsub + 1) * 128],
                    ident[0:n, 0:n]),
                    waits=[m_xn, m_id, bank_free.get(id(bank))])
                m_evc = s.step("scalar", lambda e, t=t, n=n, dsub=dsub, bank=bank: e.copy(
                    xnT[:, dsub * TS + t * 128: dsub * TS + t * 128 + n],
                    bank[:, 0:n]), waits=[m_tr])
                bank_free[id(bank)] = m_evc
                last_tr = m_tr
                m_last_evc = m_evc
                tr_i += 1
            buf_free[("xnb", t % 2)] = last_tr
        m_xnT_all = m_last_evc
        if debug:
            s.step("gpsimd", lambda e, sem: e.dma_start(
                out=d_xnT[:, :], in_=xnT[:, :]).then_inc(sem, 16),
                waits=[m_xnT_all], dma=1)

        m_w = [None, None]
        mi = 0
        m_strip_free = None
        m_xcs_free = None

        for d in range(2):
            m_db_last = [None] * 4
            for mt in range(32):
                par = mi % 2

                def wload(e, sem, d=d, mt=mt, par=par):
                    for kt in range(NKT):
                        e.dma_start(
                            out=wbuf[par][:, kt * 128:(kt + 1) * 128],
                            in_=in_w[d, kt * 128:(kt + 1) * 128,
                                     mt * 128:(mt + 1) * 128]).then_inc(sem, 16)
                m_w[par] = s.step("sync", wload,
                                  waits=[buf_free.get(("wbuf", par))], dma=NKT)

                is_xi = mt < 16
                m_evs = []
                m_last_mm = None
                for g in range(NG + 1):
                    bank = pz[g % 3] if g < NG else pzm
                    mv0 = g * 512
                    mvn = 512 if g < NG else MINI

                    def mmgroup(e, par=par, bank=bank, mv0=mv0, mvn=mvn):
                        for kt in range(NKT):
                            inst = e.matmul(
                                bank[:, 0:mvn],
                                wbuf[par][:, kt * 128:(kt + 1) * 128],
                                xnT[:, kt * TS + mv0: kt * TS + mv0 + mvn],
                                start=(kt == 0), stop=(kt == NKT - 1))
                        return inst
                    m_mm = s.step("tensor", mmgroup,
                                  waits=[m_w[par], m_xnT_all, bank_free.get(id(bank))])
                    m_last_mm = m_mm
                    if is_xi:
                        def evac(e, bank=bank, mv0=mv0, mvn=mvn):
                            return e.copy(strip[:, mv0:mv0 + mvn], bank[:, 0:mvn])
                    else:
                        def evac(e, bank=bank, mv0=mv0, mvn=mvn):
                            return e.activation(strip[:, mv0:mv0 + mvn],
                                                bank[:, 0:mvn], AF.Silu)
                    m_e = s.step("scalar", evac, waits=[m_mm, m_strip_free])
                    bank_free[id(bank)] = m_e
                    m_evs.append(m_e)
                buf_free[("wbuf", par)] = m_last_mm

                if is_xi:
                    idx = d * 16 + mt

                    def convops(e, d=d, idx=idx):
                        offs = [0, 1, 2, 3] if d == 0 else [6, 5, 4, 3]
                        e.tensor_scalar(cacc[0][:, :], strip[:, offs[0]:offs[0] + TSL],
                                        cw[:, idx * DCONV:idx * DCONV + 1], None, ALU.mult)
                        e.scalar_tensor_tensor(cacc[1][:, :], strip[:, offs[1]:offs[1] + TSL],
                                               cw[:, idx * DCONV + 1:idx * DCONV + 2],
                                               cacc[0][:, :], ALU.mult, ALU.add)
                        e.scalar_tensor_tensor(cacc[0][:, :], strip[:, offs[2]:offs[2] + TSL],
                                               cw[:, idx * DCONV + 2:idx * DCONV + 3],
                                               cacc[1][:, :], ALU.mult, ALU.add)
                        return e.scalar_tensor_tensor(cacc[1][:, :], strip[:, offs[3]:offs[3] + TSL],
                                                      cw[:, idx * DCONV + 3:idx * DCONV + 4],
                                                      cacc[0][:, :], ALU.mult, ALU.add)
                    if debug and d == 0 and mt == 0:
                        m_dbgs = s.step("gpsimd", lambda e, sem: e.dma_start(
                            out=d_strip[:, :], in_=strip[:, :]).then_inc(sem, 16),
                            waits=m_evs, dma=1)
                        m_evs = m_evs + [m_dbgs]
                    m_cv = s.step("vector", convops, waits=m_evs + [m_smw])
                    m_strip_free = m_cv

                    m_xc = s.step("scalar", lambda e, idx=idx: e.activation(
                        xc_strip[:, :], cacc[1][:, :], AF.Silu, bias=cb[:, idx:idx + 1]),
                        waits=[m_cv] + (m_xcs_free or []))

                    m_stxc = s.step("gpsimd", lambda e, sem, d=d, mt=mt: e.dma_start(
                        out=o_xc[d, mt * 128:(mt + 1) * 128, :],
                        in_=xc_strip[:, :]).then_inc(sem, 16),
                        waits=[m_xc], dma=1)

                    m_dbs = []
                    for g in range(4):
                        def dbmm(e, g=g, idx=idx, mt=mt):
                            return e.matmul(
                                pdb[g][:, :],
                                xpw[:, idx * 96:(idx + 1) * 96],
                                xc_strip[:, g * 512:(g + 1) * 512],
                                start=(mt == 0), stop=(mt == 15))
                        m_db_last[g] = s.step("tensor", dbmm,
                                              waits=[m_xc, m_smw,
                                                     bank_free.get(id(pdb[g])) if mt == 0 else None])
                        m_dbs.append(m_db_last[g])
                    m_xcs_free = [m_dbs[-1], m_stxc]
                else:
                    m_stz = s.step("gpsimd", lambda e, sem, d=d, mt=mt: e.dma_start(
                        out=o_sz[d, (mt - 16) * 128:(mt - 15) * 128, :],
                        in_=strip[:, HALO:HALO + TSL]).then_inc(sem, 16),
                        waits=m_evs, dma=1)
                    m_strip_free = m_stz
                mi += 1

            m_dbev = []
            for g in range(4):
                m_e = s.step("scalar", lambda e, d=d, g=g: e.copy(
                    dbc_sb[:, d * TSL + g * 512: d * TSL + (g + 1) * 512],
                    pdb[g][:, :]), waits=[m_db_last[g]])
                bank_free[id(pdb[g])] = m_e
                m_dbev.append(m_e)

            s.step("gpsimd", lambda e, sem, d=d: e.dma_start(
                out=o_bc[d, :, :],
                in_=dbc_sb[64:96, d * TSL:(d + 1) * TSL]).then_inc(sem, 16),
                waits=m_dbev, dma=1)

            m_dt_prev_store = None
            for mt in range(16):
                idx = d * 16 + mt
                m_sp1s = []
                for g in range(4):
                    bank = pz[g % 3]

                    def dtmm(e, d=d, mt=mt, g=g, bank=bank):
                        return e.matmul(
                            bank[:, :],
                            dtw[0:64, d * DI + mt * 128: d * DI + (mt + 1) * 128],
                            dbc_sb[0:64, d * TSL + g * 512: d * TSL + (g + 1) * 512],
                            start=True, stop=True)
                    m_mm = s.step("tensor", dtmm,
                                  waits=[m_dbev[-1], bank_free.get(id(bank))])
                    m_sp1 = s.step("scalar", lambda e, g=g, bank=bank, idx=idx: e.activation(
                        spe[:, g * 512:(g + 1) * 512], bank[:, :], AF.Exp,
                        bias=dtb[:, idx:idx + 1]), waits=[m_mm])
                    bank_free[id(bank)] = m_sp1
                    m_sp1s.append(m_sp1)
                m_sp2 = s.step("scalar", lambda e: e.activation(
                    dt_sb[:, :], spe[:, :], AF.Ln, bias=cst[:, 1:2]),
                    waits=m_sp1s + [m_dt_prev_store, m_cst])
                m_dt_prev_store = s.step("gpsimd", lambda e, sem, d=d, mt=mt: e.dma_start(
                    out=o_dt[d, mt * 128:(mt + 1) * 128, :],
                    in_=dt_sb[:, :]).then_inc(sem, 16), waits=[m_sp2], dma=1)

        s.finish(block)
    return nc



def _run_L1(inputs):
    xfull = np.asarray(inputs["L0_attn_x"]).reshape(L, DM)
    xpad = np.zeros((L + 2 * HALO, DM), np.float32)
    xpad[HALO:HALO + L] = xfull
    nc = build_L1()
    shared = {
        "in_w": np.ascontiguousarray(inputs["L0_attn_in_w"], np.float32),
        "conv_w": np.ascontiguousarray(inputs["L0_attn_conv_w"], np.float32),
        "conv_b": np.ascontiguousarray(inputs["L0_attn_conv_b"], np.float32),
        "xp_w": np.ascontiguousarray(inputs["L0_attn_xp_w"], np.float32),
        "dt_w": np.ascontiguousarray(inputs["L0_attn_dt_w"], np.float32),
        "dt_b": np.ascontiguousarray(inputs["L0_attn_dt_b"], np.float32),
    }
    in_maps = []
    for k in range(NCORES):
        m = dict(shared)
        m["x"] = np.ascontiguousarray(xpad[k * TSL:k * TSL + TSL + 2 * HALO])
        in_maps.append(m)
    res = run_bass_kernel_spmd(nc, in_maps, core_ids=list(range(NCORES)))
    out = {}
    for name in ("o_dt", "o_xc", "o_sz", "o_bc"):
        out[name] = np.concatenate([r[name] for r in res.results], axis=2)
    return out





def build_L2():
    T = 256
    NCH = L // T
    NU = DSL // 128
    nc = bass.Bass("TRN2", target_bir_lowering=False, detect_race_conditions=False)

    i_dt = nc.dram_tensor("i_dt", [2, DSL, L], F32, kind="ExternalInput").ap()
    i_xc = nc.dram_tensor("i_xc", [2, DSL, L], F32, kind="ExternalInput").ap()
    i_sz = nc.dram_tensor("i_sz", [2, DSL, L], F32, kind="ExternalInput").ap()
    i_b = nc.dram_tensor("i_b", [2, NST, L], F32, kind="ExternalInput").ap()
    i_c = nc.dram_tensor("i_c", [2, NST, L], F32, kind="ExternalInput").ap()
    i_alog = nc.dram_tensor("i_alog", [2, DSL, NST], F32, kind="ExternalInput").ap()
    i_d = nc.dram_tensor("i_d", [2, DSL], F32, kind="ExternalInput").ap()
    o_mix = nc.dram_tensor("o_mix", [2, DSL, L], F32, kind="ExternalOutput").ap()

    with contextlib.ExitStack() as ctx:
        sbt = lambda name, shape, dt=F32: ctx.enter_context(nc.sbuf_tensor(name, shape, dt))

        dA = sbt("dA", [128, NST * T])
        ubuf = sbt("ubuf", [128, NST * T])
        hbuf = sbt("hbuf", [128, NST * T])
        Brep = [sbt("Brep0", [128, NST * T]), sbt("Brep1", [128, NST * T])]
        Crep = [sbt("Crep0", [128, NST * T]), sbt("Crep1", [128, NST * T])]
        dt_c = [[sbt(f"dt_c{p}{u}", [128, T]) for u in range(NU)] for p in range(2)]
        xc_c = [[sbt(f"xc_c{p}{u}", [128, T]) for u in range(NU)] for p in range(2)]
        sz_c = [[sbt(f"sz_c{p}{u}", [128, T]) for u in range(NU)] for p in range(2)]
        vb = sbt("vb", [128, T])
        yb = sbt("yb", [128, T])
        t1b = sbt("t1b", [128, T])
        mixb = [sbt("mixb0", [128, T]), sbt("mixb1", [128, T])]
        Atile = [[sbt(f"At{d}{u}", [128, NST]) for u in range(NU)] for d in range(2)]
        Dcol = [[sbt(f"Dc{d}{u}", [128, 1]) for u in range(NU)] for d in range(2)]
        carry = [[sbt(f"cy{d}{u}", [128, NST]) for u in range(NU)] for d in range(2)]
        etmp = sbt("etmp", [128, NST])

        s = Sched(nc)
        s.make_sems(ctx)
        block = ctx.enter_context(nc.Block())
        buf_free = {}

        m_setup = []
        for d in range(2):
            for u in range(NU):
                m_l = s.step("sync", lambda e, sem, d=d, u=u: e.dma_start(
                    out=etmp[:, :] if False else Atile[d][u][:, :],
                    in_=i_alog[d, u * 128:(u + 1) * 128, :]).then_inc(sem, 16), dma=1)
                m_e = s.step("scalar", lambda e, d=d, u=u: e.activation(
                    etmp[:, :], Atile[d][u][:, :], AF.Exp), waits=[m_l,
                    m_setup[-1] if m_setup else None])
                m_n = s.step("vector", lambda e, d=d, u=u: e.tensor_scalar(
                    Atile[d][u][:, :], etmp[:, :], -1.0, None, ALU.mult), waits=[m_e])
                m_z = s.step("vector", lambda e, d=d, u=u: e.memset(
                    carry[d][u][:, :], 0.0))
                m_dc = s.step("sync", lambda e, sem, d=d, u=u: e.dma_start(
                    out=Dcol[d][u][:, :],
                    in_=i_d[d, u * 128:(u + 1) * 128].unsqueeze(1)).then_inc(sem, 16), dma=1)
                m_setup += [m_n, m_z, m_dc]
        m_set_all = m_setup

        m_dA_free = None
        m_u_free = None
        m_h_free = None
        for d in range(2):
            for c in range(NCH):
                par = c % 2
                t0 = c * T

                def loadc(e, sem, d=d, t0=t0, par=par):
                    for u in range(NU):
                        r0, r1 = u * 128, (u + 1) * 128
                        e.dma_start(out=dt_c[par][u][:, :],
                                    in_=i_dt[d, r0:r1, t0:t0 + T]).then_inc(sem, 16)
                        e.dma_start(out=xc_c[par][u][:, :],
                                    in_=i_xc[d, r0:r1, t0:t0 + T]).then_inc(sem, 16)
                        e.dma_start(out=sz_c[par][u][:, :],
                                    in_=i_sz[d, r0:r1, t0:t0 + T]).then_inc(sem, 16)
                m_ld = s.step("sync", loadc,
                              waits=[buf_free.get(("chunk", par))], dma=3 * NU)

                wB = [buf_free.get(("Brep", par))]
                def bspread(e, sem, d=d, t0=t0, par=par):
                    for n in range(NST):
                        e.dma_start(out=Brep[par][0:1, n * T:(n + 1) * T],
                                    in_=i_b[d, n:n + 1, t0:t0 + T]).then_inc(sem, 16)
                m_bs = s.step("sync", bspread, waits=wB, dma=NST)
                for k in [1, 2, 4, 8, 16, 32, 64]:
                    m_bs = s.step("sync", lambda e, sem, k=k, par=par: e.dma_start(
                        out=Brep[par][k:2 * k, :], in_=Brep[par][0:k, :]).then_inc(sem, 16),
                        dma=1)
                wC = [buf_free.get(("Crep", par))]
                def cspread(e, sem, d=d, t0=t0, par=par):
                    for n in range(NST):
                        e.dma_start(out=Crep[par][0:1, n * T:(n + 1) * T],
                                    in_=i_c[d, n:n + 1, t0:t0 + T]).then_inc(sem, 16)
                m_cs = s.step("gpsimd", cspread, waits=wC, dma=NST)
                for k in [1, 2, 4, 8, 16, 32, 64]:
                    m_cs = s.step("gpsimd", lambda e, sem, k=k, par=par: e.dma_start(
                        out=Crep[par][k:2 * k, :], in_=Crep[par][0:k, :]).then_inc(sem, 16),
                        dma=1)

                for u in range(NU):
                    m_dA = None
                    for n in range(NST):
                        m_dA = s.step("scalar", lambda e, d=d, u=u, n=n, par=par: e.activation(
                            dA[:, n * T:(n + 1) * T], dt_c[par][u][:, :], AF.Exp,
                            scale=Atile[d][u][:, n:n + 1]),
                            waits=[m_ld, m_dA_free] + (m_set_all if d == 0 and c == 0 else []))
                    m_v = s.step("gpsimd", lambda e, u=u, par=par: e.tensor_tensor(
                        vb[:, :], dt_c[par][u][:, :], xc_c[par][u][:, :], ALU.mult),
                        waits=[m_ld, buf_free.get("vb")])
                    m_u = None
                    for n in range(NST):
                        m_u = s.step("gpsimd", lambda e, n=n, par=par: e.tensor_tensor(
                            ubuf[:, n * T:(n + 1) * T], vb[:, :],
                            Brep[par][:, n * T:(n + 1) * T], ALU.mult),
                            waits=[m_v, m_bs, m_u_free])
                    buf_free["vb"] = m_u
                    buf_free[("Brep", par)] = m_u
                    m_sc = None
                    for n in range(NST):
                        m_sc = s.step("vector", lambda e, d=d, u=u, n=n: e.tensor_tensor_scan(
                            hbuf[:, n * T:(n + 1) * T], dA[:, n * T:(n + 1) * T],
                            ubuf[:, n * T:(n + 1) * T], carry[d][u][:, n:n + 1],
                            ALU.mult, ALU.add),
                            waits=[m_dA, m_u, m_h_free])
                    m_dA_free = m_sc
                    m_cy = s.step("vector", lambda e, d=d, u=u: e.tensor_copy(
                        carry[d][u][:, :],
                        hbuf[:].rearrange("p (n t) -> p n t", n=NST)[:, :, T - 1]),
                        waits=[m_sc])
                    m_hc = None
                    for n in range(NST):
                        m_hc = s.step("vector", lambda e, n=n, par=par: e.tensor_tensor(
                            ubuf[:, n * T:(n + 1) * T], hbuf[:, n * T:(n + 1) * T],
                            Crep[par][:, n * T:(n + 1) * T], ALU.mult),
                            waits=[m_sc, m_cs])
                    m_h_free = m_hc
                    buf_free[("Crep", par)] = m_hc
                    m_y = s.step("vector", lambda e: e.tensor_reduce(
                        yb[:, :], ubuf[:].rearrange("p (n t) -> p t n", n=NST),
                        AX.X, ALU.add), waits=[m_hc])
                    m_u_free = m_y
                    m_t1 = s.step("vector", lambda e, d=d, u=u, par=par: e.scalar_tensor_tensor(
                        t1b[:, :], xc_c[par][u][:, :], Dcol[d][u][:, 0:1], yb[:, :],
                        ALU.mult, ALU.add), waits=[m_y])
                    mpar = (c * NU + u) % 2
                    m_mx = s.step("vector", lambda e, u=u, par=par, mpar=mpar: e.tensor_tensor(
                        mixb[mpar][:, :], t1b[:, :], sz_c[par][u][:, :], ALU.mult),
                        waits=[m_t1, buf_free.get(("mix", mpar))])
                    m_st = s.step("gpsimd", lambda e, sem, d=d, u=u, t0=t0, mpar=mpar: e.dma_start(
                        out=o_mix[d, u * 128:(u + 1) * 128, t0:t0 + T],
                        in_=mixb[mpar][:, :]).then_inc(sem, 16),
                        waits=[m_mx], dma=1)
                    buf_free[("mix", mpar)] = m_st
                    if u == NU - 1:
                        buf_free[("chunk", par)] = m_mx

        s.finish(block)
    return nc


def _run_L2(l1out, inputs):
    dt_f = l1out["o_dt"]; xc_f = l1out["o_xc"]; sz_f = l1out["o_sz"]; bc_f = l1out["o_bc"]
    alog = np.ascontiguousarray(inputs["L0_attn_A_log"], np.float32)
    Dw = np.ascontiguousarray(inputs["L0_attn_D"], np.float32)

    def rev(a):
        out = a.copy()
        out[1] = a[1][..., ::-1]
        return np.ascontiguousarray(out)

    dt_r, xc_r, sz_r = rev(dt_f), rev(xc_f), rev(sz_f)
    b_r = rev(bc_f[:, 0:NST, :])
    c_r = rev(bc_f[:, NST:2 * NST, :])
    nc = build_L2()
    in_maps = []
    for j in range(NCORES):
        r0, r1 = j * DSL, (j + 1) * DSL
        in_maps.append({
            "i_dt": np.ascontiguousarray(dt_r[:, r0:r1]),
            "i_xc": np.ascontiguousarray(xc_r[:, r0:r1]),
            "i_sz": np.ascontiguousarray(sz_r[:, r0:r1]),
            "i_b": b_r, "i_c": c_r,
            "i_alog": np.ascontiguousarray(alog[:, r0:r1]),
            "i_d": np.ascontiguousarray(Dw[:, r0:r1]),
        })
    res = run_bass_kernel_spmd(nc, in_maps, core_ids=list(range(NCORES)))
    mix = np.concatenate([r["o_mix"] for r in res.results], axis=1)
    mix[1] = mix[1][..., ::-1]
    return np.ascontiguousarray(mix)





def build_L3():
    NTT = TSL // 128
    NG = TSL // 512
    NKT = DM // 128
    NKI = DI // 128
    NOF = FF // 128
    nc = bass.Bass("TRN2", target_bir_lowering=False, detect_race_conditions=False)

    x = nc.dram_tensor("x", [TSL, DM], F32, kind="ExternalInput").ap()
    i_mix = nc.dram_tensor("i_mix", [2, DI, TSL], F32, kind="ExternalInput").ap()
    out_w = nc.dram_tensor("out_w", [2, DI, DM], F32, kind="ExternalInput").ap()
    proj_w = nc.dram_tensor("proj_w", [2 * DM, DM], F32, kind="ExternalInput").ap()
    proj_b = nc.dram_tensor("proj_b", [DM], F32, kind="ExternalInput").ap()
    ln_g = nc.dram_tensor("ln_g", [DM], F32, kind="ExternalInput").ap()
    ln_b = nc.dram_tensor("ln_b", [DM], F32, kind="ExternalInput").ap()
    fc1_q = nc.dram_tensor("fc1_q", [DM, FF], I32, kind="ExternalInput").ap()
    fc1_s = nc.dram_tensor("fc1_s", [DM // GRP, FF], F32, kind="ExternalInput").ap()
    fc1_b = nc.dram_tensor("fc1_b", [FF], F32, kind="ExternalInput").ap()
    fc2_q = nc.dram_tensor("fc2_q", [FF, DM], I32, kind="ExternalInput").ap()
    fc2_s = nc.dram_tensor("fc2_s", [FF // GRP, DM], F32, kind="ExternalInput").ap()
    fc2_b = nc.dram_tensor("fc2_b", [DM], F32, kind="ExternalInput").ap()
    o_y = nc.dram_tensor("o_y", [TSL, DM], F32, kind="ExternalOutput").ap()
    s_hT = nc.dram_tensor("s_hT", [DM, TSL], F32).ap()
    s_res = nc.dram_tensor("s_res", [TSL, DM], F32).ap()
    s_attn = nc.dram_tensor("s_attn", [DM, TSL], F32).ap()
    s_w1 = nc.dram_tensor("s_w1", [DM, FF], F32).ap()
    s_w2 = nc.dram_tensor("s_w2", [FF, DM], F32).ap()

    with contextlib.ExitStack() as ctx:
        sbt = lambda name, shape, dt=F32: ctx.enter_context(nc.sbuf_tensor(name, shape, dt))
        pst = lambda name, shape: ctx.enter_context(nc.psum_tensor(name, shape, F32))

        AR1 = sbt("AR1", [128, 16384])
        AR2 = sbt("AR2", [128, 8192])
        AR3 = sbt("AR3", [128, 8192])
        AR4 = sbt("AR4", [128, 4096])
        AR5 = sbt("AR5", [128, 4096])
        AR6 = sbt("AR6", [128, 4096])
        AR7 = sbt("AR7", [128, 4096])

        st = sbt("stats", [128, 16])
        ident = sbt("ident", [128, 128])
        io_i = sbt("io_i", [128, 128], F32)
        io_p = sbt("io_p", [128, 1], F32)
        cst = sbt("cst", [128, 2])
        pbc = sbt("pbc", [128, NKT])
        f1bc = sbt("f1bc", [128, NOF])
        f2bc = sbt("f2bc", [128, NKT])

        xtm = [AR5[:, 2048:3072], AR5[:, 3072:4096]]
        grep = AR5[:, 0:1024]
        brep = AR5[:, 1024:2048]
        xsq = AR7[:, 0:1024]
        xn2 = AR7[:, 1024:2048]
        htm = [AR6[:, 0:1024], AR6[:, 1024:2048]]
        hstg = AR6[:, 2048:3072]
        mixc = AR1[:, 0:NKI * 512]
        ydT = [AR1[:, NKI * 512:NKI * 512 + NKT * 512], AR2[:, 0:NKT * 512]]
        wstr = [AR4[:, 0:2048], AR4[:, 2048:4096]]
        attc = AR6[:, 0:NKT * 512]
        qstg = [AR3[:, 0:4096], AR3[:, 4096:8192]]
        screp = [AR2[:, 0:4096], AR2[:, 4096:8192]]
        wdq = AR5[:, 0:4096]
        syb = AR1[:, 0:NOF * 512]
        hTc = [AR2[:, 0:4096], AR2[:, 4096:8192]]
        w2str = [AR3[:, 0:4096], AR3[:, 4096:8192]]
        ffnc = AR5[:, 0:NKT * 512]
        otm = [AR7[:, 0:1024], AR7[:, 1024:2048]]
        fxtm = [AR7[:, 2048:3072], AR7[:, 3072:4096]]

        pz = [pst("pz0", [128, 512]), pst("pz1", [128, 512]),
              pst("pz2", [128, 512]), pst("pz3", [128, 512])]

        s = Sched(nc)
        s.make_sems(ctx)
        block = ctx.enter_context(nc.Block())
        bank_free = {}
        buf_free = {}

        def mkcst(e):
            e.memset(cst[:, 0:1], EPS)
            return e.memset(cst[:, 1:2], 1.0)
        m_cst = s.step("vector", mkcst)
        m_io1 = s.step("gpsimd", lambda e: e.iota(
            io_i[:, :], [[1, 128]], channel_multiplier=0,
            allow_small_or_imprecise_dtypes=True))
        m_io2 = s.step("gpsimd", lambda e: e.iota(
            io_p[:, 0:1], [[0, 1]], channel_multiplier=1,
            allow_small_or_imprecise_dtypes=True))
        m_id = s.step("vector", lambda e: e.tensor_scalar(
            ident[:, :], io_i[:, :], io_p[:, 0:1], None, ALU.is_equal),
            waits=[m_io1, m_io2])

        def ldsetup(e, sem):
            e.dma_start(out=grep[0:1, :], in_=ln_g.unsqueeze(0)).then_inc(sem, 16)
            e.dma_start(out=brep[0:1, :], in_=ln_b.unsqueeze(0)).then_inc(sem, 16)
            for o in range(NKT):
                e.dma_start(out=pbc[:, o:o + 1],
                            in_=proj_b[o * 128:(o + 1) * 128].unsqueeze(1)).then_inc(sem, 16)
                e.dma_start(out=f2bc[:, o:o + 1],
                            in_=fc2_b[o * 128:(o + 1) * 128].unsqueeze(1)).then_inc(sem, 16)
            for o in range(NOF):
                e.dma_start(out=f1bc[:, o:o + 1],
                            in_=fc1_b[o * 128:(o + 1) * 128].unsqueeze(1)).then_inc(sem, 16)
        m_su = s.step("sync", ldsetup, dma=2 + 2 * NKT + NOF)
        m_db = m_su
        for k in [1, 2, 4, 8, 16, 32, 64]:
            def dbl(e, sem, k=k):
                e.dma_start(out=grep[k:2 * k, :], in_=grep[0:k, :]).then_inc(sem, 16)
                e.dma_start(out=brep[k:2 * k, :], in_=brep[0:k, :]).then_inc(sem, 16)
            m_db = s.step("sync", dbl, waits=[m_db], dma=2)

        tr_i = 0
        for t in range(NTT):
            m_dma = s.step("sync", lambda e, sem, t=t: e.dma_start(
                out=xtm[t % 2][:, :], in_=x[t * 128:(t + 1) * 128, :]).then_inc(sem, 16),
                waits=[buf_free.get(("xtm", t % 2))], dma=1)
            m_s1 = s.step("vector", lambda e, t=t: e.reduce_sum(
                st[:, 0:1], xtm[t % 2][:, :], AX.X), waits=[m_dma])
            m_s2 = s.step("scalar", lambda e, t=t: e.activation(
                xsq[:, :], xtm[t % 2][:, :], AF.Square,
                accum_out=st[:, 1:2]), waits=[m_dma])
            m_v1 = s.step("vector", lambda e: e.tensor_scalar(
                st[:, 2:3], st[:, 0:1], 1.0 / DM, None, ALU.mult), waits=[m_s1, m_s2])
            m_v1b = s.step("vector", lambda e: e.tensor_scalar(
                st[:, 3:4], st[:, 1:2], 1.0 / DM, None, ALU.mult), waits=[m_v1])
            m_v2 = s.step("vector", lambda e: e.tensor_tensor(
                st[:, 4:5], st[:, 2:3], st[:, 2:3], ALU.mult), waits=[m_v1b])
            m_vr = s.step("vector", lambda e: e.tensor_tensor(
                st[:, 5:6], st[:, 3:4], st[:, 4:5], ALU.subtract), waits=[m_v2])
            m_sq = s.step("scalar", lambda e: e.activation(
                st[:, 6:7], st[:, 5:6], AF.Sqrt, bias=cst[:, 0:1]),
                waits=[m_vr, m_cst])
            m_rc = s.step("vector", lambda e: e.reciprocal(
                st[:, 7:8], st[:, 6:7]), waits=[m_sq])
            m_xn = s.step("vector", lambda e, t=t: e.tensor_scalar(
                xsq[:, :], xtm[t % 2][:, :], st[:, 2:3], st[:, 7:8],
                ALU.subtract, ALU.mult), waits=[m_rc, m_s2])
            buf_free[("xtm", t % 2)] = m_xn
            m_rt = s.step("vector", lambda e, t=t: e.tensor_tensor(
                fxtm[t % 2][:, :], xtm[t % 2][:, :], xsq[:, :], ALU.add),
                waits=[m_xn, buf_free.get(("rtm", t % 2))])
            buf_free[("xtm", t % 2)] = m_rt
            m_str = s.step("gpsimd", lambda e, sem, t=t: e.dma_start(
                out=s_res[t * 128:(t + 1) * 128, :],
                in_=fxtm[t % 2][:, :]).then_inc(sem, 16), waits=[m_rt], dma=1)
            buf_free[("rtm", t % 2)] = m_str
            m_t1 = s.step("vector", lambda e: e.reduce_sum(
                st[:, 8:9], xsq[:, :], AX.X), waits=[m_xn])
            m_t2 = s.step("scalar", lambda e: e.activation(
                xn2[:, :], xsq[:, :], AF.Square, accum_out=st[:, 9:10]),
                waits=[m_xn])
            m_u1 = s.step("vector", lambda e: e.tensor_scalar(
                st[:, 10:11], st[:, 8:9], 1.0 / DM, None, ALU.mult), waits=[m_t1, m_t2])
            m_u1b = s.step("vector", lambda e: e.tensor_scalar(
                st[:, 11:12], st[:, 9:10], 1.0 / DM, None, ALU.mult), waits=[m_u1])
            m_u2 = s.step("vector", lambda e: e.tensor_tensor(
                st[:, 12:13], st[:, 10:11], st[:, 10:11], ALU.mult), waits=[m_u1b])
            m_ur = s.step("vector", lambda e: e.tensor_tensor(
                st[:, 13:14], st[:, 11:12], st[:, 12:13], ALU.subtract), waits=[m_u2])
            m_sq2 = s.step("scalar", lambda e: e.activation(
                st[:, 14:15], st[:, 13:14], AF.Sqrt, bias=cst[:, 0:1]),
                waits=[m_ur])
            m_rc2 = s.step("vector", lambda e: e.reciprocal(
                st[:, 15:16], st[:, 14:15]), waits=[m_sq2])
            m_xn2 = s.step("vector", lambda e: e.tensor_scalar(
                xn2[:, :], xsq[:, :], st[:, 10:11], st[:, 15:16],
                ALU.subtract, ALU.mult), waits=[m_rc2])
            m_h1 = s.step("vector", lambda e: e.tensor_tensor(
                xsq[:, :], xn2[:, :], grep[:, :], ALU.mult), waits=[m_xn2, m_db])
            m_h2 = s.step("vector", lambda e, t=t: e.tensor_tensor(
                htm[t % 2][:, :], xsq[:, :], brep[:, :], ALU.add),
                waits=[m_h1, buf_free.get(("htm", t % 2))])

            m_evs = []
            for dsub in range(NKT):
                bank = pz[tr_i % 2]
                m_tr = s.step("tensor", lambda e, t=t, dsub=dsub, bank=bank: e.transpose(
                    bank[:, 0:128], htm[t % 2][:, dsub * 128:(dsub + 1) * 128],
                    ident[:, :]), waits=[m_h2, m_id, bank_free.get(id(bank))])
                m_ev = s.step("scalar", lambda e, dsub=dsub, bank=bank: e.copy(
                    hstg[:, dsub * 128:(dsub + 1) * 128], bank[:, 0:128]),
                    waits=[m_tr, buf_free.get("hstg")])
                bank_free[id(bank)] = m_ev
                m_evs.append(m_ev)
                tr_i += 1
            buf_free[("htm", t % 2)] = m_evs[-1]

            def sthT(e, sem, t=t):
                for dsub in range(NKT):
                    e.dma_start(out=s_hT[dsub * 128:(dsub + 1) * 128, t * 128:(t + 1) * 128],
                                in_=hstg[:, dsub * 128:(dsub + 1) * 128]).then_inc(sem, 16)
            m_sth = s.step("sync", sthT, waits=m_evs, dma=NKT)
            buf_free["hstg"] = m_sth

        for g in range(NG):
            t0 = g * 512
            m_yd = [None, None]
            for d in range(2):
                def ldmix(e, sem, d=d, t0=t0):
                    for kt in range(NKI):
                        e.dma_start(out=mixc[:, kt * 512:(kt + 1) * 512],
                                    in_=i_mix[d, kt * 128:(kt + 1) * 128, t0:t0 + 512]).then_inc(sem, 16)
                m_mx = s.step("sync", ldmix, waits=[buf_free.get("mixc")], dma=NKI)
                m_evl = []
                for o in range(NKT):
                    par = o % 2
                    def ldow(e, sem, d=d, o=o, par=par):
                        for kt in range(NKI):
                            e.dma_start(out=wstr[par][:, kt * 128:(kt + 1) * 128],
                                        in_=out_w[d, kt * 128:(kt + 1) * 128,
                                                  o * 128:(o + 1) * 128]).then_inc(sem, 16)
                    m_w = s.step("sync", ldow, waits=[buf_free.get(("wstr", par))], dma=NKI)
                    bank = pz[o % 2]
                    def mm1(e, o=o, par=par, bank=bank):
                        for kt in range(NKI):
                            inst = e.matmul(bank[:, :], wstr[par][:, kt * 128:(kt + 1) * 128],
                                            mixc[:, kt * 512:(kt + 1) * 512],
                                            start=(kt == 0), stop=(kt == NKI - 1))
                        return inst
                    m_m = s.step("tensor", mm1, waits=[m_w, m_mx, bank_free.get(id(bank))])
                    buf_free[("wstr", par)] = m_m
                    m_e = s.step("scalar", lambda e, d=d, o=o, bank=bank: e.copy(
                        ydT[d][:, o * 512:(o + 1) * 512], bank[:, :]), waits=[m_m])
                    bank_free[id(bank)] = m_e
                    m_evl.append(m_e)
                m_yd[d] = m_evl
                buf_free["mixc"] = m_evl[-1]
            for o in range(NKT):
                par = o % 2
                def ldpj(e, sem, o=o, par=par):
                    for kt in range(2 * NKT):
                        e.dma_start(out=wstr[par][:, kt * 128:(kt + 1) * 128],
                                    in_=proj_w[kt * 128:(kt + 1) * 128,
                                               o * 128:(o + 1) * 128]).then_inc(sem, 16)
                m_w = s.step("sync", ldpj, waits=[buf_free.get(("wstr", par))], dma=2 * NKT)
                bank = pz[2 + o % 2]
                def mm2(e, o=o, par=par, bank=bank):
                    for kt in range(2 * NKT):
                        src = ydT[kt // NKT]
                        k2 = kt % NKT
                        inst = e.matmul(bank[:, :], wstr[par][:, kt * 128:(kt + 1) * 128],
                                        src[:, k2 * 512:(k2 + 1) * 512],
                                        start=(kt == 0), stop=(kt == 2 * NKT - 1))
                    return inst
                m_m = s.step("tensor", mm2,
                             waits=[m_w, m_yd[0][-1], m_yd[1][-1], bank_free.get(id(bank))])
                buf_free[("wstr", par)] = m_m
                m_e = s.step("scalar", lambda e, o=o, bank=bank: e.activation(
                    hstg[:, o * 128:(o + 1) * 128] if False else attc[:, o * 512:(o + 1) * 512],
                    bank[:, :], AF.Identity, bias=pbc[:, o:o + 1]),
                    waits=[m_m, m_su, buf_free.get("attc")])
                bank_free[id(bank)] = m_e
                m_st = s.step("gpsimd", lambda e, sem, o=o, t0=t0: e.dma_start(
                    out=s_attn[o * 128:(o + 1) * 128, t0:t0 + 512],
                    in_=attc[:, o * 512:(o + 1) * 512]).then_inc(sem, 16),
                    waits=[m_e], dma=1)
            buf_free["attc"] = m_st
        buf_free[("screp", 0)] = m_m
        buf_free[("screp", 1)] = m_m
        buf_free[("qstg", 0)] = m_m
        buf_free[("qstg", 1)] = m_m

        for (qsrc, ssrc, wdst, nkt, ncols) in (
                (fc1_q, fc1_s, s_w1, NKT, FF), (fc2_q, fc2_s, s_w2, NOF, DM)):
            for kt in range(nkt):
                par = kt % 2
                m_q = s.step("gpsimd", lambda e, sem, qsrc=qsrc, kt=kt, par=par, ncols=ncols: e.dma_start(
                    out=qstg[par][:, 0:ncols],
                    in_=qsrc[kt * 128:(kt + 1) * 128, :]).then_inc(sem, 16),
                    waits=[buf_free.get(("qstg", par))], dma=1)
                m_sp = s.step("sync", lambda e, sem, ssrc=ssrc, kt=kt, par=par, ncols=ncols: e.dma_start(
                    out=screp[par][0:1, 0:ncols],
                    in_=ssrc[kt:kt + 1, :]).then_inc(sem, 16),
                    waits=[buf_free.get(("screp", par))], dma=1)
                for k in [1, 2, 4, 8, 16, 32, 64]:
                    m_sp = s.step("sync", lambda e, sem, k=k, par=par, ncols=ncols: e.dma_start(
                        out=screp[par][k:2 * k, 0:ncols],
                        in_=screp[par][0:k, 0:ncols]).then_inc(sem, 16), dma=1)
                m_dq = s.step("vector", lambda e, par=par, ncols=ncols: e.scalar_tensor_tensor(
                    wdq[:, 0:ncols], qstg[par][:, 0:ncols], 128.0,
                    screp[par][:, 0:ncols], ALU.subtract, ALU.mult),
                    waits=[m_q, m_sp, buf_free.get("wdq")])
                buf_free[("qstg", par)] = m_dq
                buf_free[("screp", par)] = m_dq
                m_sw = s.step("scalar", lambda e, sem, wdst=wdst, kt=kt, ncols=ncols: e.dma_start(
                    out=wdst[kt * 128:(kt + 1) * 128, :],
                    in_=wdq[:, 0:ncols]).then_inc(sem, 16), waits=[m_dq], dma=1)
                buf_free["wdq"] = m_sw

        buf_free[("hTc", 0)] = m_dq
        buf_free[("hTc", 1)] = m_dq
        buf_free[("w2str", 0)] = m_dq
        buf_free[("w2str", 1)] = m_dq
        m_w1_done = m_sw
        for g in range(NG):
            t0 = g * 512
            par = g % 2
            def ldh(e, sem, t0=t0, par=par):
                for kt in range(NKT):
                    e.dma_start(out=hTc[par][:, kt * 512:(kt + 1) * 512],
                                in_=s_hT[kt * 128:(kt + 1) * 128, t0:t0 + 512]).then_inc(sem, 16)
            m_h = s.step("sync", ldh, waits=[buf_free.get(("hTc", par))], dma=NKT)
            m_sy = []
            for o in range(NOF):
                wpar = o % 2
                def ldw1(e, sem, o=o, wpar=wpar):
                    for kt in range(NKT):
                        e.dma_start(out=wstr[wpar][:, kt * 128:(kt + 1) * 128],
                                    in_=s_w1[kt * 128:(kt + 1) * 128,
                                             o * 128:(o + 1) * 128]).then_inc(sem, 16)
                m_w = s.step("sync", ldw1,
                             waits=[buf_free.get(("wstr", wpar)), m_w1_done], dma=NKT)
                bank = pz[o % 2]
                def mmf1(e, o=o, wpar=wpar, par=par, bank=bank):
                    for kt in range(NKT):
                        inst = e.matmul(bank[:, :], wstr[wpar][:, kt * 128:(kt + 1) * 128],
                                        hTc[par][:, kt * 512:(kt + 1) * 512],
                                        start=(kt == 0), stop=(kt == NKT - 1))
                    return inst
                m_m = s.step("tensor", mmf1, waits=[m_w, m_h, bank_free.get(id(bank))])
                buf_free[("wstr", wpar)] = m_m
                m_e = s.step("scalar", lambda e, o=o, bank=bank: e.activation(
                    syb[:, o * 512:(o + 1) * 512], bank[:, :], AF.Silu,
                    bias=f1bc[:, o:o + 1]), waits=[m_m, m_su, buf_free.get("syb")])
                bank_free[id(bank)] = m_e
                m_sy.append(m_e)
            buf_free[("hTc", par)] = m_m
            m_fc2 = []
            for o in range(NKT):
                wpar = o % 2
                def ldw2(e, sem, o=o, wpar=wpar):
                    for kt in range(NOF):
                        e.dma_start(out=w2str[wpar][:, kt * 128:(kt + 1) * 128],
                                    in_=s_w2[kt * 128:(kt + 1) * 128,
                                             o * 128:(o + 1) * 128]).then_inc(sem, 16)
                m_w = s.step("sync", ldw2,
                             waits=[buf_free.get(("w2str", wpar)), m_w_done], dma=NOF)
                bank = pz[2 + o % 2]
                def mmf2(e, o=o, wpar=wpar, bank=bank):
                    for kt in range(NOF):
                        inst = e.matmul(bank[:, :], w2str[wpar][:, kt * 128:(kt + 1) * 128],
                                        syb[:, kt * 512:(kt + 1) * 512],
                                        start=(kt == 0), stop=(kt == NOF - 1))
                    return inst
                m_m = s.step("tensor", mmf2, waits=[m_w, bank_free.get(id(bank))] + m_sy)
                buf_free[("w2str", wpar)] = m_m
                m_e = s.step("vector", lambda e, o=o, bank=bank: e.tensor_scalar(
                    ffnc[:, o * 512:(o + 1) * 512], bank[:, :],
                    f2bc[:, o:o + 1], 0.5, ALU.add, ALU.mult),
                    waits=[m_m, m_su, buf_free.get("ffnc")])
                bank_free[id(bank)] = m_e
                m_fc2.append(m_e)
            buf_free["syb"] = m_m
            def ldat(e, sem, t0=t0):
                for o in range(NKT):
                    e.dma_start(out=attc[:, o * 512:(o + 1) * 512],
                                in_=s_attn[o * 128:(o + 1) * 128, t0:t0 + 512]).then_inc(sem, 16)
            m_at = s.step("sync", ldat, waits=[buf_free.get("attc")], dma=NKT)
            m_fa = s.step("vector", lambda e: e.tensor_tensor(
                ffnc[:, :], ffnc[:, :], attc[:, :], ALU.add),
                waits=[m_at] + m_fc2)
            buf_free["attc"] = m_fa
            buf_free["ffnc"] = m_fa
            for ts in range(4):
                tpar = ts % 2
                m_x = s.step("sync", lambda e, sem, t0=t0, ts=ts, tpar=tpar: e.dma_start(
                    out=fxtm[tpar][:, :],
                    in_=s_res[t0 + ts * 128:t0 + (ts + 1) * 128, :]).then_inc(sem, 16),
                    waits=[buf_free.get(("fxtm", tpar)), buf_free.get(("rtm", tpar))], dma=1)
                m_adds = []
                for o in range(NKT):
                    bank = pz[(ts * NKT + o) % 2]
                    m_tr = s.step("tensor", lambda e, o=o, ts=ts, bank=bank: e.transpose(
                        bank[:, 0:128],
                        ffnc[:, o * 512 + ts * 128: o * 512 + (ts + 1) * 128],
                        ident[:, :]),
                        waits=[m_fa, m_id, bank_free.get(id(bank))])
                    m_ad = s.step("vector", lambda e, o=o, ts=ts, tpar=tpar, bank=bank: e.tensor_tensor(
                        otm[tpar][:, o * 128:(o + 1) * 128], bank[:, 0:128],
                        fxtm[tpar][:, o * 128:(o + 1) * 128], ALU.add),
                        waits=[m_tr, m_x, buf_free.get(("otm", tpar))])
                    bank_free[id(bank)] = m_ad
                    m_adds.append(m_ad)
                m_so = s.step("sync", lambda e, sem, t0=t0, ts=ts, tpar=tpar: e.dma_start(
                    out=o_y[t0 + ts * 128:t0 + (ts + 1) * 128, :],
                    in_=otm[tpar][:, :]).then_inc(sem, 16), waits=m_adds, dma=1)
                buf_free[("otm", tpar)] = m_so
                buf_free[("fxtm", tpar)] = m_adds[-1]

        s.finish(block)
    return nc


def _run_L3(mix, inputs):
    xfull = np.asarray(inputs["L0_attn_x"]).reshape(L, DM).astype(np.float32)
    nc = build_L3()
    shared = {
        "out_w": np.ascontiguousarray(inputs["L0_attn_out_w"], np.float32),
        "proj_w": np.ascontiguousarray(inputs["L0_attn_proj_w"], np.float32),
        "proj_b": np.ascontiguousarray(inputs["L0_attn_proj_b"], np.float32),
        "ln_g": np.ascontiguousarray(inputs["L0_ffn_ln_gamma"], np.float32),
        "ln_b": np.ascontiguousarray(inputs["L0_ffn_ln_beta"], np.float32),
        "fc1_q": np.ascontiguousarray(inputs["L0_ffn_fc1_q"], np.int32),
        "fc1_s": np.ascontiguousarray(inputs["L0_ffn_fc1_scales"], np.float32),
        "fc1_b": np.ascontiguousarray(inputs["L0_ffn_fc1_bias"], np.float32),
        "fc2_q": np.ascontiguousarray(inputs["L0_ffn_fc2_q"], np.int32),
        "fc2_s": np.ascontiguousarray(inputs["L0_ffn_fc2_scales"], np.float32),
        "fc2_b": np.ascontiguousarray(inputs["L0_ffn_fc2_bias"], np.float32),
    }
    in_maps = []
    for k in range(NCORES):
        m = dict(shared)
        m["x"] = np.ascontiguousarray(xfull[k * TSL:(k + 1) * TSL])
        m["i_mix"] = np.ascontiguousarray(mix[:, :, k * TSL:(k + 1) * TSL])
        in_maps.append(m)
    res = run_bass_kernel_spmd(nc, in_maps, core_ids=list(range(NCORES)))
    return np.concatenate([r["o_y"] for r in res.results], axis=0)



def _barrier(s):
    cur = [(k, s.val[k]) for k in s.SEMS if s.val[k] > 0]
    out = []
    for eng in ("tensor", "vector", "scalar", "gpsimd"):
        out.append(s.step(eng, lambda e: e.nop(), waits=cur))
    return out


def build_fused():
    T = 256
    NCH = L // T
    NU = DSL // 128
    CW = 512
    NGC = L // CW
    NKT = DM // 128
    NOF = FF // 128
    nc = bass.Bass("TRN2", target_bir_lowering=False, detect_race_conditions=False)

    x = nc.dram_tensor("x", [L, DM], F32, kind="ExternalInput").ap()
    xs = nc.dram_tensor("xs", [TSL, DM], F32, kind="ExternalInput").ap()
    inw = nc.dram_tensor("inw", [2, DM, 2 * DSL], F32, kind="ExternalInput").ap()
    conv_w = nc.dram_tensor("conv_w", [2, DSL, DCONV], F32, kind="ExternalInput").ap()
    conv_b = nc.dram_tensor("conv_b", [2, DSL], F32, kind="ExternalInput").ap()
    xp_w = nc.dram_tensor("xp_w", [2, DSL, DTR + 2 * NST], F32, kind="ExternalInput").ap()
    dt_w = nc.dram_tensor("dt_w", [2, DTR, DSL], F32, kind="ExternalInput").ap()
    dt_b = nc.dram_tensor("dt_b", [2, DSL], F32, kind="ExternalInput").ap()
    i_alog = nc.dram_tensor("i_alog", [2, DSL, NST], F32, kind="ExternalInput").ap()
    i_d = nc.dram_tensor("i_d", [2, DSL], F32, kind="ExternalInput").ap()
    ow = nc.dram_tensor("ow", [2, DSL, DM], F32, kind="ExternalInput").ap()
    proj_w = nc.dram_tensor("proj_w", [2 * DM, DM], F32, kind="ExternalInput").ap()
    proj_b = nc.dram_tensor("proj_b", [DM], F32, kind="ExternalInput").ap()
    ln_g = nc.dram_tensor("ln_g", [DM], F32, kind="ExternalInput").ap()
    ln_b = nc.dram_tensor("ln_b", [DM], F32, kind="ExternalInput").ap()
    fc1_q = nc.dram_tensor("fc1_q", [DM, FF], I32, kind="ExternalInput").ap()
    fc1_s = nc.dram_tensor("fc1_s", [DM // GRP, FF], F32, kind="ExternalInput").ap()
    fc1_b = nc.dram_tensor("fc1_b", [FF], F32, kind="ExternalInput").ap()
    fc2_q = nc.dram_tensor("fc2_q", [FF, DM], I32, kind="ExternalInput").ap()
    fc2_s = nc.dram_tensor("fc2_s", [FF // GRP, DM], F32, kind="ExternalInput").ap()
    fc2_b = nc.dram_tensor("fc2_b", [DM], F32, kind="ExternalInput").ap()
    o_y = nc.dram_tensor("o_y", [TSL, DM], F32, kind="ExternalOutput").ap()

    s_dt = nc.dram_tensor("s_dt", [2, DSL, L], F32).ap()
    s_xc = nc.dram_tensor("s_xc", [2, DSL, L], F32).ap()
    s_sz = nc.dram_tensor("s_sz", [2, DSL, L], F32).ap()
    s_dbc_in = nc.dram_tensor("s_dbc_in", [2, 96, L], F32).ap()
    s_dbc = nc.dram_tensor("s_dbc", [2, 96, L], F32).ap()
    s_mix = nc.dram_tensor("s_mix", [2, DSL, L], F32).ap()
    s_part = nc.dram_tensor("s_part", [L, DM], F32).ap()
    s_attn = nc.dram_tensor("s_attn", [TSL, DM], F32).ap()
    s_w1 = nc.dram_tensor("s_w1", [DM, FF], F32).ap()
    s_w2 = nc.dram_tensor("s_w2", [FF, DM], F32).ap()
    s_hT = nc.dram_tensor("s_hT", [DM, TSL], F32).ap()
    s_res = nc.dram_tensor("s_res", [TSL, DM], F32).ap()

    with contextlib.ExitStack() as ctx:
        sbt = lambda name, shape, dt=F32: ctx.enter_context(nc.sbuf_tensor(name, shape, dt))
        pst = lambda name, shape: ctx.enter_context(nc.psum_tensor(name, shape, F32))

        AR1 = sbt("AR1", [128, 24576])
        AR2 = sbt("AR2", [128, 8192])
        AR3 = sbt("AR3", [128, 8192])
        AR4 = sbt("AR4", [128, 4096])
        AR5 = sbt("AR5", [128, 2048])
        AR6 = sbt("AR6", [128, 2048])
        AR7 = sbt("AR7", [128, 2048])

        def rep16(sl, T):
            return bass.AP(sl.tensor, sl.offset, [[sl.ap[0][0], 128], [0, NST], [1, T]])

        st = sbt("stats", [128, 32])
        ident = sbt("ident", [128, 128])
        io_i = sbt("io_i", [128, 128], F32)
        io_p = sbt("io_p", [128, 1], F32)
        cst = sbt("cst", [128, 2])
        smallw = sbt("smallw", [128, 32])
        dtw_sb = sbt("dtw_sb", [64, 2 * DSL])
        xpw_sb = sbt("xpw_sb", [128, 2 * NU * 96])
        Atile = sbt("Atile", [128, 2 * NU * NST])
        carry = sbt("carry", [128, 2 * NU * NST])
        pbc = sbt("pbc", [128, NKT])
        f1bc = sbt("f1bc", [128, NOF])
        f2bc = sbt("f2bc", [128, NKT])

        pzb = pst("pzb", [128, 2048])
        pdt = pst("pdt", [128, 512])
        pz = [pzb[:, k * 512:(k + 1) * 512] for k in range(4)]
        ypzb = [pz[2], pz[3]]
        pdb = [pst("pdb0", [96, 512]), pst("pdb1", [96, 512])]

        s = Sched(nc)
        s.make_sems(ctx)
        block = ctx.enter_context(nc.Block())
        bank_free = {}
        buf_free = {}

        def mkcst(e):
            e.memset(cst[:, 0:1], EPS)
            return e.memset(cst[:, 1:2], 1.0)
        m_cst = s.step("vector", mkcst)
        onesrow = AR6[0:1, 1792:1920]
        onesrow32 = AR6[32:33, 1792:1920]
        def mkones(e):
            e.memset(onesrow, 1.0)
            return e.memset(onesrow32, 1.0)
        m_ones = s.step("vector", mkones)
        m_io1 = s.step("gpsimd", lambda e: e.iota(
            io_i[:, :], [[1, 128]], channel_multiplier=0,
            allow_small_or_imprecise_dtypes=True))
        m_io2 = s.step("gpsimd", lambda e: e.iota(
            io_p[:, 0:1], [[0, 1]], channel_multiplier=1,
            allow_small_or_imprecise_dtypes=True))
        m_id = s.step("vector", lambda e: e.tensor_scalar(
            ident[:, :], io_i[:, :], io_p[:, 0:1], None, ALU.is_equal),
            waits=[m_io1, m_io2])

        def ldsmall(e, sem):
            for d in range(2):
                for u in range(NU):
                    i = d * NU + u
                    r0, r1 = u * 128, (u + 1) * 128
                    e.dma_start(out=smallw[:, i * 4:(i + 1) * 4],
                                in_=conv_w[d, r0:r1, :]).then_inc(sem, 16)
                    e.dma_start(out=smallw[:, 16 + i:17 + i],
                                in_=conv_b[d, r0:r1].unsqueeze(1)).then_inc(sem, 16)
                    e.dma_start(out=smallw[:, 20 + i:21 + i],
                                in_=dt_b[d, r0:r1].unsqueeze(1)).then_inc(sem, 16)
                    e.dma_start(out=smallw[:, 24 + i:25 + i],
                                in_=i_d[d, r0:r1].unsqueeze(1)).then_inc(sem, 16)
                    e.dma_start(out=Atile[:, i * NST:(i + 1) * NST],
                                in_=i_alog[d, r0:r1, :]).then_inc(sem, 16)
                    e.dma_start(out=xpw_sb[:, i * 96:(i + 1) * 96],
                                in_=xp_w[d, r0:r1, :]).then_inc(sem, 16)
                e.dma_start(out=dtw_sb[0:64, d * DSL:(d + 1) * DSL],
                            in_=dt_w[d, :, :]).then_inc(sem, 16)
            for o in range(NKT):
                e.dma_start(out=pbc[:, o:o + 1],
                            in_=proj_b[o * 128:(o + 1) * 128].unsqueeze(1)).then_inc(sem, 16)
                e.dma_start(out=f2bc[:, o:o + 1],
                            in_=fc2_b[o * 128:(o + 1) * 128].unsqueeze(1)).then_inc(sem, 16)
            for o in range(NOF):
                e.dma_start(out=f1bc[:, o:o + 1],
                            in_=fc1_b[o * 128:(o + 1) * 128].unsqueeze(1)).then_inc(sem, 16)
        m_smw = s.step("sync", ldsmall, dma=2 * (6 * NU + 1) + 2 * NKT + NOF)

        m_ae = s.step("scalar", lambda e: e.activation(
            carry[:, :], Atile[:, :], AF.Exp), waits=[m_smw])
        m_an = s.step("vector", lambda e: e.tensor_scalar(
            Atile[:, :], carry[:, :], -1.0, None, ALU.mult), waits=[m_ae])
        m_cz = s.step("vector", lambda e: e.memset(carry[:, :], 0.0), waits=[m_an])

        def ldinw(e, sem):
            for d in range(2):
                for kt in range(NKT):
                    e.dma_start(
                        out=AR1[:, kt * 1024 + d * 512: kt * 1024 + d * 512 + 512],
                        in_=inw[d, kt * 128:(kt + 1) * 128, :]).then_inc(sem, 16)
        m_inw = s.step("sync", ldinw, dma=2 * NKT)

        xtm = [AR2[:, 0:1024], AR2[:, 1024:2048],
               AR3[:, 0:1024], AR3[:, 1024:2048]]
        xsq = AR2[:, 2048:3072]
        xnb = [AR2[:, 3072:4096], AR2[:, 4096:5120],
               AR3[:, 2048:3072], AR3[:, 3072:4096]]
        xnTb = [AR1[:, 8192:12288], AR1[:, 12288:16384]]

        def xiring(d, u, j):
            base = (d * NU + u) * 3 + j
            return AR1[:, 16384 + base * 512:16384 + (base + 1) * 512]
        cacc = [AR1[:, 22528:23040], AR1[:, 23040:23552]]
        xcc = AR1[:, 23552:24064]
        szc = AR1[:, 24064:24576]
        dbev = AR2[:, 5120:5632]
        dtT = AR2[:, 5632:6144]
        spe = AR2[:, 6144:6656]
        zpad = AR2[:, 6656:7168]

        m_zr = s.step("vector", lambda e: e.memset(zpad[:, :], 0.0))
        m_zr2 = None
        def zring(e):
            for d in range(2):
                for u in range(NU):
                    e.memset(xiring(d, u, 2), 0.0)
            return e.memset(xiring(1, 1, 2), 0.0)
        m_zr2 = s.step("vector", zring)

        tr_i = 0
        m_xi_ev = {}
        m_conv_done = {}
        m_ln_prev = None
        m_xnT_free = [None, None]
        m_last_st = None
        for c in range(NGC + 1):
            cpar = c % 2
            if c < NGC:
                m_evs_t = []
                for tt in range(4):
                    t_abs = c * 4 + tt
                    par = t_abs % 4
                    sb = par * 8
                    m_dma = s.step("sync", lambda e, sem, t_abs=t_abs, par=par: e.dma_start(
                        out=xtm[par][:, :],
                        in_=x[t_abs * 128:(t_abs + 1) * 128, :]).then_inc(sem, 16),
                        waits=[buf_free.get(("xtm", par))], dma=1)
                    m_s1 = s.step("vector", lambda e, par=par, sb=sb: e.reduce_sum(
                        st[:, sb:sb + 1], xtm[par][:, :], AX.X),
                        waits=[m_dma, buf_free.get(("st", par))])
                    m_s2 = s.step("scalar", lambda e, par=par, sb=sb: e.activation(
                        xsq[:, :], xtm[par][:, :], AF.Square,
                        accum_out=st[:, sb + 1:sb + 2]),
                        waits=[m_dma, buf_free.get(("st", par)), m_ln_prev])
                    m_ln_prev = m_s2
                    m_v1 = s.step("vector", lambda e, sb=sb: e.tensor_scalar(
                        st[:, sb + 2:sb + 4], st[:, sb:sb + 2], 1.0 / DM, None, ALU.mult),
                        waits=[m_s1, m_s2])
                    m_v2 = s.step("vector", lambda e, sb=sb: e.tensor_tensor(
                        st[:, sb + 4:sb + 5], st[:, sb + 2:sb + 3], st[:, sb + 2:sb + 3], ALU.mult), waits=[m_v1])
                    m_vr = s.step("vector", lambda e, sb=sb: e.tensor_tensor(
                        st[:, sb + 5:sb + 6], st[:, sb + 3:sb + 4], st[:, sb + 4:sb + 5], ALU.subtract), waits=[m_v2])
                    m_sq = s.step("scalar", lambda e, sb=sb: e.activation(
                        st[:, sb + 6:sb + 7], st[:, sb + 5:sb + 6], AF.Sqrt, bias=cst[:, 0:1]),
                        waits=[m_vr, m_cst])
                    m_rc = s.step("vector", lambda e, sb=sb: e.reciprocal(
                        st[:, sb + 7:sb + 8], st[:, sb + 6:sb + 7]), waits=[m_sq])
                    m_xn = s.step("vector", lambda e, par=par, sb=sb: e.tensor_scalar(
                        xnb[par][:, :], xtm[par][:, :], st[:, sb + 2:sb + 3], st[:, sb + 7:sb + 8],
                        ALU.subtract, ALU.mult),
                        waits=[m_rc, buf_free.get(("xnb", par))])
                    buf_free[("st", par)] = m_xn
                    buf_free[("xtm", par)] = m_xn
                    last_tr = None
                    for g4 in range(2):
                        bank = pz[tr_i % 2]
                        def tr4(e, par=par, g4=g4, bank=bank):
                            for q in range(4):
                                dsub = g4 * 4 + q
                                inst = e.transpose(
                                    bank[:, q * 128:(q + 1) * 128],
                                    xnb[par][:, dsub * 128:(dsub + 1) * 128],
                                    ident[:, :])
                            return inst
                        m_tr = s.step("tensor", tr4,
                                      waits=[m_xn, m_id, bank_free.get(id(bank))])
                        def ev4(e, tt=tt, g4=g4, bank=bank, cpar=cpar):
                            dst = xnTb[cpar]
                            out = bass.AP(
                                dst.tensor,
                                dst.offset + g4 * 4 * 512 + tt * 128,
                                [[dst.ap[0][0], 128], [512, 4], [1, 128]])
                            return e.copy(out, bank[:, :])
                        m_ev = s.step("scalar", ev4, waits=[m_tr, m_xnT_free[cpar]])
                        bank_free[id(bank)] = m_ev
                        m_evs_t.append(m_ev)
                        last_tr = m_tr
                        tr_i += 1
                    buf_free[("xnb", par)] = last_tr

                m_mm_last = None
                for d in range(2):
                    for ot in range(4):
                        bank = pz[2 + (ot % 2)]
                        is_xi = ot < NU
                        u = ot % NU
                        col0 = d * 512 + (0 if is_xi else 256) + u * 128

                        def xzmm(e, col0=col0, bank=bank, cpar=cpar):
                            for kt in range(NKT):
                                inst = e.matmul(
                                    bank[:, :],
                                    AR1[:, kt * 1024 + col0: kt * 1024 + col0 + 128],
                                    xnTb[cpar][:, kt * 512:(kt + 1) * 512],
                                    start=(kt == 0), stop=(kt == NKT - 1))
                            return inst
                        m_mm = s.step("tensor", xzmm,
                                      waits=[m_evs_t[-1], m_inw, bank_free.get(id(bank))])
                        m_mm_last = m_mm
                        if is_xi:
                            ring = c % 3
                            m_ev = s.step("scalar", lambda e, d=d, u=u, ring=ring, bank=bank: e.copy(
                                xiring(d, u, ring), bank[:, :]),
                                waits=[m_mm, m_conv_done.get((d, u, c - 2))])
                            m_xi_ev[(d, u, c)] = m_ev
                        else:
                            m_ev = s.step("scalar", lambda e, bank=bank: e.activation(
                                szc[:, :], bank[:, :], AF.Silu),
                                waits=[m_mm, buf_free.get("szc")])
                            m_st = s.step("gpsimd", lambda e, sem, d=d, u=u, c=c: e.dma_start(
                                out=s_sz[d, u * 128:(u + 1) * 128, c * CW:(c + 1) * CW],
                                in_=szc[:, :]).then_inc(sem, 16), waits=[m_ev], dma=1)
                            m_last_st = m_st
                            buf_free["szc"] = m_st
                        bank_free[id(bank)] = m_ev
                m_xnT_free[cpar] = m_mm_last

            if c >= 1:
                cc = c - 1
                for d in range(2):
                    for u in range(NU):
                        rm1 = xiring(d, u, (cc - 1) % 3) if cc >= 1 else xiring(d, u, 2)
                        r0 = xiring(d, u, cc % 3)
                        rp1 = (xiring(d, u, (cc + 1) % 3) if cc + 1 < NGC else zpad)
                        wv = [m_xi_ev.get((d, u, cc)), m_xi_ev.get((d, u, cc - 1)),
                              m_xi_ev.get((d, u, cc + 1)), m_zr, m_zr2, m_smw]
                        i = d * NU + u
                        offs = [j - 3 for j in range(4)] if d == 0 else [3 - j for j in range(4)]
                        m_c = None
                        for j in range(4):
                            o = offs[j]
                            wcol = smallw[:, i * 4 + j: i * 4 + j + 1]
                            segs = []
                            if o < 0:
                                segs.append((0, -o, rm1, 512 + o))
                                segs.append((-o, 512 + o, r0, 0))
                            elif o == 0:
                                segs.append((0, 512, r0, 0))
                            else:
                                segs.append((0, 512 - o, r0, o))
                                segs.append((512 - o, o, rp1, 0))
                            for (d0, n, sbuf, s0) in segs:
                                if j == 0:
                                    m_c = s.step("vector", lambda e, d0=d0, n=n, sbuf=sbuf, s0=s0, wcol=wcol: e.tensor_scalar(
                                        cacc[0][:, d0:d0 + n], sbuf[:, s0:s0 + n],
                                        wcol, None, ALU.mult),
                                        waits=wv + [buf_free.get("cacc"), m_c])
                                else:
                                    dst = cacc[j % 2]
                                    srcacc = cacc[(j + 1) % 2]
                                    m_c = s.step("vector", lambda e, d0=d0, n=n, sbuf=sbuf, s0=s0, wcol=wcol, dst=dst, srcacc=srcacc: e.scalar_tensor_tensor(
                                        dst[:, d0:d0 + n], sbuf[:, s0:s0 + n],
                                        wcol, srcacc[:, d0:d0 + n], ALU.mult, ALU.add),
                                        waits=[m_c])
                        m_conv_done[(d, u, cc)] = m_c
                        m_xc = s.step("scalar", lambda e, i=i: e.activation(
                            xcc[:, :], cacc[1][:, :], AF.Silu,
                            bias=smallw[:, 16 + i:17 + i]),
                            waits=[m_c, buf_free.get("xcc")])
                        buf_free["cacc"] = m_xc
                        m_stxc = s.step("gpsimd", lambda e, sem, d=d, u=u, cc=cc: e.dma_start(
                            out=s_xc[d, u * 128:(u + 1) * 128, cc * CW:(cc + 1) * CW],
                            in_=xcc[:, :]).then_inc(sem, 16), waits=[m_xc], dma=1)
                        m_last_st = m_stxc
                        bankd = pdb[d]
                        m_db = s.step("tensor", lambda e, u=u, i=i, bankd=bankd: e.matmul(
                            bankd[:, :], xpw_sb[:, i * 96:(i + 1) * 96], xcc[:, :],
                            start=(u == 0), stop=(u == NU - 1)),
                            waits=[m_xc, m_smw, bank_free.get(id(bankd))])
                        buf_free["xcc"] = m_db
                        if u == NU - 1:
                            m_dbe = s.step("scalar", lambda e, bankd=bankd: e.copy(
                                dbev[0:96, 0:512], bankd[:, :]),
                                waits=[m_db, buf_free.get("dbev")])
                            bank_free[id(bankd)] = m_dbe
                            m_std = s.step("gpsimd", lambda e, sem, d=d, cc=cc: e.dma_start(
                                out=s_dbc_in[d, :, cc * CW:(cc + 1) * CW],
                                in_=dbev[0:96, 0:512]).then_inc(sem, 16),
                                waits=[m_dbe], dma=1)
                            buf_free["dbev"] = m_std
                            m_last_st = m_std

        m_ar = s.step("gpsimd", lambda e: e.collective_compute(
            "AllReduce", ALU.add, replica_groups=[list(range(NCORES))],
            ins=[s_dbc_in[:]], outs=[s_dbc[:]]), waits=[m_last_st])


        barB = _barrier(s)
        dAb = [AR1[:, 0:4096], AR1[:, 12288:16384]]
        ubb = [AR1[:, 4096:8192], AR1[:, 16384:20480]]
        hbb = AR1[:, 8192:12288]
        BCrep = [AR2[:, 0:8192], AR3[:, 0:8192]]
        bcB = AR1[0:1, 20480:24576]
        bcC = AR1[32:33, 20480:24576]
        dtstg = [AR6[0:64, 0:256], AR6[0:64, 256:512]]
        dtt = [AR6[:, 512:1024], AR6[:, 1024:1536]]
        dt_c = [[AR4[:, (p * 2 + u) * T:(p * 2 + u + 1) * T]
                 for u in range(NU)] for p in range(2)]
        xc_c = [[AR4[:, 1024 + (p * 2 + u) * T: 1024 + (p * 2 + u + 1) * T]
                 for u in range(NU)] for p in range(2)]
        sz_c = [[AR4[:, 2048 + (p * 2 + u) * T: 2048 + (p * 2 + u + 1) * T]
                 for u in range(NU)] for p in range(2)]
        vb = AR4[:, 3072:3072 + T]
        yb = AR4[:, 3584:3584 + T]
        t1b = AR4[:, 3840:3840 + T]
        mixb = [AR5[:, 0:T], AR5[:, T:2 * T]]

        m_dA_free = [None, None]
        m_u_free = [None, None]
        m_h_free = None
        m_mix_last = None
        pend_st = []
        pend_t1 = []

        def emit_t1(ent):
            qi, qu, qpar, qycol, qmpar, qd, qt0, qymm = ent
            m_t1 = s.step("vector", lambda e, i=qi, u=qu, par=qpar, ycol=qycol: e.scalar_tensor_tensor(
                t1b[:, :], xc_c[par][u][:, :], smallw[:, 24 + i:25 + i],
                ypzb[ycol][:, 0:256],
                ALU.mult, ALU.add), waits=[qymm, m_smw])
            buf_free[("ypz", qycol)] = m_t1
            m_mx = s.step("vector", lambda e, mpar=qmpar, par=qpar, u=qu: e.tensor_tensor(
                mixb[mpar][:, :], t1b[:, :], sz_c[par][u][:, :], ALU.mult),
                waits=[m_t1, buf_free.get(("mix", qmpar))])
            pend_st.append((qd, qu, qt0, qmpar, m_mx))
            if qu == NU - 1:
                buf_free[("chunk", qpar)] = m_mx

        chunk_list = []
        for d in range(2):
            for ci in range(NCH):
                chunk_list.append(
                    (d, ci, ci * T if d == 0 else L - (ci + 1) * T, ci % 2))

        def emit_bcast(k):
            d, ci, t0, par = chunk_list[k]
            def bcs(e, sem, d=d, t0=t0, par=par):
                e.dma_start(out=bcB[0:1, :],
                            in_=s_dbc[d, 64:80, t0:t0 + T]).then_inc(sem, 16)
                e.dma_start(out=bcC[0:1, :],
                            in_=s_dbc[d, 80:96, t0:t0 + T]).then_inc(sem, 16)
                e.dma_start(out=dtstg[par][0:64, :],
                            in_=s_dbc[d, 0:64, t0:t0 + T]).then_inc(sem, 16)
            m_bs = s.step("sync", bcs,
                          waits=[buf_free.get("bcstg"), m_ar,
                                 buf_free.get(("dtstg", par))] + barB, dma=3)
            def dtmm(e, d=d, par=par):
                for u in range(NU):
                    inst = e.matmul(
                        pdt[:, u * 256:(u + 1) * 256],
                        dtw_sb[0:64, d * DSL + u * 128: d * DSL + (u + 1) * 128],
                        dtstg[par][0:64, :], start=True, stop=True)
                return inst
            m_dtm = s.step("tensor", dtmm,
                           waits=[m_bs, m_smw, buf_free.get("pdt")]
                           + (barB if k == 0 else []))
            buf_free[("dtstg", par)] = m_dtm
            def dtexp(e, d=d, par=par):
                for u in range(NU):
                    i2 = d * NU + u
                    inst = e.activation(
                        dtt[par][:, u * 256:(u + 1) * 256],
                        pdt[:, u * 256:(u + 1) * 256], AF.Exp,
                        bias=smallw[:, 20 + i2:21 + i2])
                return inst
            m_dte = s.step("scalar", dtexp,
                           waits=[m_dtm, buf_free.get(("dtt", par))])
            buf_free["pdt"] = m_dte
            def dtln(e, par=par):
                for u in range(NU):
                    inst = e.activation(
                        dt_c[par][u][:, :], dtt[par][:, u * 256:(u + 1) * 256],
                        AF.Ln, bias=cst[:, 1:2])
                return inst
            m_dtc = s.step("scalar", dtln,
                           waits=[m_dte, buf_free.get(("chunk", par)), m_cst])
            buf_free[("dtt", par)] = m_dtc
            evs = []
            m_mm_bc = None
            for g in range(16):
                bank = pz[g % 2]
                def bmm(e, g=g, bank=bank):
                    c0 = g * 512
                    if c0 < 4096:
                        src, orow = bcB[0:1, c0:c0 + 512], onesrow
                    else:
                        src = bcC[0:1, c0 - 4096:c0 - 3584]
                        orow = onesrow32
                    return e.matmul(bank, orow, src, start=True, stop=True)
                m_mm_bc = s.step("tensor", bmm,
                                 waits=[m_bs, m_ones,
                                        bank_free.get(("bc", g % 2))]
                                 + (barB if k == 0 and g == 0 else []))
                m_e = s.step("scalar", lambda e, g=g, par=par, bank=bank: e.activation(
                    BCrep[par][:, g * 512:(g + 1) * 512], bank,
                    AF.Identity),
                    waits=[m_mm_bc]
                    + ([buf_free.get(("BC", par))] if g == 0 else []))
                bank_free[("bc", g % 2)] = m_e
                evs.append(m_e)
            buf_free["bcstg"] = m_mm_bc
            return evs[7], evs[15], m_dtc

        nxt_ev = emit_bcast(0)
        it = 0
        for k, (d, ci, t0, par) in enumerate(chunk_list):
                m_evB, m_evC, m_dtc = nxt_ev

                def loadc(e, sem, d=d, t0=t0, par=par):
                    for u in range(NU):
                        r0, r1 = u * 128, (u + 1) * 128
                        e.dma_start(out=xc_c[par][u][:, :],
                                    in_=s_xc[d, r0:r1, t0:t0 + T]).then_inc(sem, 16)
                        e.dma_start(out=sz_c[par][u][:, :],
                                    in_=s_sz[d, r0:r1, t0:t0 + T]).then_inc(sem, 16)
                m_ld = s.step("sync", loadc,
                              waits=[buf_free.get(("chunk", par))] + barB,
                              dma=2 * NU)

                m_dAs = []
                for u in range(NU):
                    i = d * NU + u
                    p2 = u
                    dA = dAb[p2]
                    def dAacts(e, i=i, u=u, par=par, dA=dA):
                        for n in range(NST):
                            inst = e.activation(
                                dA[:, n * T:(n + 1) * T], dt_c[par][u][:, :], AF.Exp,
                                scale=Atile[:, i * NST + n: i * NST + n + 1])
                        return inst
                    m_dAs.append(s.step("scalar", dAacts,
                                  waits=[m_ld, m_dtc, m_dA_free[p2], m_an] + barB))
                if k + 1 < len(chunk_list):
                    nxt_ev = emit_bcast(k + 1)

                for u in range(NU):
                    i = d * NU + u
                    p2 = u
                    dA, ubuf, hbuf = dAb[p2], ubb[p2], hbb
                    m_dA = m_dAs[u]
                    m_v = s.step("vector", lambda e, u=u, par=par: e.tensor_tensor(
                        vb[:, :], dt_c[par][u][:, :], xc_c[par][u][:, :], ALU.mult),
                        waits=[m_ld, m_dtc, buf_free.get("vb")] + barB)
                    m_u = s.step("vector", lambda e, par=par, ubuf=ubuf: e.tensor_tensor(
                        ubuf[:].rearrange("p (n t) -> p n t", n=NST), rep16(vb, T),
                        BCrep[par][:, 0:4096].rearrange("p (n t) -> p n t", n=NST),
                        ALU.mult),
                        waits=[m_v, m_evB, m_u_free[p2]])
                    buf_free["vb"] = m_u
                    if pend_st:
                        pd, pu, pt0, pmpar, pmx = pend_st.pop(0)
                        m_st = s.step("gpsimd", lambda e, sem, d=pd, u=pu, t0=pt0,
                                      mpar=pmpar: e.dma_start(
                            out=s_mix[d, u * 128:(u + 1) * 128, t0:t0 + T],
                            in_=mixb[mpar][:, :]).then_inc(sem, 16),
                            waits=[pmx], dma=1)
                        buf_free[("mix", pmpar)] = m_st
                        m_mix_last = m_st
                    def scans(e, d=d, i=i, dA=dA, ubuf=ubuf, hbuf=hbuf):
                        for n in range(NST):
                            hv = hbuf[:, n * T:(n + 1) * T]
                            av = dA[:, n * T:(n + 1) * T]
                            uv = ubuf[:, n * T:(n + 1) * T]
                            if d == 1:
                                hv = hv[:, ::-1]; av = av[:, ::-1]; uv = uv[:, ::-1]
                            inst = e.tensor_tensor_scan(
                                hv, av, uv, carry[:, i * NST + n: i * NST + n + 1],
                                ALU.mult, ALU.add)
                        return inst
                    m_sc = s.step("vector", scans,
                                  waits=[m_dA, m_u, m_h_free, m_cz] + barB)
                    m_u_free[p2] = m_sc
                    ccol = T - 1 if d == 0 else 0
                    m_cy = s.step("vector", lambda e, i=i, ccol=ccol, hbuf=hbuf: e.tensor_copy(
                        carry[:, i * NST:(i + 1) * NST],
                        hbuf[:].rearrange("p (n t) -> p n t", n=NST)[:, :, ccol]),
                        waits=[m_sc])
                    m_hc = s.step("vector", lambda e, par=par, dA=dA, hbuf=hbuf: e.tensor_tensor(
                        dA[:, :], hbuf[:, :], BCrep[par][:, 4096:8192], ALU.mult),
                        waits=[m_sc, m_evC, m_cy])
                    m_h_free = m_hc
                    buf_free[("BC", par)] = m_hc
                    ycol = it % 2
                    def ymm(e, dA=dA, ycol=ycol):
                        for n in range(NST):
                            inst = e.matmul(ypzb[ycol][:, 0:256], ident[:, :],
                                            dA[:, n * T:(n + 1) * T],
                                            start=(n == 0), stop=(n == NST - 1))
                        return inst
                    m_ymm = s.step("tensor", ymm,
                                   waits=[m_hc, m_id, buf_free.get(("ypz", it % 2))])
                    m_h_free = m_hc
                    m_dA_free[p2] = m_ymm
                    if pend_t1:
                        emit_t1(pend_t1.pop(0))
                    pend_t1.append((i, u, par, ycol, it % 2, d, t0, m_ymm))
                    it += 1
        while pend_t1:
            emit_t1(pend_t1.pop(0))
        while pend_st:
            pd, pu, pt0, pmpar, pmx = pend_st.pop(0)
            m_st = s.step("gpsimd", lambda e, sem, d=pd, u=pu, t0=pt0,
                          mpar=pmpar: e.dma_start(
                out=s_mix[d, u * 128:(u + 1) * 128, t0:t0 + T],
                in_=mixb[mpar][:, :]).then_inc(sem, 16),
                waits=[pmx], dma=1)
            buf_free[("mix", pmpar)] = m_st
            m_mix_last = m_st

        barC0 = _barrier(s)
        Msb = AR4[:, 0:4096]
        owT = AR1[:, 10240:12288]
        owstg = AR1[:, 6144:8192]
        pjstg = AR1[:, 8192:10240]
        mixc = AR5[:, 0:2048]
        attc = AR1[:, 0:4096]
        ptst = [AR1[:, 4096:5120], AR1[:, 5120:6144]]

        m_M_last = None
        for d in range(2):
            def ldow(e, sem, d=d):
                for u in range(NU):
                    e.dma_start(out=owstg[:, u * 1024:(u + 1) * 1024],
                                in_=ow[d, u * 128:(u + 1) * 128, :]).then_inc(sem, 16)
            m_ow = s.step("sync", ldow, waits=[buf_free.get("owstg")] + barC0, dma=NU)
            m_trs = None
            for u in range(NU):
                for mt in range(NKT):
                    bank = pz[(u * NKT + mt) % 2]
                    m_tr = s.step("tensor", lambda e, u=u, mt=mt, bank=bank: e.transpose(
                        bank[:, 0:128],
                        owstg[:, u * 1024 + mt * 128: u * 1024 + (mt + 1) * 128],
                        ident[:, :]),
                        waits=[m_ow, m_id, bank_free.get(id(bank))] + barC0)
                    m_ev = s.step("scalar", lambda e, u=u, mt=mt, bank=bank: e.copy(
                        owT[:, mt * 256 + u * 128: mt * 256 + (u + 1) * 128],
                        bank[:, 0:128]),
                        waits=[m_tr, buf_free.get("owT")] + barC0)
                    bank_free[id(bank)] = m_ev
                    m_trs = m_ev
            buf_free["owstg"] = m_trs
            for oc in range(4):
                def ldpj(e, sem, d=d, oc=oc):
                    for mt in range(NKT):
                        e.dma_start(
                            out=pjstg[:, mt * 256:(mt + 1) * 256],
                            in_=proj_w[d * 1024 + mt * 128: d * 1024 + (mt + 1) * 128,
                                       oc * 256:(oc + 1) * 256]).then_inc(sem, 16)
                m_pj = s.step("sync", ldpj, waits=[buf_free.get("pjstg")] + barC0, dma=NKT)
                m_mm_l = None
                for u in range(NU):
                    bank = pz[2 + u % 2]
                    def mmM(e, u=u, bank=bank):
                        for mt in range(NKT):
                            inst = e.matmul(
                                bank[:, 0:256],
                                owT[:, mt * 256 + u * 128: mt * 256 + (u + 1) * 128],
                                pjstg[:, mt * 256:(mt + 1) * 256],
                                start=(mt == 0), stop=(mt == NKT - 1))
                        return inst
                    m_m = s.step("tensor", mmM,
                                 waits=[m_pj, m_trs, bank_free.get(id(bank))])
                    m_mm_l = m_m
                    m_e = s.step("scalar", lambda e, d=d, u=u, oc=oc, bank=bank: e.copy(
                        Msb[:, (d * NU + u) * 1024 + oc * 256:
                               (d * NU + u) * 1024 + (oc + 1) * 256],
                        bank[:, 0:256]), waits=[m_m, buf_free.get("Msb")] + barC0)
                    bank_free[id(bank)] = m_e
                    m_M_last = m_e
                buf_free["pjstg"] = m_mm_l

        qstg = [AR2[:, 0:4096], AR3[:, 0:4096]]
        screp = [AR2[:, 4096:8192], AR3[:, 4096:8192]]
        wdq = AR1[:, 12288:16384]
        dq_units = []
        for (qsrc, ssrc, wdst, nkt, ncols) in (
                (fc1_q, fc1_s, s_w1, NKT, FF), (fc2_q, fc2_s, s_w2, NOF, DM)):
            for kt in range(nkt):
                dq_units.append((qsrc, ssrc, wdst, kt, ncols))
        dq_dma = {}

        def emit_dq_dma(j):
            qsrc, ssrc, wdst, kt, ncols = dq_units[j]
            par = j % 2
            m_q = s.step("gpsimd", lambda e, sem, qsrc=qsrc, kt=kt, par=par, ncols=ncols: e.dma_start(
                out=qstg[par][:, 0:ncols],
                in_=qsrc[kt * 128:(kt + 1) * 128, :]).then_inc(sem, 16),
                waits=[buf_free.get(("qstg", par))] + barC0, dma=1)
            m_sp = s.step("scalar", lambda e, sem, ssrc=ssrc, kt=kt, par=par, ncols=ncols: e.dma_start(
                out=screp[par][0:1, 0:ncols],
                in_=ssrc[kt:kt + 1, :]).then_inc(sem, 16),
                waits=[buf_free.get(("screp", par))] + barC0, dma=1)
            for k in [1, 2, 4, 8, 16, 32, 64]:
                m_sp = s.step("scalar", lambda e, sem, k=k, par=par, ncols=ncols: e.dma_start(
                    out=screp[par][k:2 * k, 0:ncols],
                    in_=screp[par][0:k, 0:ncols]).then_inc(sem, 16), dma=1)
            dq_dma[j] = (m_q, m_sp)

        def emit_dq_comp(j):
            qsrc, ssrc, wdst, kt, ncols = dq_units[j]
            par = j % 2
            m_q, m_sp = dq_dma[j]
            m_dq = s.step("vector", lambda e, par=par, ncols=ncols: e.scalar_tensor_tensor(
                wdq[:, 0:ncols], qstg[par][:, 0:ncols], 128.0,
                screp[par][:, 0:ncols], ALU.subtract, ALU.mult),
                waits=[m_q, m_sp, buf_free.get("wdq")] + barC0)
            buf_free[("qstg", par)] = m_dq
            buf_free[("screp", par)] = m_dq
            m_sw = s.step("scalar", lambda e, sem, wdst=wdst, kt=kt, ncols=ncols: e.dma_start(
                out=wdst[kt * 128:(kt + 1) * 128, :],
                in_=wdq[:, 0:ncols]).then_inc(sem, 16), waits=[m_dq], dma=1)
            buf_free["wdq"] = m_sw
            return m_sw

        emit_dq_dma(0)
        emit_dq_dma(1)
        dq_j = 0
        m_w_done = None

        fxt = [AR1[:, 16384:17408], AR1[:, 17408:18432]]
        fxsq = AR1[:, 18432:19456]
        fxn2 = AR1[:, 19456:20480]
        fhtm = [AR6[:, 0:1024], AR6[:, 1024:2048]]
        fhstg = AR1[:, 20480:21504]
        grep = AR1[:, 21504:22528]
        brep = AR1[:, 22528:23552]
        pbrep = AR7[:, 0:1024]

        def ldgb(e, sem):
            e.dma_start(out=grep[0:1, :], in_=ln_g.unsqueeze(0)).then_inc(sem, 16)
            e.dma_start(out=brep[0:1, :], in_=ln_b.unsqueeze(0)).then_inc(sem, 16)
            e.dma_start(out=pbrep[0:1, :], in_=proj_b.unsqueeze(0)).then_inc(sem, 16)
        m_db2 = s.step("sync", ldgb, waits=barC0, dma=3)
        for k in [1, 2, 4, 8, 16, 32, 64]:
            def dbl(e, sem, k=k):
                e.dma_start(out=grep[k:2 * k, :], in_=grep[0:k, :]).then_inc(sem, 16)
                e.dma_start(out=brep[k:2 * k, :], in_=brep[0:k, :]).then_inc(sem, 16)
                e.dma_start(out=pbrep[k:2 * k, :], in_=pbrep[0:k, :]).then_inc(sem, 16)
            m_db2 = s.step("sync", dbl, waits=[m_db2], dma=3)

        ln_state = {"tr2": 0, "m_sth": None, "m_db2": m_db2}

        def emit_ln_tile(t):
            par = t % 2
            m_db2 = ln_state["m_db2"]
            m_dma = s.step("sync", lambda e, sem, t=t, par=par: e.dma_start(
                out=fxt[par][:, :], in_=xs[t * 128:(t + 1) * 128, :]).then_inc(sem, 16),
                waits=[buf_free.get(("fxt", par))] + barC0, dma=1)
            m_s1 = s.step("vector", lambda e, par=par: e.reduce_sum(
                st[:, 0:1], fxt[par][:, :], AX.X), waits=[m_dma] + barC0)
            m_s2 = s.step("scalar", lambda e, par=par: e.activation(
                fxsq[:, :], fxt[par][:, :], AF.Square, accum_out=st[:, 1:2]),
                waits=[m_dma] + barC0)
            m_v1 = s.step("vector", lambda e: e.tensor_scalar(
                st[:, 2:3], st[:, 0:1], 1.0 / DM, None, ALU.mult), waits=[m_s1, m_s2])
            m_v1b = s.step("vector", lambda e: e.tensor_scalar(
                st[:, 3:4], st[:, 1:2], 1.0 / DM, None, ALU.mult), waits=[m_v1])
            m_v2 = s.step("vector", lambda e: e.tensor_tensor(
                st[:, 4:5], st[:, 2:3], st[:, 2:3], ALU.mult), waits=[m_v1b])
            m_vr = s.step("vector", lambda e: e.tensor_tensor(
                st[:, 5:6], st[:, 3:4], st[:, 4:5], ALU.subtract), waits=[m_v2])
            m_sq = s.step("scalar", lambda e: e.activation(
                st[:, 6:7], st[:, 5:6], AF.Sqrt, bias=cst[:, 0:1]), waits=[m_vr])
            m_rc = s.step("vector", lambda e: e.reciprocal(
                st[:, 7:8], st[:, 6:7]), waits=[m_sq])
            m_xn = s.step("vector", lambda e, par=par: e.tensor_scalar(
                fxsq[:, :], fxt[par][:, :], st[:, 2:3], st[:, 7:8],
                ALU.subtract, ALU.mult), waits=[m_rc, m_s2])
            m_rt = s.step("vector", lambda e, par=par: e.tensor_tensor(
                fhtm[par][:, :], fxt[par][:, :], fxsq[:, :], ALU.add),
                waits=[m_xn, buf_free.get(("fhtm", par))])
            buf_free[("fxt", par)] = m_rt
            m_str = s.step("scalar", lambda e, sem, t=t, par=par: e.dma_start(
                out=s_res[t * 128:(t + 1) * 128, :],
                in_=fhtm[par][:, :]).then_inc(sem, 16), waits=[m_rt], dma=1)
            m_t1s = s.step("vector", lambda e: e.reduce_sum(
                st[:, 8:9], fxsq[:, :], AX.X), waits=[m_xn])
            m_t2s = s.step("scalar", lambda e: e.activation(
                fxn2[:, :], fxsq[:, :], AF.Square, accum_out=st[:, 9:10]), waits=[m_xn])
            m_u1 = s.step("vector", lambda e: e.tensor_scalar(
                st[:, 10:11], st[:, 8:9], 1.0 / DM, None, ALU.mult), waits=[m_t1s, m_t2s])
            m_u1b = s.step("vector", lambda e: e.tensor_scalar(
                st[:, 11:12], st[:, 9:10], 1.0 / DM, None, ALU.mult), waits=[m_u1])
            m_u2 = s.step("vector", lambda e: e.tensor_tensor(
                st[:, 12:13], st[:, 10:11], st[:, 10:11], ALU.mult), waits=[m_u1b])
            m_ur = s.step("vector", lambda e: e.tensor_tensor(
                st[:, 13:14], st[:, 11:12], st[:, 12:13], ALU.subtract), waits=[m_u2])
            m_sq2 = s.step("scalar", lambda e: e.activation(
                st[:, 14:15], st[:, 13:14], AF.Sqrt, bias=cst[:, 0:1]), waits=[m_ur])
            m_rc2 = s.step("vector", lambda e: e.reciprocal(
                st[:, 15:16], st[:, 14:15]), waits=[m_sq2])
            m_xn2 = s.step("vector", lambda e: e.tensor_scalar(
                fxn2[:, :], fxsq[:, :], st[:, 10:11], st[:, 15:16],
                ALU.subtract, ALU.mult), waits=[m_rc2])
            m_h1 = s.step("vector", lambda e: e.tensor_tensor(
                fxsq[:, :], fxn2[:, :], grep[:, :], ALU.mult), waits=[m_xn2, m_db2])
            m_h2 = s.step("vector", lambda e, par=par: e.tensor_tensor(
                fhtm[par][:, :], fxsq[:, :], brep[:, :], ALU.add),
                waits=[m_h1, m_str])
            m_evs = []
            for dsub in range(NKT):
                bank = pz[ln_state["tr2"] % 2]
                m_tr = s.step("tensor", lambda e, par=par, dsub=dsub, bank=bank: e.transpose(
                    bank[:, 0:128], fhtm[par][:, dsub * 128:(dsub + 1) * 128],
                    ident[:, :]), waits=[m_h2, bank_free.get(id(bank))] + barC0)
                m_ev = s.step("scalar", lambda e, dsub=dsub, bank=bank: e.copy(
                    fhstg[:, dsub * 128:(dsub + 1) * 128], bank[:, 0:128]),
                    waits=[m_tr, buf_free.get("fhstg")])
                bank_free[id(bank)] = m_ev
                m_evs.append(m_ev)
                ln_state["tr2"] += 1
            buf_free[("fhtm", par)] = m_evs[-1]

            def sthT(e, sem, t=t):
                for dsub in range(NKT):
                    e.dma_start(out=s_hT[dsub * 128:(dsub + 1) * 128, t * 128:(t + 1) * 128],
                                in_=fhstg[:, dsub * 128:(dsub + 1) * 128]).then_inc(sem, 16)
            m_sth = s.step("scalar", sthT, waits=m_evs, dma=NKT)
            buf_free["fhstg"] = m_sth
            ln_state["m_sth"] = m_sth

        m_pt_last = None
        for g in range(NGC):
            t0 = g * CW
            def ldmix(e, sem, t0=t0):
                for d in range(2):
                    for u in range(NU):
                        e.dma_start(
                            out=mixc[:, (d * NU + u) * 512:(d * NU + u + 1) * 512],
                            in_=s_mix[d, u * 128:(u + 1) * 128, t0:t0 + CW]).then_inc(sem, 16)
            m_mxl = s.step("sync", ldmix,
                           waits=[buf_free.get("mixc"), m_mix_last, m_M_last] + barC0,
                           dma=2 * NU)
            m_oev = []
            for o in range(NKT):
                bank = pz[o % 2]
                def mmP(e, o=o, bank=bank):
                    for i in range(2 * NU):
                        inst = e.matmul(
                            bank[:, :],
                            Msb[:, i * 1024 + o * 128: i * 1024 + (o + 1) * 128],
                            mixc[:, i * 512:(i + 1) * 512],
                            start=(i == 0), stop=(i == 2 * NU - 1))
                    return inst
                m_m = s.step("tensor", mmP,
                             waits=[m_mxl, m_M_last, bank_free.get(id(bank))])
                m_e = s.step("vector", lambda e, o=o, bank=bank: e.tensor_copy(
                    attc[:, o * 512:(o + 1) * 512], bank[:, :]),
                    waits=[m_m, buf_free.get("attc")])
                bank_free[id(bank)] = m_e
                m_oev.append(m_e)
            buf_free["mixc"] = m_m
            m_tr_l = None
            for ts in range(4):
                tpar = ts % 2
                ptb = ptst[tpar]
                m_ads = []
                for o in range(NKT):
                    bank = pz[2 + (ts * NKT + o) % 2]
                    m_tr = s.step("tensor", lambda e, o=o, ts=ts, bank=bank: e.transpose(
                        bank[:, 0:128],
                        attc[:, o * 512 + ts * 128: o * 512 + (ts + 1) * 128],
                        ident[:, :]),
                        waits=m_oev + [m_id, bank_free.get(id(bank))])
                    m_tr_l = m_tr
                    m_cp = s.step("vector", lambda e, o=o, ptb=ptb, bank=bank: e.tensor_copy(
                        ptb[:, o * 128:(o + 1) * 128], bank[:, 0:128]),
                        waits=[m_tr, buf_free.get(("ptb", tpar))] + barC0)
                    bank_free[id(bank)] = m_cp
                    m_ads.append(m_cp)
                m_stp = s.step("gpsimd", lambda e, sem, t0=t0, ts=ts, ptb=ptb: e.dma_start(
                    out=s_part[t0 + ts * 128: t0 + (ts + 1) * 128, :],
                    in_=ptb[:, :]).then_inc(sem, 16), waits=m_ads, dma=1)
                buf_free[("ptb", tpar)] = m_stp
                m_pt_last = m_stp
            buf_free["attc"] = m_tr_l
            for _ in range(2):
                if dq_j < len(dq_units):
                    m_w_done = emit_dq_comp(dq_j)
                    if dq_j + 2 < len(dq_units):
                        emit_dq_dma(dq_j + 2)
                    dq_j += 1
            if g < TSL // 128:
                emit_ln_tile(g)

        barC = _barrier(s)
        m_rs = s.step("gpsimd", lambda e: e.collective_compute(
            "ReduceScatter", ALU.add, replica_groups=[list(range(NCORES))],
            ins=[s_part[:]], outs=[s_attn[:]]), waits=[m_pt_last])

        m_hT_done = ln_state["m_sth"]

        syb = AR1[:, 0:NOF * 512]
        hTc = [AR1[:, 16384:20480], AR1[:, 20480:24576]]
        w1str = [AR2[:, 0:2048], AR2[:, 2048:4096]]
        w2str = [AR2[:, 4096:8192], AR3[:, 4096:8192]]
        ffnc = AR3[:, 0:4096]
        fatt = AR4[:, 0:1024]
        fres = AR4[:, 1024:2048]
        fout = [AR4[:, 2048:3072], AR4[:, 3072:4096]]

        for g in range(TSL // 512):
            t0 = g * 512
            par = g % 2
            def ldh(e, sem, t0=t0, par=par):
                for kt in range(NKT):
                    e.dma_start(out=hTc[par][:, kt * 512:(kt + 1) * 512],
                                in_=s_hT[kt * 128:(kt + 1) * 128, t0:t0 + 512]).then_inc(sem, 16)
            m_h = s.step("sync", ldh,
                         waits=[buf_free.get(("hTc", par)), m_hT_done] + barC, dma=NKT)
            m_sy = []
            m_m1_l = None
            for o in range(NOF):
                wpar = o % 2
                def ldw1(e, sem, o=o, wpar=wpar):
                    for kt in range(NKT):
                        e.dma_start(out=w1str[wpar][:, kt * 128:(kt + 1) * 128],
                                    in_=s_w1[kt * 128:(kt + 1) * 128,
                                             o * 128:(o + 1) * 128]).then_inc(sem, 16)
                m_w = s.step("sync", ldw1,
                             waits=[buf_free.get(("w1str", wpar)), m_w_done], dma=NKT)
                bank = pz[o % 2]
                def mmf1(e, o=o, wpar=wpar, par=par, bank=bank):
                    for kt in range(NKT):
                        inst = e.matmul(bank[:, :], w1str[wpar][:, kt * 128:(kt + 1) * 128],
                                        hTc[par][:, kt * 512:(kt + 1) * 512],
                                        start=(kt == 0), stop=(kt == NKT - 1))
                    return inst
                m_m = s.step("tensor", mmf1,
                             waits=[m_w, m_h, bank_free.get(id(bank))] + barC)
                m_m1_l = m_m
                buf_free[("w1str", wpar)] = m_m
                m_e = s.step("scalar", lambda e, o=o, bank=bank: e.activation(
                    syb[:, o * 512:(o + 1) * 512], bank[:, :], AF.Silu,
                    bias=f1bc[:, o:o + 1]), waits=[m_m, buf_free.get("syb")])
                bank_free[id(bank)] = m_e
                m_sy.append(m_e)
            buf_free[("hTc", par)] = m_m1_l
            m_f2 = []
            m_m2_l = None
            for o in range(NKT):
                wpar = o % 2
                def ldw2(e, sem, o=o, wpar=wpar):
                    for kt in range(NOF):
                        e.dma_start(out=w2str[wpar][:, kt * 128:(kt + 1) * 128],
                                    in_=s_w2[kt * 128:(kt + 1) * 128,
                                             o * 128:(o + 1) * 128]).then_inc(sem, 16)
                m_w = s.step("sync", ldw2,
                             waits=[buf_free.get(("w2str", wpar)), m_w_done], dma=NOF)
                bank = pz[2 + o % 2]
                def mmf2(e, o=o, wpar=wpar, bank=bank):
                    for kt in range(NOF):
                        inst = e.matmul(bank[:, :], w2str[wpar][:, kt * 128:(kt + 1) * 128],
                                        syb[:, kt * 512:(kt + 1) * 512],
                                        start=(kt == 0), stop=(kt == NOF - 1))
                    return inst
                m_m = s.step("tensor", mmf2,
                             waits=[m_w, bank_free.get(id(bank))] + m_sy)
                m_m2_l = m_m
                buf_free[("w2str", wpar)] = m_m
                m_e = s.step("vector", lambda e, o=o, bank=bank: e.tensor_scalar(
                    ffnc[:, o * 512:(o + 1) * 512], bank[:, :],
                    f2bc[:, o:o + 1], 0.5, ALU.add, ALU.mult),
                    waits=[m_m, buf_free.get("ffnc")] + barC)
                bank_free[id(bank)] = m_e
                m_f2.append(m_e)
            buf_free["syb"] = m_m2_l
            m_tr_l2 = None
            for ts in range(4):
                tpar = ts % 2
                m_lat = s.step("sync", lambda e, sem, t0=t0, ts=ts: e.dma_start(
                    out=fatt[:, :],
                    in_=s_attn[t0 + ts * 128: t0 + (ts + 1) * 128, :]).then_inc(sem, 16),
                    waits=[buf_free.get("fatt"), m_rs], dma=1)
                m_lre = s.step("sync", lambda e, sem, t0=t0, ts=ts: e.dma_start(
                    out=fres[:, :],
                    in_=s_res[t0 + ts * 128: t0 + (ts + 1) * 128, :]).then_inc(sem, 16),
                    waits=[buf_free.get("fres")], dma=1)
                m_ads = []
                for o in range(NKT):
                    bank = pz[(ts * NKT + o) % 2]
                    m_tr = s.step("tensor", lambda e, o=o, ts=ts, bank=bank: e.transpose(
                        bank[:, 0:128],
                        ffnc[:, o * 512 + ts * 128: o * 512 + (ts + 1) * 128],
                        ident[:, :]),
                        waits=m_f2 + [bank_free.get(id(bank))])
                    m_tr_l2 = m_tr
                    m_a1 = s.step("vector", lambda e, o=o, tpar=tpar, bank=bank: e.tensor_tensor(
                        fout[tpar][:, o * 128:(o + 1) * 128], bank[:, 0:128],
                        fatt[:, o * 128:(o + 1) * 128], ALU.add),
                        waits=[m_tr, m_lat, buf_free.get(("fout", tpar))])
                    bank_free[id(bank)] = m_a1
                    m_ads.append(m_a1)
                m_a2 = s.step("vector", lambda e, tpar=tpar: e.tensor_tensor(
                    fout[tpar][:, :], fout[tpar][:, :], fres[:, :], ALU.add),
                    waits=m_ads + [m_lre])
                m_a3 = s.step("vector", lambda e, tpar=tpar: e.tensor_tensor(
                    fout[tpar][:, :], fout[tpar][:, :], pbrep[:, :], ALU.add),
                    waits=[m_a2, m_db2])
                buf_free["fatt"] = m_a3
                buf_free["fres"] = m_a3
                m_so = s.step("sync", lambda e, sem, t0=t0, ts=ts, tpar=tpar: e.dma_start(
                    out=o_y[t0 + ts * 128: t0 + (ts + 1) * 128, :],
                    in_=fout[tpar][:, :]).then_inc(sem, 16), waits=[m_a3], dma=1)
                buf_free[("fout", tpar)] = m_so
            buf_free["ffnc"] = m_tr_l2

        s.finish(block)
    return nc


def kernel(**inputs):
    xfull = np.ascontiguousarray(np.asarray(inputs["L0_attn_x"]).reshape(L, DM), np.float32)
    in_w = np.asarray(inputs["L0_attn_in_w"], np.float32)
    shared = {
        "x": xfull,
        "proj_w": np.ascontiguousarray(inputs["L0_attn_proj_w"], np.float32),
        "proj_b": np.ascontiguousarray(inputs["L0_attn_proj_b"], np.float32),
        "ln_g": np.ascontiguousarray(inputs["L0_ffn_ln_gamma"], np.float32),
        "ln_b": np.ascontiguousarray(inputs["L0_ffn_ln_beta"], np.float32),
        "fc1_q": np.ascontiguousarray(inputs["L0_ffn_fc1_q"], np.int32),
        "fc1_s": np.ascontiguousarray(inputs["L0_ffn_fc1_scales"], np.float32),
        "fc1_b": np.ascontiguousarray(inputs["L0_ffn_fc1_bias"], np.float32),
        "fc2_q": np.ascontiguousarray(inputs["L0_ffn_fc2_q"], np.int32),
        "fc2_s": np.ascontiguousarray(inputs["L0_ffn_fc2_scales"], np.float32),
        "fc2_b": np.ascontiguousarray(inputs["L0_ffn_fc2_bias"], np.float32),
    }
    nc = build_fused()
    in_maps = []
    for r in range(NCORES):
        d0, d1 = r * DSL, (r + 1) * DSL
        m = dict(shared)
        m["inw"] = np.ascontiguousarray(
            np.concatenate([in_w[:, :, d0:d1], in_w[:, :, DI + d0:DI + d1]], axis=2))
        m["conv_w"] = np.ascontiguousarray(inputs["L0_attn_conv_w"][:, d0:d1], np.float32)
        m["conv_b"] = np.ascontiguousarray(inputs["L0_attn_conv_b"][:, d0:d1], np.float32)
        m["xp_w"] = np.ascontiguousarray(inputs["L0_attn_xp_w"][:, d0:d1], np.float32)
        m["dt_w"] = np.ascontiguousarray(inputs["L0_attn_dt_w"][:, :, d0:d1], np.float32)
        m["dt_b"] = np.ascontiguousarray(inputs["L0_attn_dt_b"][:, d0:d1], np.float32)
        m["i_alog"] = np.ascontiguousarray(inputs["L0_attn_A_log"][:, d0:d1], np.float32)
        m["i_d"] = np.ascontiguousarray(inputs["L0_attn_D"][:, d0:d1], np.float32)
        m["ow"] = np.ascontiguousarray(inputs["L0_attn_out_w"][:, d0:d1], np.float32)
        m["xs"] = np.ascontiguousarray(xfull[r * TSL:(r + 1) * TSL])
        in_maps.append(m)
    res = run_bass_kernel_spmd(nc, in_maps, core_ids=list(range(NCORES)))
    y = np.concatenate([r["o_y"] for r in res.results], axis=0)
    return y.reshape(1, L, DM)


# version 28
# speedup vs baseline: 1.0444x; 1.0444x over previous
import contextlib
import numpy as np
import concourse.bass as bass
import concourse.mybir as mybir
from concourse.bass_utils import run_bass_kernel_spmd

F32 = mybir.dt.float32
I32 = mybir.dt.int32
AF = mybir.ActivationFunctionType
ALU = mybir.AluOpType
AX = mybir.AxisListType

NCORES = 8
L = 16384
DM = 1024
DI = 2048
NST = 16
DTR = 64
DCONV = 4
FF = 4096
GRP = 128
TSL = L // NCORES
DSL = DI // NCORES
HALO = DCONV - 1
EPS = 1e-5


class Sched:

    ENGINES = ("sync", "tensor", "vector", "scalar", "gpsimd")
    SEMS = ("sync", "tensor", "vector", "scalar", "gpsimd", "gpdma", "scdma")

    def __init__(self, nc):
        self.nc = nc
        self.stream = {e: [] for e in self.ENGINES}
        self.val = {e: 0 for e in self.SEMS}
        self.sems = {}
        self._hw = {}

    def make_sems(self, ctx):
        for e in self.SEMS:
            self.sems[e] = ctx.enter_context(self.nc.semaphore("sem_" + e))

    def step(self, eng, fn, waits=(), dma=0):
        wl = [w for w in waits if w is not None]
        sem_map = self.sems
        key = eng
        if dma and eng == "gpsimd":
            key = "gpdma"
        elif dma and eng == "scalar":
            key = "scdma"
        pre = self.val[key]
        if dma:
            wl.append((key, pre))
            self.val[key] += 16 * dma
        else:
            self.val[key] += 1

        def emit(e, wl=wl, fn=fn, eng=eng, key=key, dma=dma):
            hw = self._hw.setdefault(eng, {})
            best = {}
            for (we, wv) in wl:
                if wv > best.get(we, 0):
                    best[we] = wv
            for we, wv in best.items():
                if wv <= hw.get(we, 0):
                    continue
                hw[we] = wv
                e.wait_ge(sem_map[we], wv)
            if dma:
                fn(e, sem_map[key])
            else:
                inst = fn(e)
                inst.then_inc(sem_map[key], 1)

        self.stream[eng].append(emit)
        return (key, self.val[key])

    def finish(self, block, extra_final=()):
        final_gp = self.val["gpdma"]
        final_sync = self.val["sync"]
        final_sc = self.val["scdma"]
        for name in self.ENGINES:
            ops = self.stream[name]

            def body(e, ops=ops, name=name):
                for op in ops:
                    op(e)
                if name == "gpsimd":
                    if final_gp:
                        e.wait_ge(self.sems["gpdma"], final_gp)
                    if final_sync:
                        e.wait_ge(self.sems["sync"], final_sync)
                    if final_sc:
                        e.wait_ge(self.sems["scdma"], final_sc)
                    for (we, wv) in extra_final:
                        e.wait_ge(self.sems[we], wv)

            getattr(block, name)(body)



def build_L1(debug=False):
    TS = TSL + 2 * HALO
    NTT = (TS + 127) // 128
    NG = TS // 512
    MINI = TS - NG * 512
    NKT = DM // 128
    nc = bass.Bass("TRN2", target_bir_lowering=False, detect_race_conditions=False)

    x = nc.dram_tensor("x", [TS, DM], F32, kind="ExternalInput").ap()
    in_w = nc.dram_tensor("in_w", [2, DM, 2 * DI], F32, kind="ExternalInput").ap()
    conv_w = nc.dram_tensor("conv_w", [2, DI, DCONV], F32, kind="ExternalInput").ap()
    conv_b = nc.dram_tensor("conv_b", [2, DI], F32, kind="ExternalInput").ap()
    xp_w = nc.dram_tensor("xp_w", [2, DI, DTR + 2 * NST], F32, kind="ExternalInput").ap()
    dt_w = nc.dram_tensor("dt_w", [2, DTR, DI], F32, kind="ExternalInput").ap()
    dt_b = nc.dram_tensor("dt_b", [2, DI], F32, kind="ExternalInput").ap()

    o_dt = nc.dram_tensor("o_dt", [2, DI, TSL], F32, kind="ExternalOutput").ap()
    o_xc = nc.dram_tensor("o_xc", [2, DI, TSL], F32, kind="ExternalOutput").ap()
    o_sz = nc.dram_tensor("o_sz", [2, DI, TSL], F32, kind="ExternalOutput").ap()
    o_bc = nc.dram_tensor("o_bc", [2, 2 * NST, TSL], F32, kind="ExternalOutput").ap()
    if debug:
        d_st = nc.dram_tensor("d_st", [128, 8 + 2 * DM], F32, kind="ExternalOutput").ap()
        d_xnT = nc.dram_tensor("d_xnT", [128, 8 * (TSL + 2 * HALO)], F32, kind="ExternalOutput").ap()
        d_strip = nc.dram_tensor("d_strip", [128, TSL + 2 * HALO], F32, kind="ExternalOutput").ap()

    with contextlib.ExitStack() as ctx:
        sbt = lambda name, shape, dt=F32: ctx.enter_context(nc.sbuf_tensor(name, shape, dt))
        pst = lambda name, shape: ctx.enter_context(nc.psum_tensor(name, shape, F32))

        xnT = sbt("xnT", [128, NKT * TS])
        xtm = [sbt("xtm0", [128, DM]), sbt("xtm1", [128, DM])]
        xnb = [sbt("xnb0", [128, DM]), sbt("xnb1", [128, DM])]
        xsq = sbt("xsq", [128, DM])
        st = sbt("stats", [128, 8])
        ident = sbt("ident", [128, 128])
        io_i = sbt("io_i", [128, 128], F32)
        io_p = sbt("io_p", [128, 1], F32)
        wbuf = [sbt("wbuf0", [128, NKT * 128]), sbt("wbuf1", [128, NKT * 128])]
        strip = sbt("strip", [128, TS])
        cacc = [sbt("cacc0", [128, TSL]), sbt("cacc1", [128, TSL])]
        xc_strip = sbt("xc_strip", [128, TSL])
        cw = sbt("cw", [128, 2 * 16 * DCONV])
        cb = sbt("cb", [128, 2 * 16])
        xpw = sbt("xpw", [128, 2 * 16 * 96])
        dtw = sbt("dtw", [64, 2 * DI])
        dtb = sbt("dtb", [128, 2 * 16])
        dbc_sb = sbt("dbc_sb", [96, 2 * TSL])
        dt_sb = sbt("dt_sb", [128, TSL])
        spe = sbt("spe", [128, TSL])
        cst = sbt("cst", [128, 2])

        pz = [pst("pz0", [128, 512]), pst("pz1", [128, 512]), pst("pz2", [128, 512])]
        pzm = pst("pzm", [128, MINI])
        pdb = [pst("pdb0", [96, 512]), pst("pdb1", [96, 512]),
               pst("pdb2", [96, 512]), pst("pdb3", [96, 512])]

        s = Sched(nc)
        s.make_sems(ctx)
        block = ctx.enter_context(nc.Block())

        ttl = [min(128, TS - t * 128) for t in range(NTT)]
        bank_free = {}
        buf_free = {}

        def mkcst(e):
            e.memset(cst[:, 0:1], EPS)
            return e.memset(cst[:, 1:2], 1.0)
        m_cst = s.step("vector", mkcst)

        m_io1 = s.step("gpsimd", lambda e: e.iota(
            io_i[:, :], [[1, 128]], channel_multiplier=0,
            allow_small_or_imprecise_dtypes=True))
        m_io2 = s.step("gpsimd", lambda e: e.iota(
            io_p[:, 0:1], [[0, 1]], channel_multiplier=1,
            allow_small_or_imprecise_dtypes=True))
        m_id = s.step("vector", lambda e: e.tensor_scalar(
            ident[:, :], io_i[:, :], io_p[:, 0:1], None, ALU.is_equal),
            waits=[m_io1, m_io2])

        def load_small(e, sem):
            for d in range(2):
                e.dma_start(out=dtw[0:64, d * DI:(d + 1) * DI],
                            in_=dt_w[d, :, :]).then_inc(sem, 16)
                for m in range(16):
                    i = d * 16 + m
                    e.dma_start(out=cw[:, i * DCONV:(i + 1) * DCONV],
                                in_=conv_w[d, m * 128:(m + 1) * 128, :]).then_inc(sem, 16)
                    e.dma_start(out=cb[:, i:i + 1],
                                in_=conv_b[d, m * 128:(m + 1) * 128].unsqueeze(1)).then_inc(sem, 16)
                    e.dma_start(out=dtb[:, i:i + 1],
                                in_=dt_b[d, m * 128:(m + 1) * 128].unsqueeze(1)).then_inc(sem, 16)
                    e.dma_start(out=xpw[:, i * 96:(i + 1) * 96],
                                in_=xp_w[d, m * 128:(m + 1) * 128, :]).then_inc(sem, 16)
        m_smw = s.step("sync", load_small, dma=2 * (1 + 16 * 4))

        tr_i = 0
        for t in range(NTT):
            n = ttl[t]
            m_dma = s.step("sync", lambda e, sem, t=t, n=n: e.dma_start(
                out=xtm[t % 2][0:n, :],
                in_=x[t * 128:t * 128 + n, :]).then_inc(sem, 16),
                waits=[buf_free.get(("xtm", t % 2))], dma=1)
            m_s1 = s.step("vector", lambda e, t=t, n=n: e.reduce_sum(
                st[0:n, 0:1], xtm[t % 2][0:n, :], AX.X), waits=[m_dma])
            m_s2 = s.step("scalar", lambda e, t=t, n=n: e.activation(
                xsq[0:n, :], xtm[t % 2][0:n, :], AF.Square,
                accum_out=st[0:n, 1:2]), waits=[m_dma])

            def statmath1(e, n=n):
                e.tensor_scalar(st[0:n, 2:3], st[0:n, 0:1], 1.0 / DM, None, ALU.mult)
                return e.tensor_scalar(st[0:n, 3:4], st[0:n, 1:2], 1.0 / DM, None, ALU.mult)
            m_v1 = s.step("vector", statmath1, waits=[m_s1, m_s2])
            m_v2 = s.step("vector", lambda e, n=n: e.tensor_tensor(
                st[0:n, 4:5], st[0:n, 2:3], st[0:n, 2:3], ALU.mult), waits=[m_v1])
            m_vr = s.step("vector", lambda e, n=n: e.tensor_tensor(
                st[0:n, 5:6], st[0:n, 3:4], st[0:n, 4:5], ALU.subtract), waits=[m_v2])
            m_sq = s.step("scalar", lambda e, n=n: e.activation(
                st[0:n, 6:7], st[0:n, 5:6], AF.Sqrt, bias=cst[0:n, 0:1]),
                waits=[m_vr, m_cst])

            m_rc = s.step("vector", lambda e, n=n: e.reciprocal(
                st[0:n, 7:8], st[0:n, 6:7]), waits=[m_sq])
            m_xn = s.step("vector", lambda e, t=t, n=n: e.tensor_scalar(
                xnb[t % 2][0:n, :], xtm[t % 2][0:n, :],
                st[0:n, 2:3], st[0:n, 7:8], ALU.subtract, ALU.mult),
                waits=[m_rc, buf_free.get(("xnb", t % 2))])
            if debug and t == 0:
                def dumpst(e, sem):
                    e.dma_start(out=d_st[:, 0:8], in_=st[:, :]).then_inc(sem, 16)
                    e.dma_start(out=d_st[:, 8:8 + DM], in_=xnb[0][:, :]).then_inc(sem, 16)
                    e.dma_start(out=d_st[:, 8 + DM:8 + 2 * DM], in_=xtm[0][:, :]).then_inc(sem, 16)
                s.step("gpsimd", dumpst, waits=[m_xn], dma=3)
            buf_free[("xtm", t % 2)] = m_xn

            last_tr = None
            for dsub in range(NKT):
                bank = pz[tr_i % 2]
                m_tr = s.step("tensor", lambda e, t=t, n=n, dsub=dsub, bank=bank: e.transpose(
                    bank[:, 0:n], xnb[t % 2][0:n, dsub * 128:(dsub + 1) * 128],
                    ident[0:n, 0:n]),
                    waits=[m_xn, m_id, bank_free.get(id(bank))])
                m_evc = s.step("scalar", lambda e, t=t, n=n, dsub=dsub, bank=bank: e.copy(
                    xnT[:, dsub * TS + t * 128: dsub * TS + t * 128 + n],
                    bank[:, 0:n]), waits=[m_tr])
                bank_free[id(bank)] = m_evc
                last_tr = m_tr
                m_last_evc = m_evc
                tr_i += 1
            buf_free[("xnb", t % 2)] = last_tr
        m_xnT_all = m_last_evc
        if debug:
            s.step("gpsimd", lambda e, sem: e.dma_start(
                out=d_xnT[:, :], in_=xnT[:, :]).then_inc(sem, 16),
                waits=[m_xnT_all], dma=1)

        m_w = [None, None]
        mi = 0
        m_strip_free = None
        m_xcs_free = None

        for d in range(2):
            m_db_last = [None] * 4
            for mt in range(32):
                par = mi % 2

                def wload(e, sem, d=d, mt=mt, par=par):
                    for kt in range(NKT):
                        e.dma_start(
                            out=wbuf[par][:, kt * 128:(kt + 1) * 128],
                            in_=in_w[d, kt * 128:(kt + 1) * 128,
                                     mt * 128:(mt + 1) * 128]).then_inc(sem, 16)
                m_w[par] = s.step("sync", wload,
                                  waits=[buf_free.get(("wbuf", par))], dma=NKT)

                is_xi = mt < 16
                m_evs = []
                m_last_mm = None
                for g in range(NG + 1):
                    bank = pz[g % 3] if g < NG else pzm
                    mv0 = g * 512
                    mvn = 512 if g < NG else MINI

                    def mmgroup(e, par=par, bank=bank, mv0=mv0, mvn=mvn):
                        for kt in range(NKT):
                            inst = e.matmul(
                                bank[:, 0:mvn],
                                wbuf[par][:, kt * 128:(kt + 1) * 128],
                                xnT[:, kt * TS + mv0: kt * TS + mv0 + mvn],
                                start=(kt == 0), stop=(kt == NKT - 1))
                        return inst
                    m_mm = s.step("tensor", mmgroup,
                                  waits=[m_w[par], m_xnT_all, bank_free.get(id(bank))])
                    m_last_mm = m_mm
                    if is_xi:
                        def evac(e, bank=bank, mv0=mv0, mvn=mvn):
                            return e.copy(strip[:, mv0:mv0 + mvn], bank[:, 0:mvn])
                    else:
                        def evac(e, bank=bank, mv0=mv0, mvn=mvn):
                            return e.activation(strip[:, mv0:mv0 + mvn],
                                                bank[:, 0:mvn], AF.Silu)
                    m_e = s.step("scalar", evac, waits=[m_mm, m_strip_free])
                    bank_free[id(bank)] = m_e
                    m_evs.append(m_e)
                buf_free[("wbuf", par)] = m_last_mm

                if is_xi:
                    idx = d * 16 + mt

                    def convops(e, d=d, idx=idx):
                        offs = [0, 1, 2, 3] if d == 0 else [6, 5, 4, 3]
                        e.tensor_scalar(cacc[0][:, :], strip[:, offs[0]:offs[0] + TSL],
                                        cw[:, idx * DCONV:idx * DCONV + 1], None, ALU.mult)
                        e.scalar_tensor_tensor(cacc[1][:, :], strip[:, offs[1]:offs[1] + TSL],
                                               cw[:, idx * DCONV + 1:idx * DCONV + 2],
                                               cacc[0][:, :], ALU.mult, ALU.add)
                        e.scalar_tensor_tensor(cacc[0][:, :], strip[:, offs[2]:offs[2] + TSL],
                                               cw[:, idx * DCONV + 2:idx * DCONV + 3],
                                               cacc[1][:, :], ALU.mult, ALU.add)
                        return e.scalar_tensor_tensor(cacc[1][:, :], strip[:, offs[3]:offs[3] + TSL],
                                                      cw[:, idx * DCONV + 3:idx * DCONV + 4],
                                                      cacc[0][:, :], ALU.mult, ALU.add)
                    if debug and d == 0 and mt == 0:
                        m_dbgs = s.step("gpsimd", lambda e, sem: e.dma_start(
                            out=d_strip[:, :], in_=strip[:, :]).then_inc(sem, 16),
                            waits=m_evs, dma=1)
                        m_evs = m_evs + [m_dbgs]
                    m_cv = s.step("vector", convops, waits=m_evs + [m_smw])
                    m_strip_free = m_cv

                    m_xc = s.step("scalar", lambda e, idx=idx: e.activation(
                        xc_strip[:, :], cacc[1][:, :], AF.Silu, bias=cb[:, idx:idx + 1]),
                        waits=[m_cv] + (m_xcs_free or []))

                    m_stxc = s.step("gpsimd", lambda e, sem, d=d, mt=mt: e.dma_start(
                        out=o_xc[d, mt * 128:(mt + 1) * 128, :],
                        in_=xc_strip[:, :]).then_inc(sem, 16),
                        waits=[m_xc], dma=1)

                    m_dbs = []
                    for g in range(4):
                        def dbmm(e, g=g, idx=idx, mt=mt):
                            return e.matmul(
                                pdb[g][:, :],
                                xpw[:, idx * 96:(idx + 1) * 96],
                                xc_strip[:, g * 512:(g + 1) * 512],
                                start=(mt == 0), stop=(mt == 15))
                        m_db_last[g] = s.step("tensor", dbmm,
                                              waits=[m_xc, m_smw,
                                                     bank_free.get(id(pdb[g])) if mt == 0 else None])
                        m_dbs.append(m_db_last[g])
                    m_xcs_free = [m_dbs[-1], m_stxc]
                else:
                    m_stz = s.step("gpsimd", lambda e, sem, d=d, mt=mt: e.dma_start(
                        out=o_sz[d, (mt - 16) * 128:(mt - 15) * 128, :],
                        in_=strip[:, HALO:HALO + TSL]).then_inc(sem, 16),
                        waits=m_evs, dma=1)
                    m_strip_free = m_stz
                mi += 1

            m_dbev = []
            for g in range(4):
                m_e = s.step("scalar", lambda e, d=d, g=g: e.copy(
                    dbc_sb[:, d * TSL + g * 512: d * TSL + (g + 1) * 512],
                    pdb[g][:, :]), waits=[m_db_last[g]])
                bank_free[id(pdb[g])] = m_e
                m_dbev.append(m_e)

            s.step("gpsimd", lambda e, sem, d=d: e.dma_start(
                out=o_bc[d, :, :],
                in_=dbc_sb[64:96, d * TSL:(d + 1) * TSL]).then_inc(sem, 16),
                waits=m_dbev, dma=1)

            m_dt_prev_store = None
            for mt in range(16):
                idx = d * 16 + mt
                m_sp1s = []
                for g in range(4):
                    bank = pz[g % 3]

                    def dtmm(e, d=d, mt=mt, g=g, bank=bank):
                        return e.matmul(
                            bank[:, :],
                            dtw[0:64, d * DI + mt * 128: d * DI + (mt + 1) * 128],
                            dbc_sb[0:64, d * TSL + g * 512: d * TSL + (g + 1) * 512],
                            start=True, stop=True)
                    m_mm = s.step("tensor", dtmm,
                                  waits=[m_dbev[-1], bank_free.get(id(bank))])
                    m_sp1 = s.step("scalar", lambda e, g=g, bank=bank, idx=idx: e.activation(
                        spe[:, g * 512:(g + 1) * 512], bank[:, :], AF.Exp,
                        bias=dtb[:, idx:idx + 1]), waits=[m_mm])
                    bank_free[id(bank)] = m_sp1
                    m_sp1s.append(m_sp1)
                m_sp2 = s.step("scalar", lambda e: e.activation(
                    dt_sb[:, :], spe[:, :], AF.Ln, bias=cst[:, 1:2]),
                    waits=m_sp1s + [m_dt_prev_store, m_cst])
                m_dt_prev_store = s.step("gpsimd", lambda e, sem, d=d, mt=mt: e.dma_start(
                    out=o_dt[d, mt * 128:(mt + 1) * 128, :],
                    in_=dt_sb[:, :]).then_inc(sem, 16), waits=[m_sp2], dma=1)

        s.finish(block)
    return nc



def _run_L1(inputs):
    xfull = np.asarray(inputs["L0_attn_x"]).reshape(L, DM)
    xpad = np.zeros((L + 2 * HALO, DM), np.float32)
    xpad[HALO:HALO + L] = xfull
    nc = build_L1()
    shared = {
        "in_w": np.ascontiguousarray(inputs["L0_attn_in_w"], np.float32),
        "conv_w": np.ascontiguousarray(inputs["L0_attn_conv_w"], np.float32),
        "conv_b": np.ascontiguousarray(inputs["L0_attn_conv_b"], np.float32),
        "xp_w": np.ascontiguousarray(inputs["L0_attn_xp_w"], np.float32),
        "dt_w": np.ascontiguousarray(inputs["L0_attn_dt_w"], np.float32),
        "dt_b": np.ascontiguousarray(inputs["L0_attn_dt_b"], np.float32),
    }
    in_maps = []
    for k in range(NCORES):
        m = dict(shared)
        m["x"] = np.ascontiguousarray(xpad[k * TSL:k * TSL + TSL + 2 * HALO])
        in_maps.append(m)
    res = run_bass_kernel_spmd(nc, in_maps, core_ids=list(range(NCORES)))
    out = {}
    for name in ("o_dt", "o_xc", "o_sz", "o_bc"):
        out[name] = np.concatenate([r[name] for r in res.results], axis=2)
    return out





def build_L2():
    T = 256
    NCH = L // T
    NU = DSL // 128
    nc = bass.Bass("TRN2", target_bir_lowering=False, detect_race_conditions=False)

    i_dt = nc.dram_tensor("i_dt", [2, DSL, L], F32, kind="ExternalInput").ap()
    i_xc = nc.dram_tensor("i_xc", [2, DSL, L], F32, kind="ExternalInput").ap()
    i_sz = nc.dram_tensor("i_sz", [2, DSL, L], F32, kind="ExternalInput").ap()
    i_b = nc.dram_tensor("i_b", [2, NST, L], F32, kind="ExternalInput").ap()
    i_c = nc.dram_tensor("i_c", [2, NST, L], F32, kind="ExternalInput").ap()
    i_alog = nc.dram_tensor("i_alog", [2, DSL, NST], F32, kind="ExternalInput").ap()
    i_d = nc.dram_tensor("i_d", [2, DSL], F32, kind="ExternalInput").ap()
    o_mix = nc.dram_tensor("o_mix", [2, DSL, L], F32, kind="ExternalOutput").ap()

    with contextlib.ExitStack() as ctx:
        sbt = lambda name, shape, dt=F32: ctx.enter_context(nc.sbuf_tensor(name, shape, dt))

        dA = sbt("dA", [128, NST * T])
        ubuf = sbt("ubuf", [128, NST * T])
        hbuf = sbt("hbuf", [128, NST * T])
        Brep = [sbt("Brep0", [128, NST * T]), sbt("Brep1", [128, NST * T])]
        Crep = [sbt("Crep0", [128, NST * T]), sbt("Crep1", [128, NST * T])]
        dt_c = [[sbt(f"dt_c{p}{u}", [128, T]) for u in range(NU)] for p in range(2)]
        xc_c = [[sbt(f"xc_c{p}{u}", [128, T]) for u in range(NU)] for p in range(2)]
        sz_c = [[sbt(f"sz_c{p}{u}", [128, T]) for u in range(NU)] for p in range(2)]
        vb = sbt("vb", [128, T])
        yb = sbt("yb", [128, T])
        t1b = sbt("t1b", [128, T])
        mixb = [sbt("mixb0", [128, T]), sbt("mixb1", [128, T])]
        Atile = [[sbt(f"At{d}{u}", [128, NST]) for u in range(NU)] for d in range(2)]
        Dcol = [[sbt(f"Dc{d}{u}", [128, 1]) for u in range(NU)] for d in range(2)]
        carry = [[sbt(f"cy{d}{u}", [128, NST]) for u in range(NU)] for d in range(2)]
        etmp = sbt("etmp", [128, NST])

        s = Sched(nc)
        s.make_sems(ctx)
        block = ctx.enter_context(nc.Block())
        buf_free = {}

        m_setup = []
        for d in range(2):
            for u in range(NU):
                m_l = s.step("sync", lambda e, sem, d=d, u=u: e.dma_start(
                    out=etmp[:, :] if False else Atile[d][u][:, :],
                    in_=i_alog[d, u * 128:(u + 1) * 128, :]).then_inc(sem, 16), dma=1)
                m_e = s.step("scalar", lambda e, d=d, u=u: e.activation(
                    etmp[:, :], Atile[d][u][:, :], AF.Exp), waits=[m_l,
                    m_setup[-1] if m_setup else None])
                m_n = s.step("vector", lambda e, d=d, u=u: e.tensor_scalar(
                    Atile[d][u][:, :], etmp[:, :], -1.0, None, ALU.mult), waits=[m_e])
                m_z = s.step("vector", lambda e, d=d, u=u: e.memset(
                    carry[d][u][:, :], 0.0))
                m_dc = s.step("sync", lambda e, sem, d=d, u=u: e.dma_start(
                    out=Dcol[d][u][:, :],
                    in_=i_d[d, u * 128:(u + 1) * 128].unsqueeze(1)).then_inc(sem, 16), dma=1)
                m_setup += [m_n, m_z, m_dc]
        m_set_all = m_setup

        m_dA_free = None
        m_u_free = None
        m_h_free = None
        for d in range(2):
            for c in range(NCH):
                par = c % 2
                t0 = c * T

                def loadc(e, sem, d=d, t0=t0, par=par):
                    for u in range(NU):
                        r0, r1 = u * 128, (u + 1) * 128
                        e.dma_start(out=dt_c[par][u][:, :],
                                    in_=i_dt[d, r0:r1, t0:t0 + T]).then_inc(sem, 16)
                        e.dma_start(out=xc_c[par][u][:, :],
                                    in_=i_xc[d, r0:r1, t0:t0 + T]).then_inc(sem, 16)
                        e.dma_start(out=sz_c[par][u][:, :],
                                    in_=i_sz[d, r0:r1, t0:t0 + T]).then_inc(sem, 16)
                m_ld = s.step("sync", loadc,
                              waits=[buf_free.get(("chunk", par))], dma=3 * NU)

                wB = [buf_free.get(("Brep", par))]
                def bspread(e, sem, d=d, t0=t0, par=par):
                    for n in range(NST):
                        e.dma_start(out=Brep[par][0:1, n * T:(n + 1) * T],
                                    in_=i_b[d, n:n + 1, t0:t0 + T]).then_inc(sem, 16)
                m_bs = s.step("sync", bspread, waits=wB, dma=NST)
                for k in [1, 2, 4, 8, 16, 32, 64]:
                    m_bs = s.step("sync", lambda e, sem, k=k, par=par: e.dma_start(
                        out=Brep[par][k:2 * k, :], in_=Brep[par][0:k, :]).then_inc(sem, 16),
                        dma=1)
                wC = [buf_free.get(("Crep", par))]
                def cspread(e, sem, d=d, t0=t0, par=par):
                    for n in range(NST):
                        e.dma_start(out=Crep[par][0:1, n * T:(n + 1) * T],
                                    in_=i_c[d, n:n + 1, t0:t0 + T]).then_inc(sem, 16)
                m_cs = s.step("gpsimd", cspread, waits=wC, dma=NST)
                for k in [1, 2, 4, 8, 16, 32, 64]:
                    m_cs = s.step("gpsimd", lambda e, sem, k=k, par=par: e.dma_start(
                        out=Crep[par][k:2 * k, :], in_=Crep[par][0:k, :]).then_inc(sem, 16),
                        dma=1)

                for u in range(NU):
                    m_dA = None
                    for n in range(NST):
                        m_dA = s.step("scalar", lambda e, d=d, u=u, n=n, par=par: e.activation(
                            dA[:, n * T:(n + 1) * T], dt_c[par][u][:, :], AF.Exp,
                            scale=Atile[d][u][:, n:n + 1]),
                            waits=[m_ld, m_dA_free] + (m_set_all if d == 0 and c == 0 else []))
                    m_v = s.step("gpsimd", lambda e, u=u, par=par: e.tensor_tensor(
                        vb[:, :], dt_c[par][u][:, :], xc_c[par][u][:, :], ALU.mult),
                        waits=[m_ld, buf_free.get("vb")])
                    m_u = None
                    for n in range(NST):
                        m_u = s.step("gpsimd", lambda e, n=n, par=par: e.tensor_tensor(
                            ubuf[:, n * T:(n + 1) * T], vb[:, :],
                            Brep[par][:, n * T:(n + 1) * T], ALU.mult),
                            waits=[m_v, m_bs, m_u_free])
                    buf_free["vb"] = m_u
                    buf_free[("Brep", par)] = m_u
                    m_sc = None
                    for n in range(NST):
                        m_sc = s.step("vector", lambda e, d=d, u=u, n=n: e.tensor_tensor_scan(
                            hbuf[:, n * T:(n + 1) * T], dA[:, n * T:(n + 1) * T],
                            ubuf[:, n * T:(n + 1) * T], carry[d][u][:, n:n + 1],
                            ALU.mult, ALU.add),
                            waits=[m_dA, m_u, m_h_free])
                    m_dA_free = m_sc
                    m_cy = s.step("vector", lambda e, d=d, u=u: e.tensor_copy(
                        carry[d][u][:, :],
                        hbuf[:].rearrange("p (n t) -> p n t", n=NST)[:, :, T - 1]),
                        waits=[m_sc])
                    m_hc = None
                    for n in range(NST):
                        m_hc = s.step("vector", lambda e, n=n, par=par: e.tensor_tensor(
                            ubuf[:, n * T:(n + 1) * T], hbuf[:, n * T:(n + 1) * T],
                            Crep[par][:, n * T:(n + 1) * T], ALU.mult),
                            waits=[m_sc, m_cs])
                    m_h_free = m_hc
                    buf_free[("Crep", par)] = m_hc
                    m_y = s.step("vector", lambda e: e.tensor_reduce(
                        yb[:, :], ubuf[:].rearrange("p (n t) -> p t n", n=NST),
                        AX.X, ALU.add), waits=[m_hc])
                    m_u_free = m_y
                    m_t1 = s.step("vector", lambda e, d=d, u=u, par=par: e.scalar_tensor_tensor(
                        t1b[:, :], xc_c[par][u][:, :], Dcol[d][u][:, 0:1], yb[:, :],
                        ALU.mult, ALU.add), waits=[m_y])
                    mpar = (c * NU + u) % 2
                    m_mx = s.step("vector", lambda e, u=u, par=par, mpar=mpar: e.tensor_tensor(
                        mixb[mpar][:, :], t1b[:, :], sz_c[par][u][:, :], ALU.mult),
                        waits=[m_t1, buf_free.get(("mix", mpar))])
                    m_st = s.step("gpsimd", lambda e, sem, d=d, u=u, t0=t0, mpar=mpar: e.dma_start(
                        out=o_mix[d, u * 128:(u + 1) * 128, t0:t0 + T],
                        in_=mixb[mpar][:, :]).then_inc(sem, 16),
                        waits=[m_mx], dma=1)
                    buf_free[("mix", mpar)] = m_st
                    if u == NU - 1:
                        buf_free[("chunk", par)] = m_mx

        s.finish(block)
    return nc


def _run_L2(l1out, inputs):
    dt_f = l1out["o_dt"]; xc_f = l1out["o_xc"]; sz_f = l1out["o_sz"]; bc_f = l1out["o_bc"]
    alog = np.ascontiguousarray(inputs["L0_attn_A_log"], np.float32)
    Dw = np.ascontiguousarray(inputs["L0_attn_D"], np.float32)

    def rev(a):
        out = a.copy()
        out[1] = a[1][..., ::-1]
        return np.ascontiguousarray(out)

    dt_r, xc_r, sz_r = rev(dt_f), rev(xc_f), rev(sz_f)
    b_r = rev(bc_f[:, 0:NST, :])
    c_r = rev(bc_f[:, NST:2 * NST, :])
    nc = build_L2()
    in_maps = []
    for j in range(NCORES):
        r0, r1 = j * DSL, (j + 1) * DSL
        in_maps.append({
            "i_dt": np.ascontiguousarray(dt_r[:, r0:r1]),
            "i_xc": np.ascontiguousarray(xc_r[:, r0:r1]),
            "i_sz": np.ascontiguousarray(sz_r[:, r0:r1]),
            "i_b": b_r, "i_c": c_r,
            "i_alog": np.ascontiguousarray(alog[:, r0:r1]),
            "i_d": np.ascontiguousarray(Dw[:, r0:r1]),
        })
    res = run_bass_kernel_spmd(nc, in_maps, core_ids=list(range(NCORES)))
    mix = np.concatenate([r["o_mix"] for r in res.results], axis=1)
    mix[1] = mix[1][..., ::-1]
    return np.ascontiguousarray(mix)





def build_L3():
    NTT = TSL // 128
    NG = TSL // 512
    NKT = DM // 128
    NKI = DI // 128
    NOF = FF // 128
    nc = bass.Bass("TRN2", target_bir_lowering=False, detect_race_conditions=False)

    x = nc.dram_tensor("x", [TSL, DM], F32, kind="ExternalInput").ap()
    i_mix = nc.dram_tensor("i_mix", [2, DI, TSL], F32, kind="ExternalInput").ap()
    out_w = nc.dram_tensor("out_w", [2, DI, DM], F32, kind="ExternalInput").ap()
    proj_w = nc.dram_tensor("proj_w", [2 * DM, DM], F32, kind="ExternalInput").ap()
    proj_b = nc.dram_tensor("proj_b", [DM], F32, kind="ExternalInput").ap()
    ln_g = nc.dram_tensor("ln_g", [DM], F32, kind="ExternalInput").ap()
    ln_b = nc.dram_tensor("ln_b", [DM], F32, kind="ExternalInput").ap()
    fc1_q = nc.dram_tensor("fc1_q", [DM, FF], I32, kind="ExternalInput").ap()
    fc1_s = nc.dram_tensor("fc1_s", [DM // GRP, FF], F32, kind="ExternalInput").ap()
    fc1_b = nc.dram_tensor("fc1_b", [FF], F32, kind="ExternalInput").ap()
    fc2_q = nc.dram_tensor("fc2_q", [FF, DM], I32, kind="ExternalInput").ap()
    fc2_s = nc.dram_tensor("fc2_s", [FF // GRP, DM], F32, kind="ExternalInput").ap()
    fc2_b = nc.dram_tensor("fc2_b", [DM], F32, kind="ExternalInput").ap()
    o_y = nc.dram_tensor("o_y", [TSL, DM], F32, kind="ExternalOutput").ap()
    s_hT = nc.dram_tensor("s_hT", [DM, TSL], F32).ap()
    s_res = nc.dram_tensor("s_res", [TSL, DM], F32).ap()
    s_attn = nc.dram_tensor("s_attn", [DM, TSL], F32).ap()
    s_w1 = nc.dram_tensor("s_w1", [DM, FF], F32).ap()
    s_w2 = nc.dram_tensor("s_w2", [FF, DM], F32).ap()

    with contextlib.ExitStack() as ctx:
        sbt = lambda name, shape, dt=F32: ctx.enter_context(nc.sbuf_tensor(name, shape, dt))
        pst = lambda name, shape: ctx.enter_context(nc.psum_tensor(name, shape, F32))

        AR1 = sbt("AR1", [128, 16384])
        AR2 = sbt("AR2", [128, 8192])
        AR3 = sbt("AR3", [128, 8192])
        AR4 = sbt("AR4", [128, 4096])
        AR5 = sbt("AR5", [128, 4096])
        AR6 = sbt("AR6", [128, 4096])
        AR7 = sbt("AR7", [128, 4096])

        st = sbt("stats", [128, 16])
        ident = sbt("ident", [128, 128])
        io_i = sbt("io_i", [128, 128], F32)
        io_p = sbt("io_p", [128, 1], F32)
        cst = sbt("cst", [128, 2])
        pbc = sbt("pbc", [128, NKT])
        f1bc = sbt("f1bc", [128, NOF])
        f2bc = sbt("f2bc", [128, NKT])

        xtm = [AR5[:, 2048:3072], AR5[:, 3072:4096]]
        grep = AR5[:, 0:1024]
        brep = AR5[:, 1024:2048]
        xsq = AR7[:, 0:1024]
        xn2 = AR7[:, 1024:2048]
        htm = [AR6[:, 0:1024], AR6[:, 1024:2048]]
        hstg = AR6[:, 2048:3072]
        mixc = AR1[:, 0:NKI * 512]
        ydT = [AR1[:, NKI * 512:NKI * 512 + NKT * 512], AR2[:, 0:NKT * 512]]
        wstr = [AR4[:, 0:2048], AR4[:, 2048:4096]]
        attc = AR6[:, 0:NKT * 512]
        qstg = [AR3[:, 0:4096], AR3[:, 4096:8192]]
        screp = [AR2[:, 0:4096], AR2[:, 4096:8192]]
        wdq = AR5[:, 0:4096]
        syb = AR1[:, 0:NOF * 512]
        hTc = [AR2[:, 0:4096], AR2[:, 4096:8192]]
        w2str = [AR3[:, 0:4096], AR3[:, 4096:8192]]
        ffnc = AR5[:, 0:NKT * 512]
        otm = [AR7[:, 0:1024], AR7[:, 1024:2048]]
        fxtm = [AR7[:, 2048:3072], AR7[:, 3072:4096]]

        pz = [pst("pz0", [128, 512]), pst("pz1", [128, 512]),
              pst("pz2", [128, 512]), pst("pz3", [128, 512])]

        s = Sched(nc)
        s.make_sems(ctx)
        block = ctx.enter_context(nc.Block())
        bank_free = {}
        buf_free = {}

        def mkcst(e):
            e.memset(cst[:, 0:1], EPS)
            return e.memset(cst[:, 1:2], 1.0)
        m_cst = s.step("vector", mkcst)
        m_io1 = s.step("gpsimd", lambda e: e.iota(
            io_i[:, :], [[1, 128]], channel_multiplier=0,
            allow_small_or_imprecise_dtypes=True))
        m_io2 = s.step("gpsimd", lambda e: e.iota(
            io_p[:, 0:1], [[0, 1]], channel_multiplier=1,
            allow_small_or_imprecise_dtypes=True))
        m_id = s.step("vector", lambda e: e.tensor_scalar(
            ident[:, :], io_i[:, :], io_p[:, 0:1], None, ALU.is_equal),
            waits=[m_io1, m_io2])

        def ldsetup(e, sem):
            e.dma_start(out=grep[0:1, :], in_=ln_g.unsqueeze(0)).then_inc(sem, 16)
            e.dma_start(out=brep[0:1, :], in_=ln_b.unsqueeze(0)).then_inc(sem, 16)
            for o in range(NKT):
                e.dma_start(out=pbc[:, o:o + 1],
                            in_=proj_b[o * 128:(o + 1) * 128].unsqueeze(1)).then_inc(sem, 16)
                e.dma_start(out=f2bc[:, o:o + 1],
                            in_=fc2_b[o * 128:(o + 1) * 128].unsqueeze(1)).then_inc(sem, 16)
            for o in range(NOF):
                e.dma_start(out=f1bc[:, o:o + 1],
                            in_=fc1_b[o * 128:(o + 1) * 128].unsqueeze(1)).then_inc(sem, 16)
        m_su = s.step("sync", ldsetup, dma=2 + 2 * NKT + NOF)
        m_db = m_su
        for k in [1, 2, 4, 8, 16, 32, 64]:
            def dbl(e, sem, k=k):
                e.dma_start(out=grep[k:2 * k, :], in_=grep[0:k, :]).then_inc(sem, 16)
                e.dma_start(out=brep[k:2 * k, :], in_=brep[0:k, :]).then_inc(sem, 16)
            m_db = s.step("sync", dbl, waits=[m_db], dma=2)

        tr_i = 0
        for t in range(NTT):
            m_dma = s.step("sync", lambda e, sem, t=t: e.dma_start(
                out=xtm[t % 2][:, :], in_=x[t * 128:(t + 1) * 128, :]).then_inc(sem, 16),
                waits=[buf_free.get(("xtm", t % 2))], dma=1)
            m_s1 = s.step("vector", lambda e, t=t: e.reduce_sum(
                st[:, 0:1], xtm[t % 2][:, :], AX.X), waits=[m_dma])
            m_s2 = s.step("scalar", lambda e, t=t: e.activation(
                xsq[:, :], xtm[t % 2][:, :], AF.Square,
                accum_out=st[:, 1:2]), waits=[m_dma])
            m_v1 = s.step("vector", lambda e: e.tensor_scalar(
                st[:, 2:3], st[:, 0:1], 1.0 / DM, None, ALU.mult), waits=[m_s1, m_s2])
            m_v1b = s.step("vector", lambda e: e.tensor_scalar(
                st[:, 3:4], st[:, 1:2], 1.0 / DM, None, ALU.mult), waits=[m_v1])
            m_v2 = s.step("vector", lambda e: e.tensor_tensor(
                st[:, 4:5], st[:, 2:3], st[:, 2:3], ALU.mult), waits=[m_v1b])
            m_vr = s.step("vector", lambda e: e.tensor_tensor(
                st[:, 5:6], st[:, 3:4], st[:, 4:5], ALU.subtract), waits=[m_v2])
            m_sq = s.step("scalar", lambda e: e.activation(
                st[:, 6:7], st[:, 5:6], AF.Sqrt, bias=cst[:, 0:1]),
                waits=[m_vr, m_cst])
            m_rc = s.step("vector", lambda e: e.reciprocal(
                st[:, 7:8], st[:, 6:7]), waits=[m_sq])
            m_xn = s.step("vector", lambda e, t=t: e.tensor_scalar(
                xsq[:, :], xtm[t % 2][:, :], st[:, 2:3], st[:, 7:8],
                ALU.subtract, ALU.mult), waits=[m_rc, m_s2])
            buf_free[("xtm", t % 2)] = m_xn
            m_rt = s.step("vector", lambda e, t=t: e.tensor_tensor(
                fxtm[t % 2][:, :], xtm[t % 2][:, :], xsq[:, :], ALU.add),
                waits=[m_xn, buf_free.get(("rtm", t % 2))])
            buf_free[("xtm", t % 2)] = m_rt
            m_str = s.step("gpsimd", lambda e, sem, t=t: e.dma_start(
                out=s_res[t * 128:(t + 1) * 128, :],
                in_=fxtm[t % 2][:, :]).then_inc(sem, 16), waits=[m_rt], dma=1)
            buf_free[("rtm", t % 2)] = m_str
            m_t1 = s.step("vector", lambda e: e.reduce_sum(
                st[:, 8:9], xsq[:, :], AX.X), waits=[m_xn])
            m_t2 = s.step("scalar", lambda e: e.activation(
                xn2[:, :], xsq[:, :], AF.Square, accum_out=st[:, 9:10]),
                waits=[m_xn])
            m_u1 = s.step("vector", lambda e: e.tensor_scalar(
                st[:, 10:11], st[:, 8:9], 1.0 / DM, None, ALU.mult), waits=[m_t1, m_t2])
            m_u1b = s.step("vector", lambda e: e.tensor_scalar(
                st[:, 11:12], st[:, 9:10], 1.0 / DM, None, ALU.mult), waits=[m_u1])
            m_u2 = s.step("vector", lambda e: e.tensor_tensor(
                st[:, 12:13], st[:, 10:11], st[:, 10:11], ALU.mult), waits=[m_u1b])
            m_ur = s.step("vector", lambda e: e.tensor_tensor(
                st[:, 13:14], st[:, 11:12], st[:, 12:13], ALU.subtract), waits=[m_u2])
            m_sq2 = s.step("scalar", lambda e: e.activation(
                st[:, 14:15], st[:, 13:14], AF.Sqrt, bias=cst[:, 0:1]),
                waits=[m_ur])
            m_rc2 = s.step("vector", lambda e: e.reciprocal(
                st[:, 15:16], st[:, 14:15]), waits=[m_sq2])
            m_xn2 = s.step("vector", lambda e: e.tensor_scalar(
                xn2[:, :], xsq[:, :], st[:, 10:11], st[:, 15:16],
                ALU.subtract, ALU.mult), waits=[m_rc2])
            m_h1 = s.step("vector", lambda e: e.tensor_tensor(
                xsq[:, :], xn2[:, :], grep[:, :], ALU.mult), waits=[m_xn2, m_db])
            m_h2 = s.step("vector", lambda e, t=t: e.tensor_tensor(
                htm[t % 2][:, :], xsq[:, :], brep[:, :], ALU.add),
                waits=[m_h1, buf_free.get(("htm", t % 2))])

            m_evs = []
            for dsub in range(NKT):
                bank = pz[tr_i % 2]
                m_tr = s.step("tensor", lambda e, t=t, dsub=dsub, bank=bank: e.transpose(
                    bank[:, 0:128], htm[t % 2][:, dsub * 128:(dsub + 1) * 128],
                    ident[:, :]), waits=[m_h2, m_id, bank_free.get(id(bank))])
                m_ev = s.step("scalar", lambda e, dsub=dsub, bank=bank: e.copy(
                    hstg[:, dsub * 128:(dsub + 1) * 128], bank[:, 0:128]),
                    waits=[m_tr, buf_free.get("hstg")])
                bank_free[id(bank)] = m_ev
                m_evs.append(m_ev)
                tr_i += 1
            buf_free[("htm", t % 2)] = m_evs[-1]

            def sthT(e, sem, t=t):
                for dsub in range(NKT):
                    e.dma_start(out=s_hT[dsub * 128:(dsub + 1) * 128, t * 128:(t + 1) * 128],
                                in_=hstg[:, dsub * 128:(dsub + 1) * 128]).then_inc(sem, 16)
            m_sth = s.step("sync", sthT, waits=m_evs, dma=NKT)
            buf_free["hstg"] = m_sth

        for g in range(NG):
            t0 = g * 512
            m_yd = [None, None]
            for d in range(2):
                def ldmix(e, sem, d=d, t0=t0):
                    for kt in range(NKI):
                        e.dma_start(out=mixc[:, kt * 512:(kt + 1) * 512],
                                    in_=i_mix[d, kt * 128:(kt + 1) * 128, t0:t0 + 512]).then_inc(sem, 16)
                m_mx = s.step("sync", ldmix, waits=[buf_free.get("mixc")], dma=NKI)
                m_evl = []
                for o in range(NKT):
                    par = o % 2
                    def ldow(e, sem, d=d, o=o, par=par):
                        for kt in range(NKI):
                            e.dma_start(out=wstr[par][:, kt * 128:(kt + 1) * 128],
                                        in_=out_w[d, kt * 128:(kt + 1) * 128,
                                                  o * 128:(o + 1) * 128]).then_inc(sem, 16)
                    m_w = s.step("sync", ldow, waits=[buf_free.get(("wstr", par))], dma=NKI)
                    bank = pz[o % 2]
                    def mm1(e, o=o, par=par, bank=bank):
                        for kt in range(NKI):
                            inst = e.matmul(bank[:, :], wstr[par][:, kt * 128:(kt + 1) * 128],
                                            mixc[:, kt * 512:(kt + 1) * 512],
                                            start=(kt == 0), stop=(kt == NKI - 1))
                        return inst
                    m_m = s.step("tensor", mm1, waits=[m_w, m_mx, bank_free.get(id(bank))])
                    buf_free[("wstr", par)] = m_m
                    m_e = s.step("scalar", lambda e, d=d, o=o, bank=bank: e.copy(
                        ydT[d][:, o * 512:(o + 1) * 512], bank[:, :]), waits=[m_m])
                    bank_free[id(bank)] = m_e
                    m_evl.append(m_e)
                m_yd[d] = m_evl
                buf_free["mixc"] = m_evl[-1]
            for o in range(NKT):
                par = o % 2
                def ldpj(e, sem, o=o, par=par):
                    for kt in range(2 * NKT):
                        e.dma_start(out=wstr[par][:, kt * 128:(kt + 1) * 128],
                                    in_=proj_w[kt * 128:(kt + 1) * 128,
                                               o * 128:(o + 1) * 128]).then_inc(sem, 16)
                m_w = s.step("sync", ldpj, waits=[buf_free.get(("wstr", par))], dma=2 * NKT)
                bank = pz[2 + o % 2]
                def mm2(e, o=o, par=par, bank=bank):
                    for kt in range(2 * NKT):
                        src = ydT[kt // NKT]
                        k2 = kt % NKT
                        inst = e.matmul(bank[:, :], wstr[par][:, kt * 128:(kt + 1) * 128],
                                        src[:, k2 * 512:(k2 + 1) * 512],
                                        start=(kt == 0), stop=(kt == 2 * NKT - 1))
                    return inst
                m_m = s.step("tensor", mm2,
                             waits=[m_w, m_yd[0][-1], m_yd[1][-1], bank_free.get(id(bank))])
                buf_free[("wstr", par)] = m_m
                m_e = s.step("scalar", lambda e, o=o, bank=bank: e.activation(
                    hstg[:, o * 128:(o + 1) * 128] if False else attc[:, o * 512:(o + 1) * 512],
                    bank[:, :], AF.Identity, bias=pbc[:, o:o + 1]),
                    waits=[m_m, m_su, buf_free.get("attc")])
                bank_free[id(bank)] = m_e
                m_st = s.step("gpsimd", lambda e, sem, o=o, t0=t0: e.dma_start(
                    out=s_attn[o * 128:(o + 1) * 128, t0:t0 + 512],
                    in_=attc[:, o * 512:(o + 1) * 512]).then_inc(sem, 16),
                    waits=[m_e], dma=1)
            buf_free["attc"] = m_st
        buf_free[("screp", 0)] = m_m
        buf_free[("screp", 1)] = m_m
        buf_free[("qstg", 0)] = m_m
        buf_free[("qstg", 1)] = m_m

        for (qsrc, ssrc, wdst, nkt, ncols) in (
                (fc1_q, fc1_s, s_w1, NKT, FF), (fc2_q, fc2_s, s_w2, NOF, DM)):
            for kt in range(nkt):
                par = kt % 2
                m_q = s.step("gpsimd", lambda e, sem, qsrc=qsrc, kt=kt, par=par, ncols=ncols: e.dma_start(
                    out=qstg[par][:, 0:ncols],
                    in_=qsrc[kt * 128:(kt + 1) * 128, :]).then_inc(sem, 16),
                    waits=[buf_free.get(("qstg", par))], dma=1)
                m_sp = s.step("sync", lambda e, sem, ssrc=ssrc, kt=kt, par=par, ncols=ncols: e.dma_start(
                    out=screp[par][0:1, 0:ncols],
                    in_=ssrc[kt:kt + 1, :]).then_inc(sem, 16),
                    waits=[buf_free.get(("screp", par))], dma=1)
                for k in [1, 2, 4, 8, 16, 32, 64]:
                    m_sp = s.step("sync", lambda e, sem, k=k, par=par, ncols=ncols: e.dma_start(
                        out=screp[par][k:2 * k, 0:ncols],
                        in_=screp[par][0:k, 0:ncols]).then_inc(sem, 16), dma=1)
                m_dq = s.step("vector", lambda e, par=par, ncols=ncols: e.scalar_tensor_tensor(
                    wdq[:, 0:ncols], qstg[par][:, 0:ncols], 128.0,
                    screp[par][:, 0:ncols], ALU.subtract, ALU.mult),
                    waits=[m_q, m_sp, buf_free.get("wdq")])
                buf_free[("qstg", par)] = m_dq
                buf_free[("screp", par)] = m_dq
                m_sw = s.step("scalar", lambda e, sem, wdst=wdst, kt=kt, ncols=ncols: e.dma_start(
                    out=wdst[kt * 128:(kt + 1) * 128, :],
                    in_=wdq[:, 0:ncols]).then_inc(sem, 16), waits=[m_dq], dma=1)
                buf_free["wdq"] = m_sw

        buf_free[("hTc", 0)] = m_dq
        buf_free[("hTc", 1)] = m_dq
        buf_free[("w2str", 0)] = m_dq
        buf_free[("w2str", 1)] = m_dq
        m_w1_done = m_sw
        for g in range(NG):
            t0 = g * 512
            par = g % 2
            def ldh(e, sem, t0=t0, par=par):
                for kt in range(NKT):
                    e.dma_start(out=hTc[par][:, kt * 512:(kt + 1) * 512],
                                in_=s_hT[kt * 128:(kt + 1) * 128, t0:t0 + 512]).then_inc(sem, 16)
            m_h = s.step("sync", ldh, waits=[buf_free.get(("hTc", par))], dma=NKT)
            m_sy = []
            for o in range(NOF):
                wpar = o % 2
                def ldw1(e, sem, o=o, wpar=wpar):
                    for kt in range(NKT):
                        e.dma_start(out=wstr[wpar][:, kt * 128:(kt + 1) * 128],
                                    in_=s_w1[kt * 128:(kt + 1) * 128,
                                             o * 128:(o + 1) * 128]).then_inc(sem, 16)
                m_w = s.step("sync", ldw1,
                             waits=[buf_free.get(("wstr", wpar)), m_w1_done], dma=NKT)
                bank = pz[o % 2]
                def mmf1(e, o=o, wpar=wpar, par=par, bank=bank):
                    for kt in range(NKT):
                        inst = e.matmul(bank[:, :], wstr[wpar][:, kt * 128:(kt + 1) * 128],
                                        hTc[par][:, kt * 512:(kt + 1) * 512],
                                        start=(kt == 0), stop=(kt == NKT - 1))
                    return inst
                m_m = s.step("tensor", mmf1, waits=[m_w, m_h, bank_free.get(id(bank))])
                buf_free[("wstr", wpar)] = m_m
                m_e = s.step("scalar", lambda e, o=o, bank=bank: e.activation(
                    syb[:, o * 512:(o + 1) * 512], bank[:, :], AF.Silu,
                    bias=f1bc[:, o:o + 1]), waits=[m_m, m_su, buf_free.get("syb")])
                bank_free[id(bank)] = m_e
                m_sy.append(m_e)
            buf_free[("hTc", par)] = m_m
            m_fc2 = []
            for o in range(NKT):
                wpar = o % 2
                def ldw2(e, sem, o=o, wpar=wpar):
                    for kt in range(NOF):
                        e.dma_start(out=w2str[wpar][:, kt * 128:(kt + 1) * 128],
                                    in_=s_w2[kt * 128:(kt + 1) * 128,
                                             o * 128:(o + 1) * 128]).then_inc(sem, 16)
                m_w = s.step("sync", ldw2,
                             waits=[buf_free.get(("w2str", wpar)), m_w_done], dma=NOF)
                bank = pz[2 + o % 2]
                def mmf2(e, o=o, wpar=wpar, bank=bank):
                    for kt in range(NOF):
                        inst = e.matmul(bank[:, :], w2str[wpar][:, kt * 128:(kt + 1) * 128],
                                        syb[:, kt * 512:(kt + 1) * 512],
                                        start=(kt == 0), stop=(kt == NOF - 1))
                    return inst
                m_m = s.step("tensor", mmf2, waits=[m_w, bank_free.get(id(bank))] + m_sy)
                buf_free[("w2str", wpar)] = m_m
                m_e = s.step("vector", lambda e, o=o, bank=bank: e.tensor_scalar(
                    ffnc[:, o * 512:(o + 1) * 512], bank[:, :],
                    f2bc[:, o:o + 1], 0.5, ALU.add, ALU.mult),
                    waits=[m_m, m_su, buf_free.get("ffnc")])
                bank_free[id(bank)] = m_e
                m_fc2.append(m_e)
            buf_free["syb"] = m_m
            def ldat(e, sem, t0=t0):
                for o in range(NKT):
                    e.dma_start(out=attc[:, o * 512:(o + 1) * 512],
                                in_=s_attn[o * 128:(o + 1) * 128, t0:t0 + 512]).then_inc(sem, 16)
            m_at = s.step("sync", ldat, waits=[buf_free.get("attc")], dma=NKT)
            m_fa = s.step("vector", lambda e: e.tensor_tensor(
                ffnc[:, :], ffnc[:, :], attc[:, :], ALU.add),
                waits=[m_at] + m_fc2)
            buf_free["attc"] = m_fa
            buf_free["ffnc"] = m_fa
            for ts in range(4):
                tpar = ts % 2
                m_x = s.step("sync", lambda e, sem, t0=t0, ts=ts, tpar=tpar: e.dma_start(
                    out=fxtm[tpar][:, :],
                    in_=s_res[t0 + ts * 128:t0 + (ts + 1) * 128, :]).then_inc(sem, 16),
                    waits=[buf_free.get(("fxtm", tpar)), buf_free.get(("rtm", tpar))], dma=1)
                m_adds = []
                for o in range(NKT):
                    bank = pz[(ts * NKT + o) % 2]
                    m_tr = s.step("tensor", lambda e, o=o, ts=ts, bank=bank: e.transpose(
                        bank[:, 0:128],
                        ffnc[:, o * 512 + ts * 128: o * 512 + (ts + 1) * 128],
                        ident[:, :]),
                        waits=[m_fa, m_id, bank_free.get(id(bank))])
                    m_ad = s.step("vector", lambda e, o=o, ts=ts, tpar=tpar, bank=bank: e.tensor_tensor(
                        otm[tpar][:, o * 128:(o + 1) * 128], bank[:, 0:128],
                        fxtm[tpar][:, o * 128:(o + 1) * 128], ALU.add),
                        waits=[m_tr, m_x, buf_free.get(("otm", tpar))])
                    bank_free[id(bank)] = m_ad
                    m_adds.append(m_ad)
                m_so = s.step("sync", lambda e, sem, t0=t0, ts=ts, tpar=tpar: e.dma_start(
                    out=o_y[t0 + ts * 128:t0 + (ts + 1) * 128, :],
                    in_=otm[tpar][:, :]).then_inc(sem, 16), waits=m_adds, dma=1)
                buf_free[("otm", tpar)] = m_so
                buf_free[("fxtm", tpar)] = m_adds[-1]

        s.finish(block)
    return nc


def _run_L3(mix, inputs):
    xfull = np.asarray(inputs["L0_attn_x"]).reshape(L, DM).astype(np.float32)
    nc = build_L3()
    shared = {
        "out_w": np.ascontiguousarray(inputs["L0_attn_out_w"], np.float32),
        "proj_w": np.ascontiguousarray(inputs["L0_attn_proj_w"], np.float32),
        "proj_b": np.ascontiguousarray(inputs["L0_attn_proj_b"], np.float32),
        "ln_g": np.ascontiguousarray(inputs["L0_ffn_ln_gamma"], np.float32),
        "ln_b": np.ascontiguousarray(inputs["L0_ffn_ln_beta"], np.float32),
        "fc1_q": np.ascontiguousarray(inputs["L0_ffn_fc1_q"], np.int32),
        "fc1_s": np.ascontiguousarray(inputs["L0_ffn_fc1_scales"], np.float32),
        "fc1_b": np.ascontiguousarray(inputs["L0_ffn_fc1_bias"], np.float32),
        "fc2_q": np.ascontiguousarray(inputs["L0_ffn_fc2_q"], np.int32),
        "fc2_s": np.ascontiguousarray(inputs["L0_ffn_fc2_scales"], np.float32),
        "fc2_b": np.ascontiguousarray(inputs["L0_ffn_fc2_bias"], np.float32),
    }
    in_maps = []
    for k in range(NCORES):
        m = dict(shared)
        m["x"] = np.ascontiguousarray(xfull[k * TSL:(k + 1) * TSL])
        m["i_mix"] = np.ascontiguousarray(mix[:, :, k * TSL:(k + 1) * TSL])
        in_maps.append(m)
    res = run_bass_kernel_spmd(nc, in_maps, core_ids=list(range(NCORES)))
    return np.concatenate([r["o_y"] for r in res.results], axis=0)



def _barrier(s):
    cur = [(k, s.val[k]) for k in s.SEMS if s.val[k] > 0]
    out = []
    for eng in ("tensor", "vector", "scalar", "gpsimd"):
        out.append(s.step(eng, lambda e: e.nop(), waits=cur))
    return out


def build_fused():
    T = 256
    NCH = L // T
    NU = DSL // 128
    CW = 512
    NGC = L // CW
    NKT = DM // 128
    NOF = FF // 128
    nc = bass.Bass("TRN2", target_bir_lowering=False, detect_race_conditions=False)

    x = nc.dram_tensor("x", [L, DM], F32, kind="ExternalInput").ap()
    xs = nc.dram_tensor("xs", [TSL, DM], F32, kind="ExternalInput").ap()
    inw = nc.dram_tensor("inw", [2, DM, 2 * DSL], F32, kind="ExternalInput").ap()
    conv_w = nc.dram_tensor("conv_w", [2, DSL, DCONV], F32, kind="ExternalInput").ap()
    conv_b = nc.dram_tensor("conv_b", [2, DSL], F32, kind="ExternalInput").ap()
    xp_w = nc.dram_tensor("xp_w", [2, DSL, DTR + 2 * NST], F32, kind="ExternalInput").ap()
    dt_w = nc.dram_tensor("dt_w", [2, DTR, DSL], F32, kind="ExternalInput").ap()
    dt_b = nc.dram_tensor("dt_b", [2, DSL], F32, kind="ExternalInput").ap()
    i_alog = nc.dram_tensor("i_alog", [2, DSL, NST], F32, kind="ExternalInput").ap()
    i_d = nc.dram_tensor("i_d", [2, DSL], F32, kind="ExternalInput").ap()
    ow = nc.dram_tensor("ow", [2, DSL, DM], F32, kind="ExternalInput").ap()
    proj_w = nc.dram_tensor("proj_w", [2 * DM, DM], F32, kind="ExternalInput").ap()
    proj_b = nc.dram_tensor("proj_b", [DM], F32, kind="ExternalInput").ap()
    ln_g = nc.dram_tensor("ln_g", [DM], F32, kind="ExternalInput").ap()
    ln_b = nc.dram_tensor("ln_b", [DM], F32, kind="ExternalInput").ap()
    fc1_q = nc.dram_tensor("fc1_q", [DM, FF], I32, kind="ExternalInput").ap()
    fc1_s = nc.dram_tensor("fc1_s", [DM // GRP, FF], F32, kind="ExternalInput").ap()
    fc1_b = nc.dram_tensor("fc1_b", [FF], F32, kind="ExternalInput").ap()
    fc2_q = nc.dram_tensor("fc2_q", [FF, DM], I32, kind="ExternalInput").ap()
    fc2_s = nc.dram_tensor("fc2_s", [FF // GRP, DM], F32, kind="ExternalInput").ap()
    fc2_b = nc.dram_tensor("fc2_b", [DM], F32, kind="ExternalInput").ap()
    o_y = nc.dram_tensor("o_y", [TSL, DM], F32, kind="ExternalOutput").ap()

    s_dt = nc.dram_tensor("s_dt", [2, DSL, L], F32).ap()
    s_xc = nc.dram_tensor("s_xc", [2, DSL, L], F32).ap()
    s_sz = nc.dram_tensor("s_sz", [2, DSL, L], F32).ap()
    s_dbc_in = nc.dram_tensor("s_dbc_in", [2, 96, L], F32).ap()
    s_dbc = nc.dram_tensor("s_dbc", [2, 96, L], F32, addr_space="Shared").ap()
    s_mix = nc.dram_tensor("s_mix", [2, DSL, L], F32).ap()
    s_part = nc.dram_tensor("s_part", [L, DM], F32).ap()
    s_attn = nc.dram_tensor("s_attn", [TSL, DM], F32).ap()
    s_w1 = nc.dram_tensor("s_w1", [DM, FF], F32).ap()
    s_w2 = nc.dram_tensor("s_w2", [FF, DM], F32).ap()
    s_hT = nc.dram_tensor("s_hT", [DM, TSL], F32).ap()
    s_res = nc.dram_tensor("s_res", [TSL, DM], F32).ap()

    with contextlib.ExitStack() as ctx:
        sbt = lambda name, shape, dt=F32: ctx.enter_context(nc.sbuf_tensor(name, shape, dt))
        pst = lambda name, shape: ctx.enter_context(nc.psum_tensor(name, shape, F32))

        AR1 = sbt("AR1", [128, 24576])
        AR2 = sbt("AR2", [128, 8192])
        AR3 = sbt("AR3", [128, 8192])
        AR4 = sbt("AR4", [128, 4096])
        AR5 = sbt("AR5", [128, 2048])
        AR6 = sbt("AR6", [128, 2048])
        AR7 = sbt("AR7", [128, 2048])

        def rep16(sl, T):
            return bass.AP(sl.tensor, sl.offset, [[sl.ap[0][0], 128], [0, NST], [1, T]])

        st = sbt("stats", [128, 32])
        ident = sbt("ident", [128, 128])
        io_i = sbt("io_i", [128, 128], F32)
        io_p = sbt("io_p", [128, 1], F32)
        cst = sbt("cst", [128, 2])
        smallw = sbt("smallw", [128, 32])
        dtw_sb = sbt("dtw_sb", [64, 2 * DSL])
        xpw_sb = sbt("xpw_sb", [128, 2 * NU * 96])
        Atile = sbt("Atile", [128, 2 * NU * NST])
        carry = sbt("carry", [128, 2 * NU * NST])
        pbc = sbt("pbc", [128, NKT])
        f1bc = sbt("f1bc", [128, NOF])
        f2bc = sbt("f2bc", [128, NKT])

        pzb = pst("pzb", [128, 2048])
        pdt = pst("pdt", [128, 512])
        pz = [pzb[:, k * 512:(k + 1) * 512] for k in range(4)]
        ypzb = [pz[2], pz[3]]
        pdb = [pst("pdb0", [96, 512]), pst("pdb1", [96, 512])]

        s = Sched(nc)
        s.make_sems(ctx)
        block = ctx.enter_context(nc.Block())
        bank_free = {}
        buf_free = {}

        def mkcst(e):
            e.memset(cst[:, 0:1], EPS)
            return e.memset(cst[:, 1:2], 1.0)
        m_cst = s.step("vector", mkcst)
        onesrow = AR6[0:1, 1792:1920]
        onesrow32 = AR6[32:33, 1792:1920]
        def mkones(e):
            e.memset(onesrow, 1.0)
            return e.memset(onesrow32, 1.0)
        m_ones = s.step("vector", mkones)
        m_io1 = s.step("gpsimd", lambda e: e.iota(
            io_i[:, :], [[1, 128]], channel_multiplier=0,
            allow_small_or_imprecise_dtypes=True))
        m_io2 = s.step("gpsimd", lambda e: e.iota(
            io_p[:, 0:1], [[0, 1]], channel_multiplier=1,
            allow_small_or_imprecise_dtypes=True))
        m_id = s.step("vector", lambda e: e.tensor_scalar(
            ident[:, :], io_i[:, :], io_p[:, 0:1], None, ALU.is_equal),
            waits=[m_io1, m_io2])

        def ldsmall(e, sem):
            for d in range(2):
                for u in range(NU):
                    i = d * NU + u
                    r0, r1 = u * 128, (u + 1) * 128
                    e.dma_start(out=smallw[:, i * 4:(i + 1) * 4],
                                in_=conv_w[d, r0:r1, :]).then_inc(sem, 16)
                    e.dma_start(out=smallw[:, 16 + i:17 + i],
                                in_=conv_b[d, r0:r1].unsqueeze(1)).then_inc(sem, 16)
                    e.dma_start(out=smallw[:, 20 + i:21 + i],
                                in_=dt_b[d, r0:r1].unsqueeze(1)).then_inc(sem, 16)
                    e.dma_start(out=smallw[:, 24 + i:25 + i],
                                in_=i_d[d, r0:r1].unsqueeze(1)).then_inc(sem, 16)
                    e.dma_start(out=Atile[:, i * NST:(i + 1) * NST],
                                in_=i_alog[d, r0:r1, :]).then_inc(sem, 16)
                    e.dma_start(out=xpw_sb[:, i * 96:(i + 1) * 96],
                                in_=xp_w[d, r0:r1, :]).then_inc(sem, 16)
                e.dma_start(out=dtw_sb[0:64, d * DSL:(d + 1) * DSL],
                            in_=dt_w[d, :, :]).then_inc(sem, 16)
            for o in range(NKT):
                e.dma_start(out=pbc[:, o:o + 1],
                            in_=proj_b[o * 128:(o + 1) * 128].unsqueeze(1)).then_inc(sem, 16)
                e.dma_start(out=f2bc[:, o:o + 1],
                            in_=fc2_b[o * 128:(o + 1) * 128].unsqueeze(1)).then_inc(sem, 16)
            for o in range(NOF):
                e.dma_start(out=f1bc[:, o:o + 1],
                            in_=fc1_b[o * 128:(o + 1) * 128].unsqueeze(1)).then_inc(sem, 16)
        m_smw = s.step("sync", ldsmall, dma=2 * (6 * NU + 1) + 2 * NKT + NOF)

        m_ae = s.step("scalar", lambda e: e.activation(
            carry[:, :], Atile[:, :], AF.Exp), waits=[m_smw])
        m_an = s.step("vector", lambda e: e.tensor_scalar(
            Atile[:, :], carry[:, :], -1.0, None, ALU.mult), waits=[m_ae])
        m_cz = s.step("vector", lambda e: e.memset(carry[:, :], 0.0), waits=[m_an])

        def ldinw(e, sem):
            for d in range(2):
                for kt in range(NKT):
                    e.dma_start(
                        out=AR1[:, kt * 1024 + d * 512: kt * 1024 + d * 512 + 512],
                        in_=inw[d, kt * 128:(kt + 1) * 128, :]).then_inc(sem, 16)
        m_inw = s.step("sync", ldinw, dma=2 * NKT)

        xtm = [AR2[:, 0:1024], AR2[:, 1024:2048],
               AR3[:, 0:1024], AR3[:, 1024:2048]]
        xsq = AR2[:, 2048:3072]
        xnb = [AR2[:, 3072:4096], AR2[:, 4096:5120],
               AR3[:, 2048:3072], AR3[:, 3072:4096]]
        xnTb = [AR1[:, 8192:12288], AR1[:, 12288:16384]]

        def xiring(d, u, j):
            base = (d * NU + u) * 3 + j
            return AR1[:, 16384 + base * 512:16384 + (base + 1) * 512]
        cacc = [AR1[:, 22528:23040], AR1[:, 23040:23552]]
        xcc = AR1[:, 23552:24064]
        szc = AR1[:, 24064:24576]
        dbev = AR2[:, 5120:5632]
        dtT = AR2[:, 5632:6144]
        spe = AR2[:, 6144:6656]
        zpad = AR2[:, 6656:7168]

        m_zr = s.step("vector", lambda e: e.memset(zpad[:, :], 0.0))
        m_zr2 = None
        def zring(e):
            for d in range(2):
                for u in range(NU):
                    e.memset(xiring(d, u, 2), 0.0)
            return e.memset(xiring(1, 1, 2), 0.0)
        m_zr2 = s.step("vector", zring)

        tr_i = 0
        m_xi_ev = {}
        m_conv_done = {}
        m_ln_prev = None
        m_xnT_free = [None, None]
        m_last_st = None
        for c in range(NGC + 1):
            cpar = c % 2
            if c < NGC:
                m_evs_t = []
                for tt in range(4):
                    t_abs = c * 4 + tt
                    par = t_abs % 4
                    sb = par * 8
                    m_dma = s.step("sync", lambda e, sem, t_abs=t_abs, par=par: e.dma_start(
                        out=xtm[par][:, :],
                        in_=x[t_abs * 128:(t_abs + 1) * 128, :]).then_inc(sem, 16),
                        waits=[buf_free.get(("xtm", par))], dma=1)
                    m_s1 = s.step("vector", lambda e, par=par, sb=sb: e.reduce_sum(
                        st[:, sb:sb + 1], xtm[par][:, :], AX.X),
                        waits=[m_dma, buf_free.get(("st", par))])
                    m_s2 = s.step("scalar", lambda e, par=par, sb=sb: e.activation(
                        xsq[:, :], xtm[par][:, :], AF.Square,
                        accum_out=st[:, sb + 1:sb + 2]),
                        waits=[m_dma, buf_free.get(("st", par)), m_ln_prev])
                    m_ln_prev = m_s2
                    m_v1 = s.step("vector", lambda e, sb=sb: e.tensor_scalar(
                        st[:, sb + 2:sb + 4], st[:, sb:sb + 2], 1.0 / DM, None, ALU.mult),
                        waits=[m_s1, m_s2])
                    m_v2 = s.step("vector", lambda e, sb=sb: e.tensor_tensor(
                        st[:, sb + 4:sb + 5], st[:, sb + 2:sb + 3], st[:, sb + 2:sb + 3], ALU.mult), waits=[m_v1])
                    m_vr = s.step("vector", lambda e, sb=sb: e.tensor_tensor(
                        st[:, sb + 5:sb + 6], st[:, sb + 3:sb + 4], st[:, sb + 4:sb + 5], ALU.subtract), waits=[m_v2])
                    m_sq = s.step("scalar", lambda e, sb=sb: e.activation(
                        st[:, sb + 6:sb + 7], st[:, sb + 5:sb + 6], AF.Sqrt, bias=cst[:, 0:1]),
                        waits=[m_vr, m_cst])
                    m_rc = s.step("vector", lambda e, sb=sb: e.reciprocal(
                        st[:, sb + 7:sb + 8], st[:, sb + 6:sb + 7]), waits=[m_sq])
                    m_xn = s.step("vector", lambda e, par=par, sb=sb: e.tensor_scalar(
                        xnb[par][:, :], xtm[par][:, :], st[:, sb + 2:sb + 3], st[:, sb + 7:sb + 8],
                        ALU.subtract, ALU.mult),
                        waits=[m_rc, buf_free.get(("xnb", par))])
                    buf_free[("st", par)] = m_xn
                    buf_free[("xtm", par)] = m_xn
                    last_tr = None
                    for g4 in range(2):
                        bank = pz[tr_i % 2]
                        def tr4(e, par=par, g4=g4, bank=bank):
                            for q in range(4):
                                dsub = g4 * 4 + q
                                inst = e.transpose(
                                    bank[:, q * 128:(q + 1) * 128],
                                    xnb[par][:, dsub * 128:(dsub + 1) * 128],
                                    ident[:, :])
                            return inst
                        m_tr = s.step("tensor", tr4,
                                      waits=[m_xn, m_id, bank_free.get(id(bank))])
                        def ev4(e, tt=tt, g4=g4, bank=bank, cpar=cpar):
                            dst = xnTb[cpar]
                            out = bass.AP(
                                dst.tensor,
                                dst.offset + g4 * 4 * 512 + tt * 128,
                                [[dst.ap[0][0], 128], [512, 4], [1, 128]])
                            return e.copy(out, bank[:, :])
                        m_ev = s.step("scalar", ev4, waits=[m_tr, m_xnT_free[cpar]])
                        bank_free[id(bank)] = m_ev
                        m_evs_t.append(m_ev)
                        last_tr = m_tr
                        tr_i += 1
                    buf_free[("xnb", par)] = last_tr

                m_mm_last = None
                for d in range(2):
                    for ot in range(4):
                        bank = pz[2 + (ot % 2)]
                        is_xi = ot < NU
                        u = ot % NU
                        col0 = d * 512 + (0 if is_xi else 256) + u * 128

                        def xzmm(e, col0=col0, bank=bank, cpar=cpar):
                            for kt in range(NKT):
                                inst = e.matmul(
                                    bank[:, :],
                                    AR1[:, kt * 1024 + col0: kt * 1024 + col0 + 128],
                                    xnTb[cpar][:, kt * 512:(kt + 1) * 512],
                                    start=(kt == 0), stop=(kt == NKT - 1))
                            return inst
                        m_mm = s.step("tensor", xzmm,
                                      waits=[m_evs_t[-1], m_inw, bank_free.get(id(bank))])
                        m_mm_last = m_mm
                        if is_xi:
                            ring = c % 3
                            m_ev = s.step("scalar", lambda e, d=d, u=u, ring=ring, bank=bank: e.copy(
                                xiring(d, u, ring), bank[:, :]),
                                waits=[m_mm, m_conv_done.get((d, u, c - 2))])
                            m_xi_ev[(d, u, c)] = m_ev
                        else:
                            m_ev = s.step("scalar", lambda e, bank=bank: e.activation(
                                szc[:, :], bank[:, :], AF.Silu),
                                waits=[m_mm, buf_free.get("szc")])
                            m_st = s.step("gpsimd", lambda e, sem, d=d, u=u, c=c: e.dma_start(
                                out=s_sz[d, u * 128:(u + 1) * 128, c * CW:(c + 1) * CW],
                                in_=szc[:, :]).then_inc(sem, 16), waits=[m_ev], dma=1)
                            m_last_st = m_st
                            buf_free["szc"] = m_st
                        bank_free[id(bank)] = m_ev
                m_xnT_free[cpar] = m_mm_last

            if c >= 1:
                cc = c - 1
                for d in range(2):
                    for u in range(NU):
                        rm1 = xiring(d, u, (cc - 1) % 3) if cc >= 1 else xiring(d, u, 2)
                        r0 = xiring(d, u, cc % 3)
                        rp1 = (xiring(d, u, (cc + 1) % 3) if cc + 1 < NGC else zpad)
                        wv = [m_xi_ev.get((d, u, cc)), m_xi_ev.get((d, u, cc - 1)),
                              m_xi_ev.get((d, u, cc + 1)), m_zr, m_zr2, m_smw]
                        i = d * NU + u
                        offs = [j - 3 for j in range(4)] if d == 0 else [3 - j for j in range(4)]
                        m_c = None
                        for j in range(4):
                            o = offs[j]
                            wcol = smallw[:, i * 4 + j: i * 4 + j + 1]
                            segs = []
                            if o < 0:
                                segs.append((0, -o, rm1, 512 + o))
                                segs.append((-o, 512 + o, r0, 0))
                            elif o == 0:
                                segs.append((0, 512, r0, 0))
                            else:
                                segs.append((0, 512 - o, r0, o))
                                segs.append((512 - o, o, rp1, 0))
                            for (d0, n, sbuf, s0) in segs:
                                if j == 0:
                                    m_c = s.step("vector", lambda e, d0=d0, n=n, sbuf=sbuf, s0=s0, wcol=wcol: e.tensor_scalar(
                                        cacc[0][:, d0:d0 + n], sbuf[:, s0:s0 + n],
                                        wcol, None, ALU.mult),
                                        waits=wv + [buf_free.get("cacc"), m_c])
                                else:
                                    dst = cacc[j % 2]
                                    srcacc = cacc[(j + 1) % 2]
                                    m_c = s.step("vector", lambda e, d0=d0, n=n, sbuf=sbuf, s0=s0, wcol=wcol, dst=dst, srcacc=srcacc: e.scalar_tensor_tensor(
                                        dst[:, d0:d0 + n], sbuf[:, s0:s0 + n],
                                        wcol, srcacc[:, d0:d0 + n], ALU.mult, ALU.add),
                                        waits=[m_c])
                        m_conv_done[(d, u, cc)] = m_c
                        m_xc = s.step("scalar", lambda e, i=i: e.activation(
                            xcc[:, :], cacc[1][:, :], AF.Silu,
                            bias=smallw[:, 16 + i:17 + i]),
                            waits=[m_c, buf_free.get("xcc")])
                        buf_free["cacc"] = m_xc
                        m_stxc = s.step("gpsimd", lambda e, sem, d=d, u=u, cc=cc: e.dma_start(
                            out=s_xc[d, u * 128:(u + 1) * 128, cc * CW:(cc + 1) * CW],
                            in_=xcc[:, :]).then_inc(sem, 16), waits=[m_xc], dma=1)
                        m_last_st = m_stxc
                        bankd = pdb[d]
                        m_db = s.step("tensor", lambda e, u=u, i=i, bankd=bankd: e.matmul(
                            bankd[:, :], xpw_sb[:, i * 96:(i + 1) * 96], xcc[:, :],
                            start=(u == 0), stop=(u == NU - 1)),
                            waits=[m_xc, m_smw, bank_free.get(id(bankd))])
                        buf_free["xcc"] = m_db
                        if u == NU - 1:
                            m_dbe = s.step("scalar", lambda e, bankd=bankd: e.copy(
                                dbev[0:96, 0:512], bankd[:, :]),
                                waits=[m_db, buf_free.get("dbev")])
                            bank_free[id(bankd)] = m_dbe
                            m_std = s.step("gpsimd", lambda e, sem, d=d, cc=cc: e.dma_start(
                                out=s_dbc_in[d, :, cc * CW:(cc + 1) * CW],
                                in_=dbev[0:96, 0:512]).then_inc(sem, 16),
                                waits=[m_dbe], dma=1)
                            buf_free["dbev"] = m_std
                            m_last_st = m_std

        m_ar = s.step("gpsimd", lambda e: e.collective_compute(
            "AllReduce", ALU.add, replica_groups=[list(range(NCORES))],
            ins=[s_dbc_in[:]], outs=[s_dbc[:]]), waits=[m_last_st])


        barB = _barrier(s)
        dAb = [AR1[:, 0:4096], AR1[:, 12288:16384]]
        ubb = [AR1[:, 4096:8192], AR1[:, 16384:20480]]
        hbb = AR1[:, 8192:12288]
        BCrep = [AR2[:, 0:8192], AR3[:, 0:8192]]
        bcB = AR1[0:1, 20480:24576]
        bcC = AR1[32:33, 20480:24576]
        dtstg = [AR6[0:64, 0:256], AR6[0:64, 256:512]]
        dtt = [AR6[:, 512:1024], AR6[:, 1024:1536]]
        dt_c = [[AR4[:, (p * 2 + u) * T:(p * 2 + u + 1) * T]
                 for u in range(NU)] for p in range(2)]
        xc_c = [[AR4[:, 1024 + (p * 2 + u) * T: 1024 + (p * 2 + u + 1) * T]
                 for u in range(NU)] for p in range(2)]
        sz_c = [[AR4[:, 2048 + (p * 2 + u) * T: 2048 + (p * 2 + u + 1) * T]
                 for u in range(NU)] for p in range(2)]
        vb = AR4[:, 3072:3072 + T]
        yb = AR4[:, 3584:3584 + T]
        t1b = AR4[:, 3840:3840 + T]
        mixb = [AR5[:, 0:T], AR5[:, T:2 * T]]

        m_dA_free = [None, None]
        m_u_free = [None, None]
        m_h_free = None
        m_mix_last = None
        pend_st = []
        pend_t1 = []

        def emit_t1(ent):
            qi, qu, qpar, qycol, qmpar, qd, qt0, qymm = ent
            m_t1 = s.step("vector", lambda e, i=qi, u=qu, par=qpar, ycol=qycol: e.scalar_tensor_tensor(
                t1b[:, :], xc_c[par][u][:, :], smallw[:, 24 + i:25 + i],
                ypzb[ycol][:, 0:256],
                ALU.mult, ALU.add), waits=[qymm, m_smw])
            buf_free[("ypz", qycol)] = m_t1
            m_mx = s.step("vector", lambda e, mpar=qmpar, par=qpar, u=qu: e.tensor_tensor(
                mixb[mpar][:, :], t1b[:, :], sz_c[par][u][:, :], ALU.mult),
                waits=[m_t1, buf_free.get(("mix", qmpar))])
            pend_st.append((qd, qu, qt0, qmpar, m_mx))
            if qu == NU - 1:
                buf_free[("chunk", qpar)] = m_mx

        chunk_list = []
        for d in range(2):
            for ci in range(NCH):
                chunk_list.append(
                    (d, ci, ci * T if d == 0 else L - (ci + 1) * T, ci % 2))

        def emit_bcast(k):
            d, ci, t0, par = chunk_list[k]
            def bcs(e, sem, d=d, t0=t0, par=par):
                e.dma_start(out=bcB[0:1, :],
                            in_=s_dbc[d, 64:80, t0:t0 + T]).then_inc(sem, 16)
                e.dma_start(out=bcC[0:1, :],
                            in_=s_dbc[d, 80:96, t0:t0 + T]).then_inc(sem, 16)
                e.dma_start(out=dtstg[par][0:64, :],
                            in_=s_dbc[d, 0:64, t0:t0 + T]).then_inc(sem, 16)
            m_bs = s.step("sync", bcs,
                          waits=[buf_free.get("bcstg"), m_ar,
                                 buf_free.get(("dtstg", par))] + barB, dma=3)
            def dtmm(e, d=d, par=par):
                for u in range(NU):
                    inst = e.matmul(
                        pdt[:, u * 256:(u + 1) * 256],
                        dtw_sb[0:64, d * DSL + u * 128: d * DSL + (u + 1) * 128],
                        dtstg[par][0:64, :], start=True, stop=True)
                return inst
            m_dtm = s.step("tensor", dtmm,
                           waits=[m_bs, m_smw, buf_free.get("pdt")]
                           + (barB if k == 0 else []))
            buf_free[("dtstg", par)] = m_dtm
            def dtexp(e, d=d, par=par):
                for u in range(NU):
                    i2 = d * NU + u
                    inst = e.activation(
                        dtt[par][:, u * 256:(u + 1) * 256],
                        pdt[:, u * 256:(u + 1) * 256], AF.Exp,
                        bias=smallw[:, 20 + i2:21 + i2])
                return inst
            m_dte = s.step("scalar", dtexp,
                           waits=[m_dtm, buf_free.get(("dtt", par))])
            buf_free["pdt"] = m_dte
            def dtln(e, par=par):
                for u in range(NU):
                    inst = e.activation(
                        dt_c[par][u][:, :], dtt[par][:, u * 256:(u + 1) * 256],
                        AF.Ln, bias=cst[:, 1:2])
                return inst
            m_dtc = s.step("scalar", dtln,
                           waits=[m_dte, buf_free.get(("chunk", par)), m_cst])
            buf_free[("dtt", par)] = m_dtc
            evs = []
            m_mm_bc = None
            for g in range(16):
                bank = pz[g % 2]
                def bmm(e, g=g, bank=bank):
                    c0 = g * 512
                    if c0 < 4096:
                        src, orow = bcB[0:1, c0:c0 + 512], onesrow
                    else:
                        src = bcC[0:1, c0 - 4096:c0 - 3584]
                        orow = onesrow32
                    return e.matmul(bank, orow, src, start=True, stop=True)
                m_mm_bc = s.step("tensor", bmm,
                                 waits=[m_bs, m_ones,
                                        bank_free.get(("bc", g % 2))]
                                 + (barB if k == 0 and g == 0 else []))
                m_e = s.step("scalar", lambda e, g=g, par=par, bank=bank: e.activation(
                    BCrep[par][:, g * 512:(g + 1) * 512], bank,
                    AF.Identity),
                    waits=[m_mm_bc]
                    + ([buf_free.get(("BC", par))] if g == 0 else []))
                bank_free[("bc", g % 2)] = m_e
                evs.append(m_e)
            buf_free["bcstg"] = m_mm_bc
            return evs[7], evs[15], m_dtc

        nxt_ev = emit_bcast(0)
        it = 0
        for k, (d, ci, t0, par) in enumerate(chunk_list):
                m_evB, m_evC, m_dtc = nxt_ev

                def loadc(e, sem, d=d, t0=t0, par=par):
                    for u in range(NU):
                        r0, r1 = u * 128, (u + 1) * 128
                        e.dma_start(out=xc_c[par][u][:, :],
                                    in_=s_xc[d, r0:r1, t0:t0 + T]).then_inc(sem, 16)
                        e.dma_start(out=sz_c[par][u][:, :],
                                    in_=s_sz[d, r0:r1, t0:t0 + T]).then_inc(sem, 16)
                m_ld = s.step("sync", loadc,
                              waits=[buf_free.get(("chunk", par))] + barB,
                              dma=2 * NU)

                m_dAs = []
                for u in range(NU):
                    i = d * NU + u
                    p2 = u
                    dA = dAb[p2]
                    def dAacts(e, i=i, u=u, par=par, dA=dA):
                        for n in range(NST):
                            inst = e.activation(
                                dA[:, n * T:(n + 1) * T], dt_c[par][u][:, :], AF.Exp,
                                scale=Atile[:, i * NST + n: i * NST + n + 1])
                        return inst
                    m_dAs.append(s.step("scalar", dAacts,
                                  waits=[m_ld, m_dtc, m_dA_free[p2], m_an] + barB))
                if k + 1 < len(chunk_list):
                    nxt_ev = emit_bcast(k + 1)

                for u in range(NU):
                    i = d * NU + u
                    p2 = u
                    dA, ubuf, hbuf = dAb[p2], ubb[p2], hbb
                    m_dA = m_dAs[u]
                    m_v = s.step("vector", lambda e, u=u, par=par: e.tensor_tensor(
                        vb[:, :], dt_c[par][u][:, :], xc_c[par][u][:, :], ALU.mult),
                        waits=[m_ld, m_dtc, buf_free.get("vb")] + barB)
                    m_u = s.step("vector", lambda e, par=par, ubuf=ubuf: e.tensor_tensor(
                        ubuf[:].rearrange("p (n t) -> p n t", n=NST), rep16(vb, T),
                        BCrep[par][:, 0:4096].rearrange("p (n t) -> p n t", n=NST),
                        ALU.mult),
                        waits=[m_v, m_evB, m_u_free[p2]])
                    buf_free["vb"] = m_u
                    if pend_st:
                        pd, pu, pt0, pmpar, pmx = pend_st.pop(0)
                        m_st = s.step("gpsimd", lambda e, sem, d=pd, u=pu, t0=pt0,
                                      mpar=pmpar: e.dma_start(
                            out=s_mix[d, u * 128:(u + 1) * 128, t0:t0 + T],
                            in_=mixb[mpar][:, :]).then_inc(sem, 16),
                            waits=[pmx], dma=1)
                        buf_free[("mix", pmpar)] = m_st
                        m_mix_last = m_st
                    def scans(e, d=d, i=i, dA=dA, ubuf=ubuf, hbuf=hbuf):
                        for n in range(NST):
                            hv = hbuf[:, n * T:(n + 1) * T]
                            av = dA[:, n * T:(n + 1) * T]
                            uv = ubuf[:, n * T:(n + 1) * T]
                            if d == 1:
                                hv = hv[:, ::-1]; av = av[:, ::-1]; uv = uv[:, ::-1]
                            inst = e.tensor_tensor_scan(
                                hv, av, uv, carry[:, i * NST + n: i * NST + n + 1],
                                ALU.mult, ALU.add)
                        return inst
                    m_sc = s.step("vector", scans,
                                  waits=[m_dA, m_u, m_h_free, m_cz] + barB)
                    m_u_free[p2] = m_sc
                    ccol = T - 1 if d == 0 else 0
                    m_cy = s.step("vector", lambda e, i=i, ccol=ccol, hbuf=hbuf: e.tensor_copy(
                        carry[:, i * NST:(i + 1) * NST],
                        hbuf[:].rearrange("p (n t) -> p n t", n=NST)[:, :, ccol]),
                        waits=[m_sc])
                    m_hc = s.step("vector", lambda e, par=par, dA=dA, hbuf=hbuf: e.tensor_tensor(
                        dA[:, :], hbuf[:, :], BCrep[par][:, 4096:8192], ALU.mult),
                        waits=[m_sc, m_evC, m_cy])
                    m_h_free = m_hc
                    buf_free[("BC", par)] = m_hc
                    ycol = it % 2
                    def ymm(e, dA=dA, ycol=ycol):
                        for n in range(NST):
                            inst = e.matmul(ypzb[ycol][:, 0:256], ident[:, :],
                                            dA[:, n * T:(n + 1) * T],
                                            start=(n == 0), stop=(n == NST - 1))
                        return inst
                    m_ymm = s.step("tensor", ymm,
                                   waits=[m_hc, m_id, buf_free.get(("ypz", it % 2))])
                    m_h_free = m_hc
                    m_dA_free[p2] = m_ymm
                    if pend_t1:
                        emit_t1(pend_t1.pop(0))
                    pend_t1.append((i, u, par, ycol, it % 2, d, t0, m_ymm))
                    it += 1
        while pend_t1:
            emit_t1(pend_t1.pop(0))
        while pend_st:
            pd, pu, pt0, pmpar, pmx = pend_st.pop(0)
            m_st = s.step("gpsimd", lambda e, sem, d=pd, u=pu, t0=pt0,
                          mpar=pmpar: e.dma_start(
                out=s_mix[d, u * 128:(u + 1) * 128, t0:t0 + T],
                in_=mixb[mpar][:, :]).then_inc(sem, 16),
                waits=[pmx], dma=1)
            buf_free[("mix", pmpar)] = m_st
            m_mix_last = m_st

        barC0 = _barrier(s)
        Msb = AR4[:, 0:4096]
        owT = AR1[:, 10240:12288]
        owstg = AR1[:, 6144:8192]
        pjstg = AR1[:, 8192:10240]
        mixc = AR5[:, 0:2048]
        attc = AR1[:, 0:4096]
        ptst = [AR1[:, 4096:5120], AR1[:, 5120:6144]]

        m_M_last = None
        for d in range(2):
            def ldow(e, sem, d=d):
                for u in range(NU):
                    e.dma_start(out=owstg[:, u * 1024:(u + 1) * 1024],
                                in_=ow[d, u * 128:(u + 1) * 128, :]).then_inc(sem, 16)
            m_ow = s.step("sync", ldow, waits=[buf_free.get("owstg")] + barC0, dma=NU)
            m_trs = None
            for u in range(NU):
                for mt in range(NKT):
                    bank = pz[(u * NKT + mt) % 2]
                    m_tr = s.step("tensor", lambda e, u=u, mt=mt, bank=bank: e.transpose(
                        bank[:, 0:128],
                        owstg[:, u * 1024 + mt * 128: u * 1024 + (mt + 1) * 128],
                        ident[:, :]),
                        waits=[m_ow, m_id, bank_free.get(id(bank))] + barC0)
                    m_ev = s.step("scalar", lambda e, u=u, mt=mt, bank=bank: e.copy(
                        owT[:, mt * 256 + u * 128: mt * 256 + (u + 1) * 128],
                        bank[:, 0:128]),
                        waits=[m_tr, buf_free.get("owT")] + barC0)
                    bank_free[id(bank)] = m_ev
                    m_trs = m_ev
            buf_free["owstg"] = m_trs
            for oc in range(4):
                def ldpj(e, sem, d=d, oc=oc):
                    for mt in range(NKT):
                        e.dma_start(
                            out=pjstg[:, mt * 256:(mt + 1) * 256],
                            in_=proj_w[d * 1024 + mt * 128: d * 1024 + (mt + 1) * 128,
                                       oc * 256:(oc + 1) * 256]).then_inc(sem, 16)
                m_pj = s.step("sync", ldpj, waits=[buf_free.get("pjstg")] + barC0, dma=NKT)
                m_mm_l = None
                for u in range(NU):
                    bank = pz[2 + u % 2]
                    def mmM(e, u=u, bank=bank):
                        for mt in range(NKT):
                            inst = e.matmul(
                                bank[:, 0:256],
                                owT[:, mt * 256 + u * 128: mt * 256 + (u + 1) * 128],
                                pjstg[:, mt * 256:(mt + 1) * 256],
                                start=(mt == 0), stop=(mt == NKT - 1))
                        return inst
                    m_m = s.step("tensor", mmM,
                                 waits=[m_pj, m_trs, bank_free.get(id(bank))])
                    m_mm_l = m_m
                    m_e = s.step("scalar", lambda e, d=d, u=u, oc=oc, bank=bank: e.copy(
                        Msb[:, (d * NU + u) * 1024 + oc * 256:
                               (d * NU + u) * 1024 + (oc + 1) * 256],
                        bank[:, 0:256]), waits=[m_m, buf_free.get("Msb")] + barC0)
                    bank_free[id(bank)] = m_e
                    m_M_last = m_e
                buf_free["pjstg"] = m_mm_l

        qstg = [AR2[:, 0:4096], AR3[:, 0:4096]]
        screp = [AR2[:, 4096:8192], AR3[:, 4096:8192]]
        wdq = AR1[:, 12288:16384]
        dq_units = []
        for (qsrc, ssrc, wdst, nkt, ncols) in (
                (fc1_q, fc1_s, s_w1, NKT, FF), (fc2_q, fc2_s, s_w2, NOF, DM)):
            for kt in range(nkt):
                dq_units.append((qsrc, ssrc, wdst, kt, ncols))
        dq_dma = {}

        def emit_dq_dma(j):
            qsrc, ssrc, wdst, kt, ncols = dq_units[j]
            par = j % 2
            m_q = s.step("gpsimd", lambda e, sem, qsrc=qsrc, kt=kt, par=par, ncols=ncols: e.dma_start(
                out=qstg[par][:, 0:ncols],
                in_=qsrc[kt * 128:(kt + 1) * 128, :]).then_inc(sem, 16),
                waits=[buf_free.get(("qstg", par))] + barC0, dma=1)
            m_sp = s.step("scalar", lambda e, sem, ssrc=ssrc, kt=kt, par=par, ncols=ncols: e.dma_start(
                out=screp[par][0:1, 0:ncols],
                in_=ssrc[kt:kt + 1, :]).then_inc(sem, 16),
                waits=[buf_free.get(("screp", par))] + barC0, dma=1)
            for k in [1, 2, 4, 8, 16, 32, 64]:
                m_sp = s.step("scalar", lambda e, sem, k=k, par=par, ncols=ncols: e.dma_start(
                    out=screp[par][k:2 * k, 0:ncols],
                    in_=screp[par][0:k, 0:ncols]).then_inc(sem, 16), dma=1)
            dq_dma[j] = (m_q, m_sp)

        def emit_dq_comp(j):
            qsrc, ssrc, wdst, kt, ncols = dq_units[j]
            par = j % 2
            m_q, m_sp = dq_dma[j]
            m_dq = s.step("vector", lambda e, par=par, ncols=ncols: e.scalar_tensor_tensor(
                wdq[:, 0:ncols], qstg[par][:, 0:ncols], 128.0,
                screp[par][:, 0:ncols], ALU.subtract, ALU.mult),
                waits=[m_q, m_sp, buf_free.get("wdq")] + barC0)
            buf_free[("qstg", par)] = m_dq
            buf_free[("screp", par)] = m_dq
            m_sw = s.step("scalar", lambda e, sem, wdst=wdst, kt=kt, ncols=ncols: e.dma_start(
                out=wdst[kt * 128:(kt + 1) * 128, :],
                in_=wdq[:, 0:ncols]).then_inc(sem, 16), waits=[m_dq], dma=1)
            buf_free["wdq"] = m_sw
            return m_sw

        emit_dq_dma(0)
        emit_dq_dma(1)
        dq_j = 0
        m_w_done = None

        m_pt_last = None
        for g in range(NGC):
            t0 = g * CW
            def ldmix(e, sem, t0=t0):
                for d in range(2):
                    for u in range(NU):
                        e.dma_start(
                            out=mixc[:, (d * NU + u) * 512:(d * NU + u + 1) * 512],
                            in_=s_mix[d, u * 128:(u + 1) * 128, t0:t0 + CW]).then_inc(sem, 16)
            m_mxl = s.step("sync", ldmix,
                           waits=[buf_free.get("mixc"), m_mix_last, m_M_last] + barC0,
                           dma=2 * NU)
            m_oev = []
            for o in range(NKT):
                bank = pz[o % 2]
                def mmP(e, o=o, bank=bank):
                    for i in range(2 * NU):
                        inst = e.matmul(
                            bank[:, :],
                            Msb[:, i * 1024 + o * 128: i * 1024 + (o + 1) * 128],
                            mixc[:, i * 512:(i + 1) * 512],
                            start=(i == 0), stop=(i == 2 * NU - 1))
                    return inst
                m_m = s.step("tensor", mmP,
                             waits=[m_mxl, m_M_last, bank_free.get(id(bank))])
                m_e = s.step("vector", lambda e, o=o, bank=bank: e.tensor_copy(
                    attc[:, o * 512:(o + 1) * 512], bank[:, :]),
                    waits=[m_m, buf_free.get("attc")])
                bank_free[id(bank)] = m_e
                m_oev.append(m_e)
            buf_free["mixc"] = m_m
            m_tr_l = None
            for ts in range(4):
                tpar = ts % 2
                ptb = ptst[tpar]
                m_ads = []
                for o in range(NKT):
                    bank = pz[2 + (ts * NKT + o) % 2]
                    m_tr = s.step("tensor", lambda e, o=o, ts=ts, bank=bank: e.transpose(
                        bank[:, 0:128],
                        attc[:, o * 512 + ts * 128: o * 512 + (ts + 1) * 128],
                        ident[:, :]),
                        waits=m_oev + [m_id, bank_free.get(id(bank))])
                    m_tr_l = m_tr
                    m_cp = s.step("vector", lambda e, o=o, ptb=ptb, bank=bank: e.tensor_copy(
                        ptb[:, o * 128:(o + 1) * 128], bank[:, 0:128]),
                        waits=[m_tr, buf_free.get(("ptb", tpar))] + barC0)
                    bank_free[id(bank)] = m_cp
                    m_ads.append(m_cp)
                m_stp = s.step("gpsimd", lambda e, sem, t0=t0, ts=ts, ptb=ptb: e.dma_start(
                    out=s_part[t0 + ts * 128: t0 + (ts + 1) * 128, :],
                    in_=ptb[:, :]).then_inc(sem, 16), waits=m_ads, dma=1)
                buf_free[("ptb", tpar)] = m_stp
                m_pt_last = m_stp
            buf_free["attc"] = m_tr_l
            for _ in range(2):
                if dq_j < len(dq_units):
                    m_w_done = emit_dq_comp(dq_j)
                    if dq_j + 2 < len(dq_units):
                        emit_dq_dma(dq_j + 2)
                    dq_j += 1

        barC = _barrier(s)
        m_rs = s.step("gpsimd", lambda e: e.collective_compute(
            "ReduceScatter", ALU.add, replica_groups=[list(range(NCORES))],
            ins=[s_part[:]], outs=[s_attn[:]]), waits=[m_pt_last])

        fxt = [AR1[:, 0:1024], AR1[:, 1024:2048]]
        fxsq = AR1[:, 2048:3072]
        fxn2 = AR1[:, 3072:4096]
        fhtm = [AR6[:, 0:1024], AR6[:, 1024:2048]]
        fhstg = AR1[:, 4096:5120]
        grep = AR1[:, 5120:6144]
        brep = AR1[:, 6144:7168]
        pbrep = AR7[:, 0:1024]

        def ldgb(e, sem):
            e.dma_start(out=grep[0:1, :], in_=ln_g.unsqueeze(0)).then_inc(sem, 16)
            e.dma_start(out=brep[0:1, :], in_=ln_b.unsqueeze(0)).then_inc(sem, 16)
            e.dma_start(out=pbrep[0:1, :], in_=proj_b.unsqueeze(0)).then_inc(sem, 16)
        m_db2 = s.step("sync", ldgb, waits=barC, dma=3)
        for k in [1, 2, 4, 8, 16, 32, 64]:
            def dbl(e, sem, k=k):
                e.dma_start(out=grep[k:2 * k, :], in_=grep[0:k, :]).then_inc(sem, 16)
                e.dma_start(out=brep[k:2 * k, :], in_=brep[0:k, :]).then_inc(sem, 16)
                e.dma_start(out=pbrep[k:2 * k, :], in_=pbrep[0:k, :]).then_inc(sem, 16)
            m_db2 = s.step("sync", dbl, waits=[m_db2], dma=3)

        tr2 = 0
        m_sth = None
        m_hT_chunk = {}
        for t in range(TSL // 128):
            par = t % 2
            m_dma = s.step("sync", lambda e, sem, t=t, par=par: e.dma_start(
                out=fxt[par][:, :], in_=xs[t * 128:(t + 1) * 128, :]).then_inc(sem, 16),
                waits=[buf_free.get(("fxt", par))] + barC, dma=1)
            m_s1 = s.step("vector", lambda e, par=par: e.reduce_sum(
                st[:, 0:1], fxt[par][:, :], AX.X), waits=[m_dma] + barC)
            m_s2 = s.step("scalar", lambda e, par=par: e.activation(
                fxsq[:, :], fxt[par][:, :], AF.Square, accum_out=st[:, 1:2]),
                waits=[m_dma] + barC)
            m_v1 = s.step("vector", lambda e: e.tensor_scalar(
                st[:, 2:3], st[:, 0:1], 1.0 / DM, None, ALU.mult), waits=[m_s1, m_s2])
            m_v1b = s.step("vector", lambda e: e.tensor_scalar(
                st[:, 3:4], st[:, 1:2], 1.0 / DM, None, ALU.mult), waits=[m_v1])
            m_v2 = s.step("vector", lambda e: e.tensor_tensor(
                st[:, 4:5], st[:, 2:3], st[:, 2:3], ALU.mult), waits=[m_v1b])
            m_vr = s.step("vector", lambda e: e.tensor_tensor(
                st[:, 5:6], st[:, 3:4], st[:, 4:5], ALU.subtract), waits=[m_v2])
            m_sq = s.step("scalar", lambda e: e.activation(
                st[:, 6:7], st[:, 5:6], AF.Sqrt, bias=cst[:, 0:1]), waits=[m_vr])
            m_rc = s.step("vector", lambda e: e.reciprocal(
                st[:, 7:8], st[:, 6:7]), waits=[m_sq])
            m_xn = s.step("vector", lambda e, par=par: e.tensor_scalar(
                fxsq[:, :], fxt[par][:, :], st[:, 2:3], st[:, 7:8],
                ALU.subtract, ALU.mult), waits=[m_rc, m_s2])
            m_rt = s.step("vector", lambda e, par=par: e.tensor_tensor(
                fhtm[par][:, :], fxt[par][:, :], fxsq[:, :], ALU.add),
                waits=[m_xn, buf_free.get(("fhtm", par))])
            buf_free[("fxt", par)] = m_rt
            m_str = s.step("scalar", lambda e, sem, t=t, par=par: e.dma_start(
                out=s_res[t * 128:(t + 1) * 128, :],
                in_=fhtm[par][:, :]).then_inc(sem, 16), waits=[m_rt], dma=1)
            m_t1s = s.step("vector", lambda e: e.reduce_sum(
                st[:, 8:9], fxsq[:, :], AX.X), waits=[m_xn])
            m_t2s = s.step("scalar", lambda e: e.activation(
                fxn2[:, :], fxsq[:, :], AF.Square, accum_out=st[:, 9:10]), waits=[m_xn])
            m_u1 = s.step("vector", lambda e: e.tensor_scalar(
                st[:, 10:11], st[:, 8:9], 1.0 / DM, None, ALU.mult), waits=[m_t1s, m_t2s])
            m_u1b = s.step("vector", lambda e: e.tensor_scalar(
                st[:, 11:12], st[:, 9:10], 1.0 / DM, None, ALU.mult), waits=[m_u1])
            m_u2 = s.step("vector", lambda e: e.tensor_tensor(
                st[:, 12:13], st[:, 10:11], st[:, 10:11], ALU.mult), waits=[m_u1b])
            m_ur = s.step("vector", lambda e: e.tensor_tensor(
                st[:, 13:14], st[:, 11:12], st[:, 12:13], ALU.subtract), waits=[m_u2])
            m_sq2 = s.step("scalar", lambda e: e.activation(
                st[:, 14:15], st[:, 13:14], AF.Sqrt, bias=cst[:, 0:1]), waits=[m_ur])
            m_rc2 = s.step("vector", lambda e: e.reciprocal(
                st[:, 15:16], st[:, 14:15]), waits=[m_sq2])
            m_xn2 = s.step("vector", lambda e: e.tensor_scalar(
                fxn2[:, :], fxsq[:, :], st[:, 10:11], st[:, 15:16],
                ALU.subtract, ALU.mult), waits=[m_rc2])
            m_h1 = s.step("vector", lambda e: e.tensor_tensor(
                fxsq[:, :], fxn2[:, :], grep[:, :], ALU.mult), waits=[m_xn2, m_db2])
            m_h2 = s.step("vector", lambda e, par=par: e.tensor_tensor(
                fhtm[par][:, :], fxsq[:, :], brep[:, :], ALU.add),
                waits=[m_h1, m_str])
            m_evs = []
            for dsub in range(NKT):
                bank = pz[tr2 % 2]
                m_tr = s.step("tensor", lambda e, par=par, dsub=dsub, bank=bank: e.transpose(
                    bank[:, 0:128], fhtm[par][:, dsub * 128:(dsub + 1) * 128],
                    ident[:, :]), waits=[m_h2, bank_free.get(id(bank))] + barC)
                m_ev = s.step("scalar", lambda e, dsub=dsub, bank=bank: e.copy(
                    fhstg[:, dsub * 128:(dsub + 1) * 128], bank[:, 0:128]),
                    waits=[m_tr, buf_free.get("fhstg")])
                bank_free[id(bank)] = m_ev
                m_evs.append(m_ev)
                tr2 += 1
            buf_free[("fhtm", par)] = m_evs[-1]

            def sthT(e, sem, t=t):
                for dsub in range(NKT):
                    e.dma_start(out=s_hT[dsub * 128:(dsub + 1) * 128, t * 128:(t + 1) * 128],
                                in_=fhstg[:, dsub * 128:(dsub + 1) * 128]).then_inc(sem, 16)
            m_sth = s.step("scalar", sthT, waits=m_evs, dma=NKT)
            buf_free["fhstg"] = m_sth
            m_hT_chunk[t // 4] = m_sth
        m_hT_done = m_sth

        syb = AR1[:, 0:NOF * 512]
        hTc = [AR1[:, 16384:20480], AR1[:, 20480:24576]]
        w1str = [AR2[:, 0:2048], AR2[:, 2048:4096]]
        w2str = [AR2[:, 4096:8192], AR3[:, 4096:8192]]
        ffnc = AR3[:, 0:4096]
        fatt = AR4[:, 0:1024]
        fres = AR4[:, 1024:2048]
        fout = [AR4[:, 2048:3072], AR4[:, 3072:4096]]

        for g in range(TSL // 512):
            t0 = g * 512
            par = g % 2
            def ldh(e, sem, t0=t0, par=par):
                for kt in range(NKT):
                    e.dma_start(out=hTc[par][:, kt * 512:(kt + 1) * 512],
                                in_=s_hT[kt * 128:(kt + 1) * 128, t0:t0 + 512]).then_inc(sem, 16)
            m_h = s.step("sync", ldh,
                         waits=[buf_free.get(("hTc", par)), m_hT_chunk[g]], dma=NKT)
            m_sy = []
            m_m1_l = None
            for o in range(NOF):
                wpar = o % 2
                def ldw1(e, sem, o=o, wpar=wpar):
                    for kt in range(NKT):
                        e.dma_start(out=w1str[wpar][:, kt * 128:(kt + 1) * 128],
                                    in_=s_w1[kt * 128:(kt + 1) * 128,
                                             o * 128:(o + 1) * 128]).then_inc(sem, 16)
                m_w = s.step("sync", ldw1,
                             waits=[buf_free.get(("w1str", wpar)), m_w_done], dma=NKT)
                bank = pz[o % 2]
                def mmf1(e, o=o, wpar=wpar, par=par, bank=bank):
                    for kt in range(NKT):
                        inst = e.matmul(bank[:, :], w1str[wpar][:, kt * 128:(kt + 1) * 128],
                                        hTc[par][:, kt * 512:(kt + 1) * 512],
                                        start=(kt == 0), stop=(kt == NKT - 1))
                    return inst
                m_m = s.step("tensor", mmf1,
                             waits=[m_w, m_h, bank_free.get(id(bank))] + barC)
                m_m1_l = m_m
                buf_free[("w1str", wpar)] = m_m
                m_e = s.step("scalar", lambda e, o=o, bank=bank: e.activation(
                    syb[:, o * 512:(o + 1) * 512], bank[:, :], AF.Silu,
                    bias=f1bc[:, o:o + 1]), waits=[m_m, buf_free.get("syb")])
                bank_free[id(bank)] = m_e
                m_sy.append(m_e)
            buf_free[("hTc", par)] = m_m1_l
            m_f2 = []
            m_m2_l = None
            for o in range(NKT):
                wpar = o % 2
                def ldw2(e, sem, o=o, wpar=wpar):
                    for kt in range(NOF):
                        e.dma_start(out=w2str[wpar][:, kt * 128:(kt + 1) * 128],
                                    in_=s_w2[kt * 128:(kt + 1) * 128,
                                             o * 128:(o + 1) * 128]).then_inc(sem, 16)
                m_w = s.step("sync", ldw2,
                             waits=[buf_free.get(("w2str", wpar)), m_w_done], dma=NOF)
                bank = pz[2 + o % 2]
                def mmf2(e, o=o, wpar=wpar, bank=bank):
                    for kt in range(NOF):
                        inst = e.matmul(bank[:, :], w2str[wpar][:, kt * 128:(kt + 1) * 128],
                                        syb[:, kt * 512:(kt + 1) * 512],
                                        start=(kt == 0), stop=(kt == NOF - 1))
                    return inst
                m_m = s.step("tensor", mmf2,
                             waits=[m_w, bank_free.get(id(bank))] + m_sy)
                m_m2_l = m_m
                buf_free[("w2str", wpar)] = m_m
                m_e = s.step("vector", lambda e, o=o, bank=bank: e.tensor_scalar(
                    ffnc[:, o * 512:(o + 1) * 512], bank[:, :],
                    f2bc[:, o:o + 1], 0.5, ALU.add, ALU.mult),
                    waits=[m_m, buf_free.get("ffnc")] + barC)
                bank_free[id(bank)] = m_e
                m_f2.append(m_e)
            buf_free["syb"] = m_m2_l
            m_tr_l2 = None
            for ts in range(4):
                tpar = ts % 2
                m_lat = s.step("sync", lambda e, sem, t0=t0, ts=ts: e.dma_start(
                    out=fatt[:, :],
                    in_=s_attn[t0 + ts * 128: t0 + (ts + 1) * 128, :]).then_inc(sem, 16),
                    waits=[buf_free.get("fatt"), m_rs], dma=1)
                m_lre = s.step("sync", lambda e, sem, t0=t0, ts=ts: e.dma_start(
                    out=fres[:, :],
                    in_=s_res[t0 + ts * 128: t0 + (ts + 1) * 128, :]).then_inc(sem, 16),
                    waits=[buf_free.get("fres")], dma=1)
                m_ads = []
                for o in range(NKT):
                    bank = pz[(ts * NKT + o) % 2]
                    m_tr = s.step("tensor", lambda e, o=o, ts=ts, bank=bank: e.transpose(
                        bank[:, 0:128],
                        ffnc[:, o * 512 + ts * 128: o * 512 + (ts + 1) * 128],
                        ident[:, :]),
                        waits=m_f2 + [bank_free.get(id(bank))])
                    m_tr_l2 = m_tr
                    m_a1 = s.step("vector", lambda e, o=o, tpar=tpar, bank=bank: e.tensor_tensor(
                        fout[tpar][:, o * 128:(o + 1) * 128], bank[:, 0:128],
                        fatt[:, o * 128:(o + 1) * 128], ALU.add),
                        waits=[m_tr, m_lat, buf_free.get(("fout", tpar))])
                    bank_free[id(bank)] = m_a1
                    m_ads.append(m_a1)
                m_a2 = s.step("vector", lambda e, tpar=tpar: e.tensor_tensor(
                    fout[tpar][:, :], fout[tpar][:, :], fres[:, :], ALU.add),
                    waits=m_ads + [m_lre])
                m_a3 = s.step("vector", lambda e, tpar=tpar: e.tensor_tensor(
                    fout[tpar][:, :], fout[tpar][:, :], pbrep[:, :], ALU.add),
                    waits=[m_a2, m_db2])
                buf_free["fatt"] = m_a3
                buf_free["fres"] = m_a3
                m_so = s.step("sync", lambda e, sem, t0=t0, ts=ts, tpar=tpar: e.dma_start(
                    out=o_y[t0 + ts * 128: t0 + (ts + 1) * 128, :],
                    in_=fout[tpar][:, :]).then_inc(sem, 16), waits=[m_a3], dma=1)
                buf_free[("fout", tpar)] = m_so
            buf_free["ffnc"] = m_tr_l2

        s.finish(block)
    return nc


def kernel(**inputs):
    xfull = np.ascontiguousarray(np.asarray(inputs["L0_attn_x"]).reshape(L, DM), np.float32)
    in_w = np.asarray(inputs["L0_attn_in_w"], np.float32)
    shared = {
        "x": xfull,
        "proj_w": np.ascontiguousarray(inputs["L0_attn_proj_w"], np.float32),
        "proj_b": np.ascontiguousarray(inputs["L0_attn_proj_b"], np.float32),
        "ln_g": np.ascontiguousarray(inputs["L0_ffn_ln_gamma"], np.float32),
        "ln_b": np.ascontiguousarray(inputs["L0_ffn_ln_beta"], np.float32),
        "fc1_q": np.ascontiguousarray(inputs["L0_ffn_fc1_q"], np.int32),
        "fc1_s": np.ascontiguousarray(inputs["L0_ffn_fc1_scales"], np.float32),
        "fc1_b": np.ascontiguousarray(inputs["L0_ffn_fc1_bias"], np.float32),
        "fc2_q": np.ascontiguousarray(inputs["L0_ffn_fc2_q"], np.int32),
        "fc2_s": np.ascontiguousarray(inputs["L0_ffn_fc2_scales"], np.float32),
        "fc2_b": np.ascontiguousarray(inputs["L0_ffn_fc2_bias"], np.float32),
    }
    nc = build_fused()
    in_maps = []
    for r in range(NCORES):
        d0, d1 = r * DSL, (r + 1) * DSL
        m = dict(shared)
        m["inw"] = np.ascontiguousarray(
            np.concatenate([in_w[:, :, d0:d1], in_w[:, :, DI + d0:DI + d1]], axis=2))
        m["conv_w"] = np.ascontiguousarray(inputs["L0_attn_conv_w"][:, d0:d1], np.float32)
        m["conv_b"] = np.ascontiguousarray(inputs["L0_attn_conv_b"][:, d0:d1], np.float32)
        m["xp_w"] = np.ascontiguousarray(inputs["L0_attn_xp_w"][:, d0:d1], np.float32)
        m["dt_w"] = np.ascontiguousarray(inputs["L0_attn_dt_w"][:, :, d0:d1], np.float32)
        m["dt_b"] = np.ascontiguousarray(inputs["L0_attn_dt_b"][:, d0:d1], np.float32)
        m["i_alog"] = np.ascontiguousarray(inputs["L0_attn_A_log"][:, d0:d1], np.float32)
        m["i_d"] = np.ascontiguousarray(inputs["L0_attn_D"][:, d0:d1], np.float32)
        m["ow"] = np.ascontiguousarray(inputs["L0_attn_out_w"][:, d0:d1], np.float32)
        m["xs"] = np.ascontiguousarray(xfull[r * TSL:(r + 1) * TSL])
        in_maps.append(m)
    res = run_bass_kernel_spmd(nc, in_maps, core_ids=list(range(NCORES)))
    y = np.concatenate([r["o_y"] for r in res.results], axis=0)
    return y.reshape(1, L, DM)


# version 30
# speedup vs baseline: 1.0482x; 1.0037x over previous
import contextlib
import numpy as np
import concourse.bass as bass
import concourse.mybir as mybir
from concourse.bass_utils import run_bass_kernel_spmd

F32 = mybir.dt.float32
I32 = mybir.dt.int32
AF = mybir.ActivationFunctionType
ALU = mybir.AluOpType
AX = mybir.AxisListType

NCORES = 8
L = 16384
DM = 1024
DI = 2048
NST = 16
DTR = 64
DCONV = 4
FF = 4096
GRP = 128
TSL = L // NCORES
DSL = DI // NCORES
HALO = DCONV - 1
EPS = 1e-5


class Sched:

    ENGINES = ("sync", "tensor", "vector", "scalar", "gpsimd")
    SEMS = ("sync", "tensor", "vector", "scalar", "gpsimd", "gpdma", "scdma")

    def __init__(self, nc):
        self.nc = nc
        self.stream = {e: [] for e in self.ENGINES}
        self.val = {e: 0 for e in self.SEMS}
        self.sems = {}
        self._hw = {}

    def make_sems(self, ctx):
        for e in self.SEMS:
            self.sems[e] = ctx.enter_context(self.nc.semaphore("sem_" + e))

    def step(self, eng, fn, waits=(), dma=0):
        wl = [w for w in waits if w is not None]
        sem_map = self.sems
        key = eng
        if dma and eng == "gpsimd":
            key = "gpdma"
        elif dma and eng == "scalar":
            key = "scdma"
        pre = self.val[key]
        if dma:
            wl.append((key, pre))
            self.val[key] += 16 * dma
        else:
            self.val[key] += 1

        def emit(e, wl=wl, fn=fn, eng=eng, key=key, dma=dma):
            hw = self._hw.setdefault(eng, {})
            best = {}
            for (we, wv) in wl:
                if wv > best.get(we, 0):
                    best[we] = wv
            for we, wv in best.items():
                if wv <= hw.get(we, 0):
                    continue
                hw[we] = wv
                e.wait_ge(sem_map[we], wv)
            if dma:
                fn(e, sem_map[key])
            else:
                inst = fn(e)
                inst.then_inc(sem_map[key], 1)

        self.stream[eng].append(emit)
        return (key, self.val[key])

    def finish(self, block, extra_final=()):
        final_gp = self.val["gpdma"]
        final_sync = self.val["sync"]
        final_sc = self.val["scdma"]
        for name in self.ENGINES:
            ops = self.stream[name]

            def body(e, ops=ops, name=name):
                for op in ops:
                    op(e)
                if name == "gpsimd":
                    if final_gp:
                        e.wait_ge(self.sems["gpdma"], final_gp)
                    if final_sync:
                        e.wait_ge(self.sems["sync"], final_sync)
                    if final_sc:
                        e.wait_ge(self.sems["scdma"], final_sc)
                    for (we, wv) in extra_final:
                        e.wait_ge(self.sems[we], wv)

            getattr(block, name)(body)



def build_L1(debug=False):
    TS = TSL + 2 * HALO
    NTT = (TS + 127) // 128
    NG = TS // 512
    MINI = TS - NG * 512
    NKT = DM // 128
    nc = bass.Bass("TRN2", target_bir_lowering=False, detect_race_conditions=False)

    x = nc.dram_tensor("x", [TS, DM], F32, kind="ExternalInput").ap()
    in_w = nc.dram_tensor("in_w", [2, DM, 2 * DI], F32, kind="ExternalInput").ap()
    conv_w = nc.dram_tensor("conv_w", [2, DI, DCONV], F32, kind="ExternalInput").ap()
    conv_b = nc.dram_tensor("conv_b", [2, DI], F32, kind="ExternalInput").ap()
    xp_w = nc.dram_tensor("xp_w", [2, DI, DTR + 2 * NST], F32, kind="ExternalInput").ap()
    dt_w = nc.dram_tensor("dt_w", [2, DTR, DI], F32, kind="ExternalInput").ap()
    dt_b = nc.dram_tensor("dt_b", [2, DI], F32, kind="ExternalInput").ap()

    o_dt = nc.dram_tensor("o_dt", [2, DI, TSL], F32, kind="ExternalOutput").ap()
    o_xc = nc.dram_tensor("o_xc", [2, DI, TSL], F32, kind="ExternalOutput").ap()
    o_sz = nc.dram_tensor("o_sz", [2, DI, TSL], F32, kind="ExternalOutput").ap()
    o_bc = nc.dram_tensor("o_bc", [2, 2 * NST, TSL], F32, kind="ExternalOutput").ap()
    if debug:
        d_st = nc.dram_tensor("d_st", [128, 8 + 2 * DM], F32, kind="ExternalOutput").ap()
        d_xnT = nc.dram_tensor("d_xnT", [128, 8 * (TSL + 2 * HALO)], F32, kind="ExternalOutput").ap()
        d_strip = nc.dram_tensor("d_strip", [128, TSL + 2 * HALO], F32, kind="ExternalOutput").ap()

    with contextlib.ExitStack() as ctx:
        sbt = lambda name, shape, dt=F32: ctx.enter_context(nc.sbuf_tensor(name, shape, dt))
        pst = lambda name, shape: ctx.enter_context(nc.psum_tensor(name, shape, F32))

        xnT = sbt("xnT", [128, NKT * TS])
        xtm = [sbt("xtm0", [128, DM]), sbt("xtm1", [128, DM])]
        xnb = [sbt("xnb0", [128, DM]), sbt("xnb1", [128, DM])]
        xsq = sbt("xsq", [128, DM])
        st = sbt("stats", [128, 8])
        ident = sbt("ident", [128, 128])
        io_i = sbt("io_i", [128, 128], F32)
        io_p = sbt("io_p", [128, 1], F32)
        wbuf = [sbt("wbuf0", [128, NKT * 128]), sbt("wbuf1", [128, NKT * 128])]
        strip = sbt("strip", [128, TS])
        cacc = [sbt("cacc0", [128, TSL]), sbt("cacc1", [128, TSL])]
        xc_strip = sbt("xc_strip", [128, TSL])
        cw = sbt("cw", [128, 2 * 16 * DCONV])
        cb = sbt("cb", [128, 2 * 16])
        xpw = sbt("xpw", [128, 2 * 16 * 96])
        dtw = sbt("dtw", [64, 2 * DI])
        dtb = sbt("dtb", [128, 2 * 16])
        dbc_sb = sbt("dbc_sb", [96, 2 * TSL])
        dt_sb = sbt("dt_sb", [128, TSL])
        spe = sbt("spe", [128, TSL])
        cst = sbt("cst", [128, 2])

        pz = [pst("pz0", [128, 512]), pst("pz1", [128, 512]), pst("pz2", [128, 512])]
        pzm = pst("pzm", [128, MINI])
        pdb = [pst("pdb0", [96, 512]), pst("pdb1", [96, 512]),
               pst("pdb2", [96, 512]), pst("pdb3", [96, 512])]

        s = Sched(nc)
        s.make_sems(ctx)
        block = ctx.enter_context(nc.Block())

        ttl = [min(128, TS - t * 128) for t in range(NTT)]
        bank_free = {}
        buf_free = {}

        def mkcst(e):
            e.memset(cst[:, 0:1], EPS)
            return e.memset(cst[:, 1:2], 1.0)
        m_cst = s.step("vector", mkcst)

        m_io1 = s.step("gpsimd", lambda e: e.iota(
            io_i[:, :], [[1, 128]], channel_multiplier=0,
            allow_small_or_imprecise_dtypes=True))
        m_io2 = s.step("gpsimd", lambda e: e.iota(
            io_p[:, 0:1], [[0, 1]], channel_multiplier=1,
            allow_small_or_imprecise_dtypes=True))
        m_id = s.step("vector", lambda e: e.tensor_scalar(
            ident[:, :], io_i[:, :], io_p[:, 0:1], None, ALU.is_equal),
            waits=[m_io1, m_io2])

        def load_small(e, sem):
            for d in range(2):
                e.dma_start(out=dtw[0:64, d * DI:(d + 1) * DI],
                            in_=dt_w[d, :, :]).then_inc(sem, 16)
                for m in range(16):
                    i = d * 16 + m
                    e.dma_start(out=cw[:, i * DCONV:(i + 1) * DCONV],
                                in_=conv_w[d, m * 128:(m + 1) * 128, :]).then_inc(sem, 16)
                    e.dma_start(out=cb[:, i:i + 1],
                                in_=conv_b[d, m * 128:(m + 1) * 128].unsqueeze(1)).then_inc(sem, 16)
                    e.dma_start(out=dtb[:, i:i + 1],
                                in_=dt_b[d, m * 128:(m + 1) * 128].unsqueeze(1)).then_inc(sem, 16)
                    e.dma_start(out=xpw[:, i * 96:(i + 1) * 96],
                                in_=xp_w[d, m * 128:(m + 1) * 128, :]).then_inc(sem, 16)
        m_smw = s.step("sync", load_small, dma=2 * (1 + 16 * 4))

        tr_i = 0
        for t in range(NTT):
            n = ttl[t]
            m_dma = s.step("sync", lambda e, sem, t=t, n=n: e.dma_start(
                out=xtm[t % 2][0:n, :],
                in_=x[t * 128:t * 128 + n, :]).then_inc(sem, 16),
                waits=[buf_free.get(("xtm", t % 2))], dma=1)
            m_s1 = s.step("vector", lambda e, t=t, n=n: e.reduce_sum(
                st[0:n, 0:1], xtm[t % 2][0:n, :], AX.X), waits=[m_dma])
            m_s2 = s.step("scalar", lambda e, t=t, n=n: e.activation(
                xsq[0:n, :], xtm[t % 2][0:n, :], AF.Square,
                accum_out=st[0:n, 1:2]), waits=[m_dma])

            def statmath1(e, n=n):
                e.tensor_scalar(st[0:n, 2:3], st[0:n, 0:1], 1.0 / DM, None, ALU.mult)
                return e.tensor_scalar(st[0:n, 3:4], st[0:n, 1:2], 1.0 / DM, None, ALU.mult)
            m_v1 = s.step("vector", statmath1, waits=[m_s1, m_s2])
            m_v2 = s.step("vector", lambda e, n=n: e.tensor_tensor(
                st[0:n, 4:5], st[0:n, 2:3], st[0:n, 2:3], ALU.mult), waits=[m_v1])
            m_vr = s.step("vector", lambda e, n=n: e.tensor_tensor(
                st[0:n, 5:6], st[0:n, 3:4], st[0:n, 4:5], ALU.subtract), waits=[m_v2])
            m_sq = s.step("scalar", lambda e, n=n: e.activation(
                st[0:n, 6:7], st[0:n, 5:6], AF.Sqrt, bias=cst[0:n, 0:1]),
                waits=[m_vr, m_cst])

            m_rc = s.step("vector", lambda e, n=n: e.reciprocal(
                st[0:n, 7:8], st[0:n, 6:7]), waits=[m_sq])
            m_xn = s.step("vector", lambda e, t=t, n=n: e.tensor_scalar(
                xnb[t % 2][0:n, :], xtm[t % 2][0:n, :],
                st[0:n, 2:3], st[0:n, 7:8], ALU.subtract, ALU.mult),
                waits=[m_rc, buf_free.get(("xnb", t % 2))])
            if debug and t == 0:
                def dumpst(e, sem):
                    e.dma_start(out=d_st[:, 0:8], in_=st[:, :]).then_inc(sem, 16)
                    e.dma_start(out=d_st[:, 8:8 + DM], in_=xnb[0][:, :]).then_inc(sem, 16)
                    e.dma_start(out=d_st[:, 8 + DM:8 + 2 * DM], in_=xtm[0][:, :]).then_inc(sem, 16)
                s.step("gpsimd", dumpst, waits=[m_xn], dma=3)
            buf_free[("xtm", t % 2)] = m_xn

            last_tr = None
            for dsub in range(NKT):
                bank = pz[tr_i % 2]
                m_tr = s.step("tensor", lambda e, t=t, n=n, dsub=dsub, bank=bank: e.transpose(
                    bank[:, 0:n], xnb[t % 2][0:n, dsub * 128:(dsub + 1) * 128],
                    ident[0:n, 0:n]),
                    waits=[m_xn, m_id, bank_free.get(id(bank))])
                m_evc = s.step("scalar", lambda e, t=t, n=n, dsub=dsub, bank=bank: e.copy(
                    xnT[:, dsub * TS + t * 128: dsub * TS + t * 128 + n],
                    bank[:, 0:n]), waits=[m_tr])
                bank_free[id(bank)] = m_evc
                last_tr = m_tr
                m_last_evc = m_evc
                tr_i += 1
            buf_free[("xnb", t % 2)] = last_tr
        m_xnT_all = m_last_evc
        if debug:
            s.step("gpsimd", lambda e, sem: e.dma_start(
                out=d_xnT[:, :], in_=xnT[:, :]).then_inc(sem, 16),
                waits=[m_xnT_all], dma=1)

        m_w = [None, None]
        mi = 0
        m_strip_free = None
        m_xcs_free = None

        for d in range(2):
            m_db_last = [None] * 4
            for mt in range(32):
                par = mi % 2

                def wload(e, sem, d=d, mt=mt, par=par):
                    for kt in range(NKT):
                        e.dma_start(
                            out=wbuf[par][:, kt * 128:(kt + 1) * 128],
                            in_=in_w[d, kt * 128:(kt + 1) * 128,
                                     mt * 128:(mt + 1) * 128]).then_inc(sem, 16)
                m_w[par] = s.step("sync", wload,
                                  waits=[buf_free.get(("wbuf", par))], dma=NKT)

                is_xi = mt < 16
                m_evs = []
                m_last_mm = None
                for g in range(NG + 1):
                    bank = pz[g % 3] if g < NG else pzm
                    mv0 = g * 512
                    mvn = 512 if g < NG else MINI

                    def mmgroup(e, par=par, bank=bank, mv0=mv0, mvn=mvn):
                        for kt in range(NKT):
                            inst = e.matmul(
                                bank[:, 0:mvn],
                                wbuf[par][:, kt * 128:(kt + 1) * 128],
                                xnT[:, kt * TS + mv0: kt * TS + mv0 + mvn],
                                start=(kt == 0), stop=(kt == NKT - 1))
                        return inst
                    m_mm = s.step("tensor", mmgroup,
                                  waits=[m_w[par], m_xnT_all, bank_free.get(id(bank))])
                    m_last_mm = m_mm
                    if is_xi:
                        def evac(e, bank=bank, mv0=mv0, mvn=mvn):
                            return e.copy(strip[:, mv0:mv0 + mvn], bank[:, 0:mvn])
                    else:
                        def evac(e, bank=bank, mv0=mv0, mvn=mvn):
                            return e.activation(strip[:, mv0:mv0 + mvn],
                                                bank[:, 0:mvn], AF.Silu)
                    m_e = s.step("scalar", evac, waits=[m_mm, m_strip_free])
                    bank_free[id(bank)] = m_e
                    m_evs.append(m_e)
                buf_free[("wbuf", par)] = m_last_mm

                if is_xi:
                    idx = d * 16 + mt

                    def convops(e, d=d, idx=idx):
                        offs = [0, 1, 2, 3] if d == 0 else [6, 5, 4, 3]
                        e.tensor_scalar(cacc[0][:, :], strip[:, offs[0]:offs[0] + TSL],
                                        cw[:, idx * DCONV:idx * DCONV + 1], None, ALU.mult)
                        e.scalar_tensor_tensor(cacc[1][:, :], strip[:, offs[1]:offs[1] + TSL],
                                               cw[:, idx * DCONV + 1:idx * DCONV + 2],
                                               cacc[0][:, :], ALU.mult, ALU.add)
                        e.scalar_tensor_tensor(cacc[0][:, :], strip[:, offs[2]:offs[2] + TSL],
                                               cw[:, idx * DCONV + 2:idx * DCONV + 3],
                                               cacc[1][:, :], ALU.mult, ALU.add)
                        return e.scalar_tensor_tensor(cacc[1][:, :], strip[:, offs[3]:offs[3] + TSL],
                                                      cw[:, idx * DCONV + 3:idx * DCONV + 4],
                                                      cacc[0][:, :], ALU.mult, ALU.add)
                    if debug and d == 0 and mt == 0:
                        m_dbgs = s.step("gpsimd", lambda e, sem: e.dma_start(
                            out=d_strip[:, :], in_=strip[:, :]).then_inc(sem, 16),
                            waits=m_evs, dma=1)
                        m_evs = m_evs + [m_dbgs]
                    m_cv = s.step("vector", convops, waits=m_evs + [m_smw])
                    m_strip_free = m_cv

                    m_xc = s.step("scalar", lambda e, idx=idx: e.activation(
                        xc_strip[:, :], cacc[1][:, :], AF.Silu, bias=cb[:, idx:idx + 1]),
                        waits=[m_cv] + (m_xcs_free or []))

                    m_stxc = s.step("gpsimd", lambda e, sem, d=d, mt=mt: e.dma_start(
                        out=o_xc[d, mt * 128:(mt + 1) * 128, :],
                        in_=xc_strip[:, :]).then_inc(sem, 16),
                        waits=[m_xc], dma=1)

                    m_dbs = []
                    for g in range(4):
                        def dbmm(e, g=g, idx=idx, mt=mt):
                            return e.matmul(
                                pdb[g][:, :],
                                xpw[:, idx * 96:(idx + 1) * 96],
                                xc_strip[:, g * 512:(g + 1) * 512],
                                start=(mt == 0), stop=(mt == 15))
                        m_db_last[g] = s.step("tensor", dbmm,
                                              waits=[m_xc, m_smw,
                                                     bank_free.get(id(pdb[g])) if mt == 0 else None])
                        m_dbs.append(m_db_last[g])
                    m_xcs_free = [m_dbs[-1], m_stxc]
                else:
                    m_stz = s.step("gpsimd", lambda e, sem, d=d, mt=mt: e.dma_start(
                        out=o_sz[d, (mt - 16) * 128:(mt - 15) * 128, :],
                        in_=strip[:, HALO:HALO + TSL]).then_inc(sem, 16),
                        waits=m_evs, dma=1)
                    m_strip_free = m_stz
                mi += 1

            m_dbev = []
            for g in range(4):
                m_e = s.step("scalar", lambda e, d=d, g=g: e.copy(
                    dbc_sb[:, d * TSL + g * 512: d * TSL + (g + 1) * 512],
                    pdb[g][:, :]), waits=[m_db_last[g]])
                bank_free[id(pdb[g])] = m_e
                m_dbev.append(m_e)

            s.step("gpsimd", lambda e, sem, d=d: e.dma_start(
                out=o_bc[d, :, :],
                in_=dbc_sb[64:96, d * TSL:(d + 1) * TSL]).then_inc(sem, 16),
                waits=m_dbev, dma=1)

            m_dt_prev_store = None
            for mt in range(16):
                idx = d * 16 + mt
                m_sp1s = []
                for g in range(4):
                    bank = pz[g % 3]

                    def dtmm(e, d=d, mt=mt, g=g, bank=bank):
                        return e.matmul(
                            bank[:, :],
                            dtw[0:64, d * DI + mt * 128: d * DI + (mt + 1) * 128],
                            dbc_sb[0:64, d * TSL + g * 512: d * TSL + (g + 1) * 512],
                            start=True, stop=True)
                    m_mm = s.step("tensor", dtmm,
                                  waits=[m_dbev[-1], bank_free.get(id(bank))])
                    m_sp1 = s.step("scalar", lambda e, g=g, bank=bank, idx=idx: e.activation(
                        spe[:, g * 512:(g + 1) * 512], bank[:, :], AF.Exp,
                        bias=dtb[:, idx:idx + 1]), waits=[m_mm])
                    bank_free[id(bank)] = m_sp1
                    m_sp1s.append(m_sp1)
                m_sp2 = s.step("scalar", lambda e: e.activation(
                    dt_sb[:, :], spe[:, :], AF.Ln, bias=cst[:, 1:2]),
                    waits=m_sp1s + [m_dt_prev_store, m_cst])
                m_dt_prev_store = s.step("gpsimd", lambda e, sem, d=d, mt=mt: e.dma_start(
                    out=o_dt[d, mt * 128:(mt + 1) * 128, :],
                    in_=dt_sb[:, :]).then_inc(sem, 16), waits=[m_sp2], dma=1)

        s.finish(block)
    return nc



def _run_L1(inputs):
    xfull = np.asarray(inputs["L0_attn_x"]).reshape(L, DM)
    xpad = np.zeros((L + 2 * HALO, DM), np.float32)
    xpad[HALO:HALO + L] = xfull
    nc = build_L1()
    shared = {
        "in_w": np.ascontiguousarray(inputs["L0_attn_in_w"], np.float32),
        "conv_w": np.ascontiguousarray(inputs["L0_attn_conv_w"], np.float32),
        "conv_b": np.ascontiguousarray(inputs["L0_attn_conv_b"], np.float32),
        "xp_w": np.ascontiguousarray(inputs["L0_attn_xp_w"], np.float32),
        "dt_w": np.ascontiguousarray(inputs["L0_attn_dt_w"], np.float32),
        "dt_b": np.ascontiguousarray(inputs["L0_attn_dt_b"], np.float32),
    }
    in_maps = []
    for k in range(NCORES):
        m = dict(shared)
        m["x"] = np.ascontiguousarray(xpad[k * TSL:k * TSL + TSL + 2 * HALO])
        in_maps.append(m)
    res = run_bass_kernel_spmd(nc, in_maps, core_ids=list(range(NCORES)))
    out = {}
    for name in ("o_dt", "o_xc", "o_sz", "o_bc"):
        out[name] = np.concatenate([r[name] for r in res.results], axis=2)
    return out





def build_L2():
    T = 256
    NCH = L // T
    NU = DSL // 128
    nc = bass.Bass("TRN2", target_bir_lowering=False, detect_race_conditions=False)

    i_dt = nc.dram_tensor("i_dt", [2, DSL, L], F32, kind="ExternalInput").ap()
    i_xc = nc.dram_tensor("i_xc", [2, DSL, L], F32, kind="ExternalInput").ap()
    i_sz = nc.dram_tensor("i_sz", [2, DSL, L], F32, kind="ExternalInput").ap()
    i_b = nc.dram_tensor("i_b", [2, NST, L], F32, kind="ExternalInput").ap()
    i_c = nc.dram_tensor("i_c", [2, NST, L], F32, kind="ExternalInput").ap()
    i_alog = nc.dram_tensor("i_alog", [2, DSL, NST], F32, kind="ExternalInput").ap()
    i_d = nc.dram_tensor("i_d", [2, DSL], F32, kind="ExternalInput").ap()
    o_mix = nc.dram_tensor("o_mix", [2, DSL, L], F32, kind="ExternalOutput").ap()

    with contextlib.ExitStack() as ctx:
        sbt = lambda name, shape, dt=F32: ctx.enter_context(nc.sbuf_tensor(name, shape, dt))

        dA = sbt("dA", [128, NST * T])
        ubuf = sbt("ubuf", [128, NST * T])
        hbuf = sbt("hbuf", [128, NST * T])
        Brep = [sbt("Brep0", [128, NST * T]), sbt("Brep1", [128, NST * T])]
        Crep = [sbt("Crep0", [128, NST * T]), sbt("Crep1", [128, NST * T])]
        dt_c = [[sbt(f"dt_c{p}{u}", [128, T]) for u in range(NU)] for p in range(2)]
        xc_c = [[sbt(f"xc_c{p}{u}", [128, T]) for u in range(NU)] for p in range(2)]
        sz_c = [[sbt(f"sz_c{p}{u}", [128, T]) for u in range(NU)] for p in range(2)]
        vb = sbt("vb", [128, T])
        yb = sbt("yb", [128, T])
        t1b = sbt("t1b", [128, T])
        mixb = [sbt("mixb0", [128, T]), sbt("mixb1", [128, T])]
        Atile = [[sbt(f"At{d}{u}", [128, NST]) for u in range(NU)] for d in range(2)]
        Dcol = [[sbt(f"Dc{d}{u}", [128, 1]) for u in range(NU)] for d in range(2)]
        carry = [[sbt(f"cy{d}{u}", [128, NST]) for u in range(NU)] for d in range(2)]
        etmp = sbt("etmp", [128, NST])

        s = Sched(nc)
        s.make_sems(ctx)
        block = ctx.enter_context(nc.Block())
        buf_free = {}

        m_setup = []
        for d in range(2):
            for u in range(NU):
                m_l = s.step("sync", lambda e, sem, d=d, u=u: e.dma_start(
                    out=etmp[:, :] if False else Atile[d][u][:, :],
                    in_=i_alog[d, u * 128:(u + 1) * 128, :]).then_inc(sem, 16), dma=1)
                m_e = s.step("scalar", lambda e, d=d, u=u: e.activation(
                    etmp[:, :], Atile[d][u][:, :], AF.Exp), waits=[m_l,
                    m_setup[-1] if m_setup else None])
                m_n = s.step("vector", lambda e, d=d, u=u: e.tensor_scalar(
                    Atile[d][u][:, :], etmp[:, :], -1.0, None, ALU.mult), waits=[m_e])
                m_z = s.step("vector", lambda e, d=d, u=u: e.memset(
                    carry[d][u][:, :], 0.0))
                m_dc = s.step("sync", lambda e, sem, d=d, u=u: e.dma_start(
                    out=Dcol[d][u][:, :],
                    in_=i_d[d, u * 128:(u + 1) * 128].unsqueeze(1)).then_inc(sem, 16), dma=1)
                m_setup += [m_n, m_z, m_dc]
        m_set_all = m_setup

        m_dA_free = None
        m_u_free = None
        m_h_free = None
        for d in range(2):
            for c in range(NCH):
                par = c % 2
                t0 = c * T

                def loadc(e, sem, d=d, t0=t0, par=par):
                    for u in range(NU):
                        r0, r1 = u * 128, (u + 1) * 128
                        e.dma_start(out=dt_c[par][u][:, :],
                                    in_=i_dt[d, r0:r1, t0:t0 + T]).then_inc(sem, 16)
                        e.dma_start(out=xc_c[par][u][:, :],
                                    in_=i_xc[d, r0:r1, t0:t0 + T]).then_inc(sem, 16)
                        e.dma_start(out=sz_c[par][u][:, :],
                                    in_=i_sz[d, r0:r1, t0:t0 + T]).then_inc(sem, 16)
                m_ld = s.step("sync", loadc,
                              waits=[buf_free.get(("chunk", par))], dma=3 * NU)

                wB = [buf_free.get(("Brep", par))]
                def bspread(e, sem, d=d, t0=t0, par=par):
                    for n in range(NST):
                        e.dma_start(out=Brep[par][0:1, n * T:(n + 1) * T],
                                    in_=i_b[d, n:n + 1, t0:t0 + T]).then_inc(sem, 16)
                m_bs = s.step("sync", bspread, waits=wB, dma=NST)
                for k in [1, 2, 4, 8, 16, 32, 64]:
                    m_bs = s.step("sync", lambda e, sem, k=k, par=par: e.dma_start(
                        out=Brep[par][k:2 * k, :], in_=Brep[par][0:k, :]).then_inc(sem, 16),
                        dma=1)
                wC = [buf_free.get(("Crep", par))]
                def cspread(e, sem, d=d, t0=t0, par=par):
                    for n in range(NST):
                        e.dma_start(out=Crep[par][0:1, n * T:(n + 1) * T],
                                    in_=i_c[d, n:n + 1, t0:t0 + T]).then_inc(sem, 16)
                m_cs = s.step("gpsimd", cspread, waits=wC, dma=NST)
                for k in [1, 2, 4, 8, 16, 32, 64]:
                    m_cs = s.step("gpsimd", lambda e, sem, k=k, par=par: e.dma_start(
                        out=Crep[par][k:2 * k, :], in_=Crep[par][0:k, :]).then_inc(sem, 16),
                        dma=1)

                for u in range(NU):
                    m_dA = None
                    for n in range(NST):
                        m_dA = s.step("scalar", lambda e, d=d, u=u, n=n, par=par: e.activation(
                            dA[:, n * T:(n + 1) * T], dt_c[par][u][:, :], AF.Exp,
                            scale=Atile[d][u][:, n:n + 1]),
                            waits=[m_ld, m_dA_free] + (m_set_all if d == 0 and c == 0 else []))
                    m_v = s.step("gpsimd", lambda e, u=u, par=par: e.tensor_tensor(
                        vb[:, :], dt_c[par][u][:, :], xc_c[par][u][:, :], ALU.mult),
                        waits=[m_ld, buf_free.get("vb")])
                    m_u = None
                    for n in range(NST):
                        m_u = s.step("gpsimd", lambda e, n=n, par=par: e.tensor_tensor(
                            ubuf[:, n * T:(n + 1) * T], vb[:, :],
                            Brep[par][:, n * T:(n + 1) * T], ALU.mult),
                            waits=[m_v, m_bs, m_u_free])
                    buf_free["vb"] = m_u
                    buf_free[("Brep", par)] = m_u
                    m_sc = None
                    for n in range(NST):
                        m_sc = s.step("vector", lambda e, d=d, u=u, n=n: e.tensor_tensor_scan(
                            hbuf[:, n * T:(n + 1) * T], dA[:, n * T:(n + 1) * T],
                            ubuf[:, n * T:(n + 1) * T], carry[d][u][:, n:n + 1],
                            ALU.mult, ALU.add),
                            waits=[m_dA, m_u, m_h_free])
                    m_dA_free = m_sc
                    m_cy = s.step("vector", lambda e, d=d, u=u: e.tensor_copy(
                        carry[d][u][:, :],
                        hbuf[:].rearrange("p (n t) -> p n t", n=NST)[:, :, T - 1]),
                        waits=[m_sc])
                    m_hc = None
                    for n in range(NST):
                        m_hc = s.step("vector", lambda e, n=n, par=par: e.tensor_tensor(
                            ubuf[:, n * T:(n + 1) * T], hbuf[:, n * T:(n + 1) * T],
                            Crep[par][:, n * T:(n + 1) * T], ALU.mult),
                            waits=[m_sc, m_cs])
                    m_h_free = m_hc
                    buf_free[("Crep", par)] = m_hc
                    m_y = s.step("vector", lambda e: e.tensor_reduce(
                        yb[:, :], ubuf[:].rearrange("p (n t) -> p t n", n=NST),
                        AX.X, ALU.add), waits=[m_hc])
                    m_u_free = m_y
                    m_t1 = s.step("vector", lambda e, d=d, u=u, par=par: e.scalar_tensor_tensor(
                        t1b[:, :], xc_c[par][u][:, :], Dcol[d][u][:, 0:1], yb[:, :],
                        ALU.mult, ALU.add), waits=[m_y])
                    mpar = (c * NU + u) % 2
                    m_mx = s.step("vector", lambda e, u=u, par=par, mpar=mpar: e.tensor_tensor(
                        mixb[mpar][:, :], t1b[:, :], sz_c[par][u][:, :], ALU.mult),
                        waits=[m_t1, buf_free.get(("mix", mpar))])
                    m_st = s.step("gpsimd", lambda e, sem, d=d, u=u, t0=t0, mpar=mpar: e.dma_start(
                        out=o_mix[d, u * 128:(u + 1) * 128, t0:t0 + T],
                        in_=mixb[mpar][:, :]).then_inc(sem, 16),
                        waits=[m_mx], dma=1)
                    buf_free[("mix", mpar)] = m_st
                    if u == NU - 1:
                        buf_free[("chunk", par)] = m_mx

        s.finish(block)
    return nc


def _run_L2(l1out, inputs):
    dt_f = l1out["o_dt"]; xc_f = l1out["o_xc"]; sz_f = l1out["o_sz"]; bc_f = l1out["o_bc"]
    alog = np.ascontiguousarray(inputs["L0_attn_A_log"], np.float32)
    Dw = np.ascontiguousarray(inputs["L0_attn_D"], np.float32)

    def rev(a):
        out = a.copy()
        out[1] = a[1][..., ::-1]
        return np.ascontiguousarray(out)

    dt_r, xc_r, sz_r = rev(dt_f), rev(xc_f), rev(sz_f)
    b_r = rev(bc_f[:, 0:NST, :])
    c_r = rev(bc_f[:, NST:2 * NST, :])
    nc = build_L2()
    in_maps = []
    for j in range(NCORES):
        r0, r1 = j * DSL, (j + 1) * DSL
        in_maps.append({
            "i_dt": np.ascontiguousarray(dt_r[:, r0:r1]),
            "i_xc": np.ascontiguousarray(xc_r[:, r0:r1]),
            "i_sz": np.ascontiguousarray(sz_r[:, r0:r1]),
            "i_b": b_r, "i_c": c_r,
            "i_alog": np.ascontiguousarray(alog[:, r0:r1]),
            "i_d": np.ascontiguousarray(Dw[:, r0:r1]),
        })
    res = run_bass_kernel_spmd(nc, in_maps, core_ids=list(range(NCORES)))
    mix = np.concatenate([r["o_mix"] for r in res.results], axis=1)
    mix[1] = mix[1][..., ::-1]
    return np.ascontiguousarray(mix)





def build_L3():
    NTT = TSL // 128
    NG = TSL // 512
    NKT = DM // 128
    NKI = DI // 128
    NOF = FF // 128
    nc = bass.Bass("TRN2", target_bir_lowering=False, detect_race_conditions=False)

    x = nc.dram_tensor("x", [TSL, DM], F32, kind="ExternalInput").ap()
    i_mix = nc.dram_tensor("i_mix", [2, DI, TSL], F32, kind="ExternalInput").ap()
    out_w = nc.dram_tensor("out_w", [2, DI, DM], F32, kind="ExternalInput").ap()
    proj_w = nc.dram_tensor("proj_w", [2 * DM, DM], F32, kind="ExternalInput").ap()
    proj_b = nc.dram_tensor("proj_b", [DM], F32, kind="ExternalInput").ap()
    ln_g = nc.dram_tensor("ln_g", [DM], F32, kind="ExternalInput").ap()
    ln_b = nc.dram_tensor("ln_b", [DM], F32, kind="ExternalInput").ap()
    fc1_q = nc.dram_tensor("fc1_q", [DM, FF], I32, kind="ExternalInput").ap()
    fc1_s = nc.dram_tensor("fc1_s", [DM // GRP, FF], F32, kind="ExternalInput").ap()
    fc1_b = nc.dram_tensor("fc1_b", [FF], F32, kind="ExternalInput").ap()
    fc2_q = nc.dram_tensor("fc2_q", [FF, DM], I32, kind="ExternalInput").ap()
    fc2_s = nc.dram_tensor("fc2_s", [FF // GRP, DM], F32, kind="ExternalInput").ap()
    fc2_b = nc.dram_tensor("fc2_b", [DM], F32, kind="ExternalInput").ap()
    o_y = nc.dram_tensor("o_y", [TSL, DM], F32, kind="ExternalOutput").ap()
    s_hT = nc.dram_tensor("s_hT", [DM, TSL], F32).ap()
    s_res = nc.dram_tensor("s_res", [TSL, DM], F32).ap()
    s_attn = nc.dram_tensor("s_attn", [DM, TSL], F32).ap()
    s_w1 = nc.dram_tensor("s_w1", [DM, FF], F32).ap()
    s_w2 = nc.dram_tensor("s_w2", [FF, DM], F32).ap()

    with contextlib.ExitStack() as ctx:
        sbt = lambda name, shape, dt=F32: ctx.enter_context(nc.sbuf_tensor(name, shape, dt))
        pst = lambda name, shape: ctx.enter_context(nc.psum_tensor(name, shape, F32))

        AR1 = sbt("AR1", [128, 16384])
        AR2 = sbt("AR2", [128, 8192])
        AR3 = sbt("AR3", [128, 8192])
        AR4 = sbt("AR4", [128, 4096])
        AR5 = sbt("AR5", [128, 4096])
        AR6 = sbt("AR6", [128, 4096])
        AR7 = sbt("AR7", [128, 4096])

        st = sbt("stats", [128, 16])
        ident = sbt("ident", [128, 128])
        io_i = sbt("io_i", [128, 128], F32)
        io_p = sbt("io_p", [128, 1], F32)
        cst = sbt("cst", [128, 2])
        pbc = sbt("pbc", [128, NKT])
        f1bc = sbt("f1bc", [128, NOF])
        f2bc = sbt("f2bc", [128, NKT])

        xtm = [AR5[:, 2048:3072], AR5[:, 3072:4096]]
        grep = AR5[:, 0:1024]
        brep = AR5[:, 1024:2048]
        xsq = AR7[:, 0:1024]
        xn2 = AR7[:, 1024:2048]
        htm = [AR6[:, 0:1024], AR6[:, 1024:2048]]
        hstg = AR6[:, 2048:3072]
        mixc = AR1[:, 0:NKI * 512]
        ydT = [AR1[:, NKI * 512:NKI * 512 + NKT * 512], AR2[:, 0:NKT * 512]]
        wstr = [AR4[:, 0:2048], AR4[:, 2048:4096]]
        attc = AR6[:, 0:NKT * 512]
        qstg = [AR3[:, 0:4096], AR3[:, 4096:8192]]
        screp = [AR2[:, 0:4096], AR2[:, 4096:8192]]
        wdq = AR5[:, 0:4096]
        syb = AR1[:, 0:NOF * 512]
        hTc = [AR2[:, 0:4096], AR2[:, 4096:8192]]
        w2str = [AR3[:, 0:4096], AR3[:, 4096:8192]]
        ffnc = AR5[:, 0:NKT * 512]
        otm = [AR7[:, 0:1024], AR7[:, 1024:2048]]
        fxtm = [AR7[:, 2048:3072], AR7[:, 3072:4096]]

        pz = [pst("pz0", [128, 512]), pst("pz1", [128, 512]),
              pst("pz2", [128, 512]), pst("pz3", [128, 512])]

        s = Sched(nc)
        s.make_sems(ctx)
        block = ctx.enter_context(nc.Block())
        bank_free = {}
        buf_free = {}

        def mkcst(e):
            e.memset(cst[:, 0:1], EPS)
            return e.memset(cst[:, 1:2], 1.0)
        m_cst = s.step("vector", mkcst)
        m_io1 = s.step("gpsimd", lambda e: e.iota(
            io_i[:, :], [[1, 128]], channel_multiplier=0,
            allow_small_or_imprecise_dtypes=True))
        m_io2 = s.step("gpsimd", lambda e: e.iota(
            io_p[:, 0:1], [[0, 1]], channel_multiplier=1,
            allow_small_or_imprecise_dtypes=True))
        m_id = s.step("vector", lambda e: e.tensor_scalar(
            ident[:, :], io_i[:, :], io_p[:, 0:1], None, ALU.is_equal),
            waits=[m_io1, m_io2])

        def ldsetup(e, sem):
            e.dma_start(out=grep[0:1, :], in_=ln_g.unsqueeze(0)).then_inc(sem, 16)
            e.dma_start(out=brep[0:1, :], in_=ln_b.unsqueeze(0)).then_inc(sem, 16)
            for o in range(NKT):
                e.dma_start(out=pbc[:, o:o + 1],
                            in_=proj_b[o * 128:(o + 1) * 128].unsqueeze(1)).then_inc(sem, 16)
                e.dma_start(out=f2bc[:, o:o + 1],
                            in_=fc2_b[o * 128:(o + 1) * 128].unsqueeze(1)).then_inc(sem, 16)
            for o in range(NOF):
                e.dma_start(out=f1bc[:, o:o + 1],
                            in_=fc1_b[o * 128:(o + 1) * 128].unsqueeze(1)).then_inc(sem, 16)
        m_su = s.step("sync", ldsetup, dma=2 + 2 * NKT + NOF)
        m_db = m_su
        for k in [1, 2, 4, 8, 16, 32, 64]:
            def dbl(e, sem, k=k):
                e.dma_start(out=grep[k:2 * k, :], in_=grep[0:k, :]).then_inc(sem, 16)
                e.dma_start(out=brep[k:2 * k, :], in_=brep[0:k, :]).then_inc(sem, 16)
            m_db = s.step("sync", dbl, waits=[m_db], dma=2)

        tr_i = 0
        for t in range(NTT):
            m_dma = s.step("sync", lambda e, sem, t=t: e.dma_start(
                out=xtm[t % 2][:, :], in_=x[t * 128:(t + 1) * 128, :]).then_inc(sem, 16),
                waits=[buf_free.get(("xtm", t % 2))], dma=1)
            m_s1 = s.step("vector", lambda e, t=t: e.reduce_sum(
                st[:, 0:1], xtm[t % 2][:, :], AX.X), waits=[m_dma])
            m_s2 = s.step("scalar", lambda e, t=t: e.activation(
                xsq[:, :], xtm[t % 2][:, :], AF.Square,
                accum_out=st[:, 1:2]), waits=[m_dma])
            m_v1 = s.step("vector", lambda e: e.tensor_scalar(
                st[:, 2:3], st[:, 0:1], 1.0 / DM, None, ALU.mult), waits=[m_s1, m_s2])
            m_v1b = s.step("vector", lambda e: e.tensor_scalar(
                st[:, 3:4], st[:, 1:2], 1.0 / DM, None, ALU.mult), waits=[m_v1])
            m_v2 = s.step("vector", lambda e: e.tensor_tensor(
                st[:, 4:5], st[:, 2:3], st[:, 2:3], ALU.mult), waits=[m_v1b])
            m_vr = s.step("vector", lambda e: e.tensor_tensor(
                st[:, 5:6], st[:, 3:4], st[:, 4:5], ALU.subtract), waits=[m_v2])
            m_sq = s.step("scalar", lambda e: e.activation(
                st[:, 6:7], st[:, 5:6], AF.Sqrt, bias=cst[:, 0:1]),
                waits=[m_vr, m_cst])
            m_rc = s.step("vector", lambda e: e.reciprocal(
                st[:, 7:8], st[:, 6:7]), waits=[m_sq])
            m_xn = s.step("vector", lambda e, t=t: e.tensor_scalar(
                xsq[:, :], xtm[t % 2][:, :], st[:, 2:3], st[:, 7:8],
                ALU.subtract, ALU.mult), waits=[m_rc, m_s2])
            buf_free[("xtm", t % 2)] = m_xn
            m_rt = s.step("vector", lambda e, t=t: e.tensor_tensor(
                fxtm[t % 2][:, :], xtm[t % 2][:, :], xsq[:, :], ALU.add),
                waits=[m_xn, buf_free.get(("rtm", t % 2))])
            buf_free[("xtm", t % 2)] = m_rt
            m_str = s.step("gpsimd", lambda e, sem, t=t: e.dma_start(
                out=s_res[t * 128:(t + 1) * 128, :],
                in_=fxtm[t % 2][:, :]).then_inc(sem, 16), waits=[m_rt], dma=1)
            buf_free[("rtm", t % 2)] = m_str
            m_t1 = s.step("vector", lambda e: e.reduce_sum(
                st[:, 8:9], xsq[:, :], AX.X), waits=[m_xn])
            m_t2 = s.step("scalar", lambda e: e.activation(
                xn2[:, :], xsq[:, :], AF.Square, accum_out=st[:, 9:10]),
                waits=[m_xn])
            m_u1 = s.step("vector", lambda e: e.tensor_scalar(
                st[:, 10:11], st[:, 8:9], 1.0 / DM, None, ALU.mult), waits=[m_t1, m_t2])
            m_u1b = s.step("vector", lambda e: e.tensor_scalar(
                st[:, 11:12], st[:, 9:10], 1.0 / DM, None, ALU.mult), waits=[m_u1])
            m_u2 = s.step("vector", lambda e: e.tensor_tensor(
                st[:, 12:13], st[:, 10:11], st[:, 10:11], ALU.mult), waits=[m_u1b])
            m_ur = s.step("vector", lambda e: e.tensor_tensor(
                st[:, 13:14], st[:, 11:12], st[:, 12:13], ALU.subtract), waits=[m_u2])
            m_sq2 = s.step("scalar", lambda e: e.activation(
                st[:, 14:15], st[:, 13:14], AF.Sqrt, bias=cst[:, 0:1]),
                waits=[m_ur])
            m_rc2 = s.step("vector", lambda e: e.reciprocal(
                st[:, 15:16], st[:, 14:15]), waits=[m_sq2])
            m_xn2 = s.step("vector", lambda e: e.tensor_scalar(
                xn2[:, :], xsq[:, :], st[:, 10:11], st[:, 15:16],
                ALU.subtract, ALU.mult), waits=[m_rc2])
            m_h1 = s.step("vector", lambda e: e.tensor_tensor(
                xsq[:, :], xn2[:, :], grep[:, :], ALU.mult), waits=[m_xn2, m_db])
            m_h2 = s.step("vector", lambda e, t=t: e.tensor_tensor(
                htm[t % 2][:, :], xsq[:, :], brep[:, :], ALU.add),
                waits=[m_h1, buf_free.get(("htm", t % 2))])

            m_evs = []
            for dsub in range(NKT):
                bank = pz[tr_i % 2]
                m_tr = s.step("tensor", lambda e, t=t, dsub=dsub, bank=bank: e.transpose(
                    bank[:, 0:128], htm[t % 2][:, dsub * 128:(dsub + 1) * 128],
                    ident[:, :]), waits=[m_h2, m_id, bank_free.get(id(bank))])
                m_ev = s.step("scalar", lambda e, dsub=dsub, bank=bank: e.copy(
                    hstg[:, dsub * 128:(dsub + 1) * 128], bank[:, 0:128]),
                    waits=[m_tr, buf_free.get("hstg")])
                bank_free[id(bank)] = m_ev
                m_evs.append(m_ev)
                tr_i += 1
            buf_free[("htm", t % 2)] = m_evs[-1]

            def sthT(e, sem, t=t):
                for dsub in range(NKT):
                    e.dma_start(out=s_hT[dsub * 128:(dsub + 1) * 128, t * 128:(t + 1) * 128],
                                in_=hstg[:, dsub * 128:(dsub + 1) * 128]).then_inc(sem, 16)
            m_sth = s.step("sync", sthT, waits=m_evs, dma=NKT)
            buf_free["hstg"] = m_sth

        for g in range(NG):
            t0 = g * 512
            m_yd = [None, None]
            for d in range(2):
                def ldmix(e, sem, d=d, t0=t0):
                    for kt in range(NKI):
                        e.dma_start(out=mixc[:, kt * 512:(kt + 1) * 512],
                                    in_=i_mix[d, kt * 128:(kt + 1) * 128, t0:t0 + 512]).then_inc(sem, 16)
                m_mx = s.step("sync", ldmix, waits=[buf_free.get("mixc")], dma=NKI)
                m_evl = []
                for o in range(NKT):
                    par = o % 2
                    def ldow(e, sem, d=d, o=o, par=par):
                        for kt in range(NKI):
                            e.dma_start(out=wstr[par][:, kt * 128:(kt + 1) * 128],
                                        in_=out_w[d, kt * 128:(kt + 1) * 128,
                                                  o * 128:(o + 1) * 128]).then_inc(sem, 16)
                    m_w = s.step("sync", ldow, waits=[buf_free.get(("wstr", par))], dma=NKI)
                    bank = pz[o % 2]
                    def mm1(e, o=o, par=par, bank=bank):
                        for kt in range(NKI):
                            inst = e.matmul(bank[:, :], wstr[par][:, kt * 128:(kt + 1) * 128],
                                            mixc[:, kt * 512:(kt + 1) * 512],
                                            start=(kt == 0), stop=(kt == NKI - 1))
                        return inst
                    m_m = s.step("tensor", mm1, waits=[m_w, m_mx, bank_free.get(id(bank))])
                    buf_free[("wstr", par)] = m_m
                    m_e = s.step("scalar", lambda e, d=d, o=o, bank=bank: e.copy(
                        ydT[d][:, o * 512:(o + 1) * 512], bank[:, :]), waits=[m_m])
                    bank_free[id(bank)] = m_e
                    m_evl.append(m_e)
                m_yd[d] = m_evl
                buf_free["mixc"] = m_evl[-1]
            for o in range(NKT):
                par = o % 2
                def ldpj(e, sem, o=o, par=par):
                    for kt in range(2 * NKT):
                        e.dma_start(out=wstr[par][:, kt * 128:(kt + 1) * 128],
                                    in_=proj_w[kt * 128:(kt + 1) * 128,
                                               o * 128:(o + 1) * 128]).then_inc(sem, 16)
                m_w = s.step("sync", ldpj, waits=[buf_free.get(("wstr", par))], dma=2 * NKT)
                bank = pz[2 + o % 2]
                def mm2(e, o=o, par=par, bank=bank):
                    for kt in range(2 * NKT):
                        src = ydT[kt // NKT]
                        k2 = kt % NKT
                        inst = e.matmul(bank[:, :], wstr[par][:, kt * 128:(kt + 1) * 128],
                                        src[:, k2 * 512:(k2 + 1) * 512],
                                        start=(kt == 0), stop=(kt == 2 * NKT - 1))
                    return inst
                m_m = s.step("tensor", mm2,
                             waits=[m_w, m_yd[0][-1], m_yd[1][-1], bank_free.get(id(bank))])
                buf_free[("wstr", par)] = m_m
                m_e = s.step("scalar", lambda e, o=o, bank=bank: e.activation(
                    hstg[:, o * 128:(o + 1) * 128] if False else attc[:, o * 512:(o + 1) * 512],
                    bank[:, :], AF.Identity, bias=pbc[:, o:o + 1]),
                    waits=[m_m, m_su, buf_free.get("attc")])
                bank_free[id(bank)] = m_e
                m_st = s.step("gpsimd", lambda e, sem, o=o, t0=t0: e.dma_start(
                    out=s_attn[o * 128:(o + 1) * 128, t0:t0 + 512],
                    in_=attc[:, o * 512:(o + 1) * 512]).then_inc(sem, 16),
                    waits=[m_e], dma=1)
            buf_free["attc"] = m_st
        buf_free[("screp", 0)] = m_m
        buf_free[("screp", 1)] = m_m
        buf_free[("qstg", 0)] = m_m
        buf_free[("qstg", 1)] = m_m

        for (qsrc, ssrc, wdst, nkt, ncols) in (
                (fc1_q, fc1_s, s_w1, NKT, FF), (fc2_q, fc2_s, s_w2, NOF, DM)):
            for kt in range(nkt):
                par = kt % 2
                m_q = s.step("gpsimd", lambda e, sem, qsrc=qsrc, kt=kt, par=par, ncols=ncols: e.dma_start(
                    out=qstg[par][:, 0:ncols],
                    in_=qsrc[kt * 128:(kt + 1) * 128, :]).then_inc(sem, 16),
                    waits=[buf_free.get(("qstg", par))], dma=1)
                m_sp = s.step("sync", lambda e, sem, ssrc=ssrc, kt=kt, par=par, ncols=ncols: e.dma_start(
                    out=screp[par][0:1, 0:ncols],
                    in_=ssrc[kt:kt + 1, :]).then_inc(sem, 16),
                    waits=[buf_free.get(("screp", par))], dma=1)
                for k in [1, 2, 4, 8, 16, 32, 64]:
                    m_sp = s.step("sync", lambda e, sem, k=k, par=par, ncols=ncols: e.dma_start(
                        out=screp[par][k:2 * k, 0:ncols],
                        in_=screp[par][0:k, 0:ncols]).then_inc(sem, 16), dma=1)
                m_dq = s.step("vector", lambda e, par=par, ncols=ncols: e.scalar_tensor_tensor(
                    wdq[:, 0:ncols], qstg[par][:, 0:ncols], 128.0,
                    screp[par][:, 0:ncols], ALU.subtract, ALU.mult),
                    waits=[m_q, m_sp, buf_free.get("wdq")])
                buf_free[("qstg", par)] = m_dq
                buf_free[("screp", par)] = m_dq
                m_sw = s.step("scalar", lambda e, sem, wdst=wdst, kt=kt, ncols=ncols: e.dma_start(
                    out=wdst[kt * 128:(kt + 1) * 128, :],
                    in_=wdq[:, 0:ncols]).then_inc(sem, 16), waits=[m_dq], dma=1)
                buf_free["wdq"] = m_sw

        buf_free[("hTc", 0)] = m_dq
        buf_free[("hTc", 1)] = m_dq
        buf_free[("w2str", 0)] = m_dq
        buf_free[("w2str", 1)] = m_dq
        m_w1_done = m_sw
        for g in range(NG):
            t0 = g * 512
            par = g % 2
            def ldh(e, sem, t0=t0, par=par):
                for kt in range(NKT):
                    e.dma_start(out=hTc[par][:, kt * 512:(kt + 1) * 512],
                                in_=s_hT[kt * 128:(kt + 1) * 128, t0:t0 + 512]).then_inc(sem, 16)
            m_h = s.step("sync", ldh, waits=[buf_free.get(("hTc", par))], dma=NKT)
            m_sy = []
            for o in range(NOF):
                wpar = o % 2
                def ldw1(e, sem, o=o, wpar=wpar):
                    for kt in range(NKT):
                        e.dma_start(out=wstr[wpar][:, kt * 128:(kt + 1) * 128],
                                    in_=s_w1[kt * 128:(kt + 1) * 128,
                                             o * 128:(o + 1) * 128]).then_inc(sem, 16)
                m_w = s.step("sync", ldw1,
                             waits=[buf_free.get(("wstr", wpar)), m_w1_done], dma=NKT)
                bank = pz[o % 2]
                def mmf1(e, o=o, wpar=wpar, par=par, bank=bank):
                    for kt in range(NKT):
                        inst = e.matmul(bank[:, :], wstr[wpar][:, kt * 128:(kt + 1) * 128],
                                        hTc[par][:, kt * 512:(kt + 1) * 512],
                                        start=(kt == 0), stop=(kt == NKT - 1))
                    return inst
                m_m = s.step("tensor", mmf1, waits=[m_w, m_h, bank_free.get(id(bank))])
                buf_free[("wstr", wpar)] = m_m
                m_e = s.step("scalar", lambda e, o=o, bank=bank: e.activation(
                    syb[:, o * 512:(o + 1) * 512], bank[:, :], AF.Silu,
                    bias=f1bc[:, o:o + 1]), waits=[m_m, m_su, buf_free.get("syb")])
                bank_free[id(bank)] = m_e
                m_sy.append(m_e)
            buf_free[("hTc", par)] = m_m
            m_fc2 = []
            for o in range(NKT):
                wpar = o % 2
                def ldw2(e, sem, o=o, wpar=wpar):
                    for kt in range(NOF):
                        e.dma_start(out=w2str[wpar][:, kt * 128:(kt + 1) * 128],
                                    in_=s_w2[kt * 128:(kt + 1) * 128,
                                             o * 128:(o + 1) * 128]).then_inc(sem, 16)
                m_w = s.step("sync", ldw2,
                             waits=[buf_free.get(("w2str", wpar)), m_w_done], dma=NOF)
                bank = pz[2 + o % 2]
                def mmf2(e, o=o, wpar=wpar, bank=bank):
                    for kt in range(NOF):
                        inst = e.matmul(bank[:, :], w2str[wpar][:, kt * 128:(kt + 1) * 128],
                                        syb[:, kt * 512:(kt + 1) * 512],
                                        start=(kt == 0), stop=(kt == NOF - 1))
                    return inst
                m_m = s.step("tensor", mmf2, waits=[m_w, bank_free.get(id(bank))] + m_sy)
                buf_free[("w2str", wpar)] = m_m
                m_e = s.step("vector", lambda e, o=o, bank=bank: e.tensor_scalar(
                    ffnc[:, o * 512:(o + 1) * 512], bank[:, :],
                    f2bc[:, o:o + 1], 0.5, ALU.add, ALU.mult),
                    waits=[m_m, m_su, buf_free.get("ffnc")])
                bank_free[id(bank)] = m_e
                m_fc2.append(m_e)
            buf_free["syb"] = m_m
            def ldat(e, sem, t0=t0):
                for o in range(NKT):
                    e.dma_start(out=attc[:, o * 512:(o + 1) * 512],
                                in_=s_attn[o * 128:(o + 1) * 128, t0:t0 + 512]).then_inc(sem, 16)
            m_at = s.step("sync", ldat, waits=[buf_free.get("attc")], dma=NKT)
            m_fa = s.step("vector", lambda e: e.tensor_tensor(
                ffnc[:, :], ffnc[:, :], attc[:, :], ALU.add),
                waits=[m_at] + m_fc2)
            buf_free["attc"] = m_fa
            buf_free["ffnc"] = m_fa
            for ts in range(4):
                tpar = ts % 2
                m_x = s.step("sync", lambda e, sem, t0=t0, ts=ts, tpar=tpar: e.dma_start(
                    out=fxtm[tpar][:, :],
                    in_=s_res[t0 + ts * 128:t0 + (ts + 1) * 128, :]).then_inc(sem, 16),
                    waits=[buf_free.get(("fxtm", tpar)), buf_free.get(("rtm", tpar))], dma=1)
                m_adds = []
                for o in range(NKT):
                    bank = pz[(ts * NKT + o) % 2]
                    m_tr = s.step("tensor", lambda e, o=o, ts=ts, bank=bank: e.transpose(
                        bank[:, 0:128],
                        ffnc[:, o * 512 + ts * 128: o * 512 + (ts + 1) * 128],
                        ident[:, :]),
                        waits=[m_fa, m_id, bank_free.get(id(bank))])
                    m_ad = s.step("vector", lambda e, o=o, ts=ts, tpar=tpar, bank=bank: e.tensor_tensor(
                        otm[tpar][:, o * 128:(o + 1) * 128], bank[:, 0:128],
                        fxtm[tpar][:, o * 128:(o + 1) * 128], ALU.add),
                        waits=[m_tr, m_x, buf_free.get(("otm", tpar))])
                    bank_free[id(bank)] = m_ad
                    m_adds.append(m_ad)
                m_so = s.step("sync", lambda e, sem, t0=t0, ts=ts, tpar=tpar: e.dma_start(
                    out=o_y[t0 + ts * 128:t0 + (ts + 1) * 128, :],
                    in_=otm[tpar][:, :]).then_inc(sem, 16), waits=m_adds, dma=1)
                buf_free[("otm", tpar)] = m_so
                buf_free[("fxtm", tpar)] = m_adds[-1]

        s.finish(block)
    return nc


def _run_L3(mix, inputs):
    xfull = np.asarray(inputs["L0_attn_x"]).reshape(L, DM).astype(np.float32)
    nc = build_L3()
    shared = {
        "out_w": np.ascontiguousarray(inputs["L0_attn_out_w"], np.float32),
        "proj_w": np.ascontiguousarray(inputs["L0_attn_proj_w"], np.float32),
        "proj_b": np.ascontiguousarray(inputs["L0_attn_proj_b"], np.float32),
        "ln_g": np.ascontiguousarray(inputs["L0_ffn_ln_gamma"], np.float32),
        "ln_b": np.ascontiguousarray(inputs["L0_ffn_ln_beta"], np.float32),
        "fc1_q": np.ascontiguousarray(inputs["L0_ffn_fc1_q"], np.int32),
        "fc1_s": np.ascontiguousarray(inputs["L0_ffn_fc1_scales"], np.float32),
        "fc1_b": np.ascontiguousarray(inputs["L0_ffn_fc1_bias"], np.float32),
        "fc2_q": np.ascontiguousarray(inputs["L0_ffn_fc2_q"], np.int32),
        "fc2_s": np.ascontiguousarray(inputs["L0_ffn_fc2_scales"], np.float32),
        "fc2_b": np.ascontiguousarray(inputs["L0_ffn_fc2_bias"], np.float32),
    }
    in_maps = []
    for k in range(NCORES):
        m = dict(shared)
        m["x"] = np.ascontiguousarray(xfull[k * TSL:(k + 1) * TSL])
        m["i_mix"] = np.ascontiguousarray(mix[:, :, k * TSL:(k + 1) * TSL])
        in_maps.append(m)
    res = run_bass_kernel_spmd(nc, in_maps, core_ids=list(range(NCORES)))
    return np.concatenate([r["o_y"] for r in res.results], axis=0)



def _barrier(s):
    cur = [(k, s.val[k]) for k in s.SEMS if s.val[k] > 0]
    out = []
    for eng in ("tensor", "vector", "scalar", "gpsimd"):
        out.append(s.step(eng, lambda e: e.nop(), waits=cur))
    return out


def build_fused():
    T = 256
    NCH = L // T
    NU = DSL // 128
    CW = 512
    NGC = L // CW
    NKT = DM // 128
    NOF = FF // 128
    nc = bass.Bass("TRN2", target_bir_lowering=False, detect_race_conditions=False)

    x = nc.dram_tensor("x", [L, DM], F32, kind="ExternalInput").ap()
    xs = nc.dram_tensor("xs", [TSL, DM], F32, kind="ExternalInput").ap()
    inw = nc.dram_tensor("inw", [2, DM, 2 * DSL], F32, kind="ExternalInput").ap()
    conv_w = nc.dram_tensor("conv_w", [2, DSL, DCONV], F32, kind="ExternalInput").ap()
    conv_b = nc.dram_tensor("conv_b", [2, DSL], F32, kind="ExternalInput").ap()
    xp_w = nc.dram_tensor("xp_w", [2, DSL, DTR + 2 * NST], F32, kind="ExternalInput").ap()
    dt_w = nc.dram_tensor("dt_w", [2, DTR, DSL], F32, kind="ExternalInput").ap()
    dt_b = nc.dram_tensor("dt_b", [2, DSL], F32, kind="ExternalInput").ap()
    i_alog = nc.dram_tensor("i_alog", [2, DSL, NST], F32, kind="ExternalInput").ap()
    i_d = nc.dram_tensor("i_d", [2, DSL], F32, kind="ExternalInput").ap()
    ow = nc.dram_tensor("ow", [2, DSL, DM], F32, kind="ExternalInput").ap()
    proj_w = nc.dram_tensor("proj_w", [2 * DM, DM], F32, kind="ExternalInput").ap()
    proj_b = nc.dram_tensor("proj_b", [DM], F32, kind="ExternalInput").ap()
    ln_g = nc.dram_tensor("ln_g", [DM], F32, kind="ExternalInput").ap()
    ln_b = nc.dram_tensor("ln_b", [DM], F32, kind="ExternalInput").ap()
    fc1_q = nc.dram_tensor("fc1_q", [DM, FF], I32, kind="ExternalInput").ap()
    fc1_s = nc.dram_tensor("fc1_s", [DM // GRP, FF], F32, kind="ExternalInput").ap()
    fc1_b = nc.dram_tensor("fc1_b", [FF], F32, kind="ExternalInput").ap()
    fc2_q = nc.dram_tensor("fc2_q", [FF, DM], I32, kind="ExternalInput").ap()
    fc2_s = nc.dram_tensor("fc2_s", [FF // GRP, DM], F32, kind="ExternalInput").ap()
    fc2_b = nc.dram_tensor("fc2_b", [DM], F32, kind="ExternalInput").ap()
    o_y = nc.dram_tensor("o_y", [TSL, DM], F32, kind="ExternalOutput").ap()

    s_dt = nc.dram_tensor("s_dt", [2, DSL, L], F32).ap()
    s_xc = nc.dram_tensor("s_xc", [2, DSL, L], F32).ap()
    s_sz = nc.dram_tensor("s_sz", [2, DSL, L], F32).ap()
    s_dbc_in = nc.dram_tensor("s_dbc_in", [2, 96, L], F32).ap()
    s_dbc = nc.dram_tensor("s_dbc", [2, 96, L], F32, addr_space="Shared").ap()
    s_mix = nc.dram_tensor("s_mix", [2, DSL, L], F32).ap()
    s_part = nc.dram_tensor("s_part", [L, DM], F32).ap()
    s_attn = nc.dram_tensor("s_attn", [TSL, DM], F32).ap()
    s_w1 = nc.dram_tensor("s_w1", [DM, FF], F32).ap()
    s_w2 = nc.dram_tensor("s_w2", [FF, DM], F32).ap()
    s_hT = nc.dram_tensor("s_hT", [DM, TSL], F32).ap()
    s_res = nc.dram_tensor("s_res", [TSL, DM], F32).ap()

    with contextlib.ExitStack() as ctx:
        sbt = lambda name, shape, dt=F32: ctx.enter_context(nc.sbuf_tensor(name, shape, dt))
        pst = lambda name, shape: ctx.enter_context(nc.psum_tensor(name, shape, F32))

        AR1 = sbt("AR1", [128, 24576])
        AR2 = sbt("AR2", [128, 8192])
        AR3 = sbt("AR3", [128, 8192])
        AR4 = sbt("AR4", [128, 4096])
        AR5 = sbt("AR5", [128, 2048])
        AR6 = sbt("AR6", [128, 2048])
        AR7 = sbt("AR7", [128, 2048])

        def rep16(sl, T):
            return bass.AP(sl.tensor, sl.offset, [[sl.ap[0][0], 128], [0, NST], [1, T]])

        st = sbt("stats", [128, 32])
        ident = sbt("ident", [128, 128])
        io_i = sbt("io_i", [128, 128], F32)
        io_p = sbt("io_p", [128, 1], F32)
        cst = sbt("cst", [128, 2])
        smallw = sbt("smallw", [128, 32])
        dtw_sb = sbt("dtw_sb", [64, 2 * DSL])
        xpw_sb = sbt("xpw_sb", [128, 2 * NU * 96])
        Atile = sbt("Atile", [128, 2 * NU * NST])
        carry = sbt("carry", [128, 2 * NU * NST])
        pbc = sbt("pbc", [128, NKT])
        f1bc = sbt("f1bc", [128, NOF])
        f2bc = sbt("f2bc", [128, NKT])

        pzb = pst("pzb", [128, 2048])
        pdt = pst("pdt", [128, 512])
        pz = [pzb[:, k * 512:(k + 1) * 512] for k in range(4)]
        ypzb = [pz[2], pz[3]]
        pdb = [pst("pdb0", [96, 512]), pst("pdb1", [96, 512])]

        s = Sched(nc)
        s.make_sems(ctx)
        block = ctx.enter_context(nc.Block())
        bank_free = {}
        buf_free = {}

        def mkcst(e):
            e.memset(cst[:, 0:1], EPS)
            return e.memset(cst[:, 1:2], 1.0)
        m_cst = s.step("vector", mkcst)
        onesrow = AR6[0:1, 1792:1920]
        onesrow32 = AR6[32:33, 1792:1920]
        def mkones(e):
            e.memset(onesrow, 1.0)
            return e.memset(onesrow32, 1.0)
        m_ones = s.step("vector", mkones)
        m_io1 = s.step("gpsimd", lambda e: e.iota(
            io_i[:, :], [[1, 128]], channel_multiplier=0,
            allow_small_or_imprecise_dtypes=True))
        m_io2 = s.step("gpsimd", lambda e: e.iota(
            io_p[:, 0:1], [[0, 1]], channel_multiplier=1,
            allow_small_or_imprecise_dtypes=True))
        m_id = s.step("vector", lambda e: e.tensor_scalar(
            ident[:, :], io_i[:, :], io_p[:, 0:1], None, ALU.is_equal),
            waits=[m_io1, m_io2])

        def ldsmall(e, sem):
            for d in range(2):
                for u in range(NU):
                    i = d * NU + u
                    r0, r1 = u * 128, (u + 1) * 128
                    e.dma_start(out=smallw[:, i * 4:(i + 1) * 4],
                                in_=conv_w[d, r0:r1, :]).then_inc(sem, 16)
                    e.dma_start(out=smallw[:, 16 + i:17 + i],
                                in_=conv_b[d, r0:r1].unsqueeze(1)).then_inc(sem, 16)
                    e.dma_start(out=smallw[:, 20 + i:21 + i],
                                in_=dt_b[d, r0:r1].unsqueeze(1)).then_inc(sem, 16)
                    e.dma_start(out=smallw[:, 24 + i:25 + i],
                                in_=i_d[d, r0:r1].unsqueeze(1)).then_inc(sem, 16)
                    e.dma_start(out=Atile[:, i * NST:(i + 1) * NST],
                                in_=i_alog[d, r0:r1, :]).then_inc(sem, 16)
                    e.dma_start(out=xpw_sb[:, i * 96:(i + 1) * 96],
                                in_=xp_w[d, r0:r1, :]).then_inc(sem, 16)
                e.dma_start(out=dtw_sb[0:64, d * DSL:(d + 1) * DSL],
                            in_=dt_w[d, :, :]).then_inc(sem, 16)
            for o in range(NKT):
                e.dma_start(out=pbc[:, o:o + 1],
                            in_=proj_b[o * 128:(o + 1) * 128].unsqueeze(1)).then_inc(sem, 16)
                e.dma_start(out=f2bc[:, o:o + 1],
                            in_=fc2_b[o * 128:(o + 1) * 128].unsqueeze(1)).then_inc(sem, 16)
            for o in range(NOF):
                e.dma_start(out=f1bc[:, o:o + 1],
                            in_=fc1_b[o * 128:(o + 1) * 128].unsqueeze(1)).then_inc(sem, 16)
        m_smw = s.step("sync", ldsmall, dma=2 * (6 * NU + 1) + 2 * NKT + NOF)

        m_ae = s.step("scalar", lambda e: e.activation(
            carry[:, :], Atile[:, :], AF.Exp), waits=[m_smw])
        m_an = s.step("vector", lambda e: e.tensor_scalar(
            Atile[:, :], carry[:, :], -1.0, None, ALU.mult), waits=[m_ae])
        m_cz = s.step("vector", lambda e: e.memset(carry[:, :], 0.0), waits=[m_an])

        def ldinw(e, sem):
            for d in range(2):
                for kt in range(NKT):
                    e.dma_start(
                        out=AR1[:, kt * 1024 + d * 512: kt * 1024 + d * 512 + 512],
                        in_=inw[d, kt * 128:(kt + 1) * 128, :]).then_inc(sem, 16)
        m_inw = s.step("sync", ldinw, dma=2 * NKT)

        xtm = [AR2[:, 0:1024], AR2[:, 1024:2048],
               AR3[:, 0:1024], AR3[:, 1024:2048]]
        xsq = AR2[:, 2048:3072]
        xnb = [AR2[:, 3072:4096], AR2[:, 4096:5120],
               AR3[:, 2048:3072], AR3[:, 3072:4096]]
        xnTb = [AR1[:, 8192:12288], AR1[:, 12288:16384]]

        def xiring(d, u, j):
            base = (d * NU + u) * 3 + j
            return AR1[:, 16384 + base * 512:16384 + (base + 1) * 512]
        cacc = [AR1[:, 22528:23040], AR1[:, 23040:23552]]
        xcc = AR1[:, 23552:24064]
        szc = AR1[:, 24064:24576]
        dbev = AR2[:, 5120:5632]
        dtT = AR2[:, 5632:6144]
        spe = AR2[:, 6144:6656]
        zpad = AR2[:, 6656:7168]

        m_zr = s.step("vector", lambda e: e.memset(zpad[:, :], 0.0))
        m_zr2 = None
        def zring(e):
            for d in range(2):
                for u in range(NU):
                    e.memset(xiring(d, u, 2), 0.0)
            return e.memset(xiring(1, 1, 2), 0.0)
        m_zr2 = s.step("vector", zring)

        tr_i = 0
        m_xi_ev = {}
        m_conv_done = {}
        m_ln_prev = None
        m_xnT_free = [None, None]
        m_last_st = None
        for c in range(NGC + 1):
            cpar = c % 2
            if c < NGC:
                m_evs_t = []
                for tt in range(4):
                    t_abs = c * 4 + tt
                    par = t_abs % 4
                    sb = par * 8
                    m_dma = s.step("sync", lambda e, sem, t_abs=t_abs, par=par: e.dma_start(
                        out=xtm[par][:, :],
                        in_=x[t_abs * 128:(t_abs + 1) * 128, :]).then_inc(sem, 16),
                        waits=[buf_free.get(("xtm", par))], dma=1)
                    m_s1 = s.step("vector", lambda e, par=par, sb=sb: e.reduce_sum(
                        st[:, sb:sb + 1], xtm[par][:, :], AX.X),
                        waits=[m_dma, buf_free.get(("st", par))])
                    m_s2 = s.step("scalar", lambda e, par=par, sb=sb: e.activation(
                        xsq[:, :], xtm[par][:, :], AF.Square,
                        accum_out=st[:, sb + 1:sb + 2]),
                        waits=[m_dma, buf_free.get(("st", par)), m_ln_prev])
                    m_ln_prev = m_s2
                    m_v1 = s.step("vector", lambda e, sb=sb: e.tensor_scalar(
                        st[:, sb + 2:sb + 4], st[:, sb:sb + 2], 1.0 / DM, None, ALU.mult),
                        waits=[m_s1, m_s2])
                    m_v2 = s.step("vector", lambda e, sb=sb: e.tensor_tensor(
                        st[:, sb + 4:sb + 5], st[:, sb + 2:sb + 3], st[:, sb + 2:sb + 3], ALU.mult), waits=[m_v1])
                    m_vr = s.step("vector", lambda e, sb=sb: e.tensor_tensor(
                        st[:, sb + 5:sb + 6], st[:, sb + 3:sb + 4], st[:, sb + 4:sb + 5], ALU.subtract), waits=[m_v2])
                    m_sq = s.step("scalar", lambda e, sb=sb: e.activation(
                        st[:, sb + 6:sb + 7], st[:, sb + 5:sb + 6], AF.Sqrt, bias=cst[:, 0:1]),
                        waits=[m_vr, m_cst])
                    m_rc = s.step("vector", lambda e, sb=sb: e.reciprocal(
                        st[:, sb + 7:sb + 8], st[:, sb + 6:sb + 7]), waits=[m_sq])
                    m_xn = s.step("vector", lambda e, par=par, sb=sb: e.tensor_scalar(
                        xnb[par][:, :], xtm[par][:, :], st[:, sb + 2:sb + 3], st[:, sb + 7:sb + 8],
                        ALU.subtract, ALU.mult),
                        waits=[m_rc, buf_free.get(("xnb", par))])
                    buf_free[("st", par)] = m_xn
                    buf_free[("xtm", par)] = m_xn
                    last_tr = None
                    for g4 in range(2):
                        bank = pz[tr_i % 2]
                        def tr4(e, par=par, g4=g4, bank=bank):
                            for q in range(4):
                                dsub = g4 * 4 + q
                                inst = e.transpose(
                                    bank[:, q * 128:(q + 1) * 128],
                                    xnb[par][:, dsub * 128:(dsub + 1) * 128],
                                    ident[:, :])
                            return inst
                        m_tr = s.step("tensor", tr4,
                                      waits=[m_xn, m_id, bank_free.get(id(bank))])
                        def ev4(e, tt=tt, g4=g4, bank=bank, cpar=cpar):
                            dst = xnTb[cpar]
                            out = bass.AP(
                                dst.tensor,
                                dst.offset + g4 * 4 * 512 + tt * 128,
                                [[dst.ap[0][0], 128], [512, 4], [1, 128]])
                            return e.copy(out, bank[:, :])
                        m_ev = s.step("scalar", ev4, waits=[m_tr, m_xnT_free[cpar]])
                        bank_free[id(bank)] = m_ev
                        m_evs_t.append(m_ev)
                        last_tr = m_tr
                        tr_i += 1
                    buf_free[("xnb", par)] = last_tr

                m_mm_last = None
                for d in range(2):
                    for ot in range(4):
                        bank = pz[2 + (ot % 2)]
                        is_xi = ot < NU
                        u = ot % NU
                        col0 = d * 512 + (0 if is_xi else 256) + u * 128

                        def xzmm(e, col0=col0, bank=bank, cpar=cpar):
                            for kt in range(NKT):
                                inst = e.matmul(
                                    bank[:, :],
                                    AR1[:, kt * 1024 + col0: kt * 1024 + col0 + 128],
                                    xnTb[cpar][:, kt * 512:(kt + 1) * 512],
                                    start=(kt == 0), stop=(kt == NKT - 1))
                            return inst
                        m_mm = s.step("tensor", xzmm,
                                      waits=[m_evs_t[-1], m_inw, bank_free.get(id(bank))])
                        m_mm_last = m_mm
                        if is_xi:
                            ring = c % 3
                            m_ev = s.step("scalar", lambda e, d=d, u=u, ring=ring, bank=bank: e.copy(
                                xiring(d, u, ring), bank[:, :]),
                                waits=[m_mm, m_conv_done.get((d, u, c - 2))])
                            m_xi_ev[(d, u, c)] = m_ev
                        else:
                            m_ev = s.step("scalar", lambda e, bank=bank: e.activation(
                                szc[:, :], bank[:, :], AF.Silu),
                                waits=[m_mm, buf_free.get("szc")])
                            m_st = s.step("gpsimd", lambda e, sem, d=d, u=u, c=c: e.dma_start(
                                out=s_sz[d, u * 128:(u + 1) * 128, c * CW:(c + 1) * CW],
                                in_=szc[:, :]).then_inc(sem, 16), waits=[m_ev], dma=1)
                            m_last_st = m_st
                            buf_free["szc"] = m_st
                        bank_free[id(bank)] = m_ev
                m_xnT_free[cpar] = m_mm_last

            if c >= 1:
                cc = c - 1
                for d in range(2):
                    for u in range(NU):
                        rm1 = xiring(d, u, (cc - 1) % 3) if cc >= 1 else xiring(d, u, 2)
                        r0 = xiring(d, u, cc % 3)
                        rp1 = (xiring(d, u, (cc + 1) % 3) if cc + 1 < NGC else zpad)
                        wv = [m_xi_ev.get((d, u, cc)), m_xi_ev.get((d, u, cc - 1)),
                              m_xi_ev.get((d, u, cc + 1)), m_zr, m_zr2, m_smw]
                        i = d * NU + u
                        offs = [j - 3 for j in range(4)] if d == 0 else [3 - j for j in range(4)]
                        m_c = None
                        for j in range(4):
                            o = offs[j]
                            wcol = smallw[:, i * 4 + j: i * 4 + j + 1]
                            segs = []
                            if o < 0:
                                segs.append((0, -o, rm1, 512 + o))
                                segs.append((-o, 512 + o, r0, 0))
                            elif o == 0:
                                segs.append((0, 512, r0, 0))
                            else:
                                segs.append((0, 512 - o, r0, o))
                                segs.append((512 - o, o, rp1, 0))
                            for (d0, n, sbuf, s0) in segs:
                                if j == 0:
                                    m_c = s.step("vector", lambda e, d0=d0, n=n, sbuf=sbuf, s0=s0, wcol=wcol: e.tensor_scalar(
                                        cacc[0][:, d0:d0 + n], sbuf[:, s0:s0 + n],
                                        wcol, None, ALU.mult),
                                        waits=wv + [buf_free.get("cacc"), m_c])
                                else:
                                    dst = cacc[j % 2]
                                    srcacc = cacc[(j + 1) % 2]
                                    m_c = s.step("vector", lambda e, d0=d0, n=n, sbuf=sbuf, s0=s0, wcol=wcol, dst=dst, srcacc=srcacc: e.scalar_tensor_tensor(
                                        dst[:, d0:d0 + n], sbuf[:, s0:s0 + n],
                                        wcol, srcacc[:, d0:d0 + n], ALU.mult, ALU.add),
                                        waits=[m_c])
                        m_conv_done[(d, u, cc)] = m_c
                        m_xc = s.step("scalar", lambda e, i=i: e.activation(
                            xcc[:, :], cacc[1][:, :], AF.Silu,
                            bias=smallw[:, 16 + i:17 + i]),
                            waits=[m_c, buf_free.get("xcc")])
                        buf_free["cacc"] = m_xc
                        m_stxc = s.step("gpsimd", lambda e, sem, d=d, u=u, cc=cc: e.dma_start(
                            out=s_xc[d, u * 128:(u + 1) * 128, cc * CW:(cc + 1) * CW],
                            in_=xcc[:, :]).then_inc(sem, 16), waits=[m_xc], dma=1)
                        m_last_st = m_stxc
                        bankd = pdb[d]
                        m_db = s.step("tensor", lambda e, u=u, i=i, bankd=bankd: e.matmul(
                            bankd[:, :], xpw_sb[:, i * 96:(i + 1) * 96], xcc[:, :],
                            start=(u == 0), stop=(u == NU - 1)),
                            waits=[m_xc, m_smw, bank_free.get(id(bankd))])
                        buf_free["xcc"] = m_db
                        if u == NU - 1:
                            m_dbe = s.step("scalar", lambda e, bankd=bankd: e.copy(
                                dbev[0:96, 0:512], bankd[:, :]),
                                waits=[m_db, buf_free.get("dbev")])
                            bank_free[id(bankd)] = m_dbe
                            m_std = s.step("gpsimd", lambda e, sem, d=d, cc=cc: e.dma_start(
                                out=s_dbc_in[d, :, cc * CW:(cc + 1) * CW],
                                in_=dbev[0:96, 0:512]).then_inc(sem, 16),
                                waits=[m_dbe], dma=1)
                            buf_free["dbev"] = m_std
                            m_last_st = m_std

        m_ar_d = []
        for dd in range(2):
            m_ar_d.append(s.step("gpsimd", lambda e, dd=dd: e.collective_compute(
                "AllReduce", ALU.add, replica_groups=[list(range(NCORES))],
                ins=[s_dbc_in[dd]], outs=[s_dbc[dd]]), waits=[m_last_st]))


        barB = _barrier(s)
        dAb = [AR1[:, 0:4096], AR1[:, 12288:16384]]
        ubb = [AR1[:, 4096:8192], AR1[:, 16384:20480]]
        hbb = AR1[:, 8192:12288]
        BCrep = [AR2[:, 0:8192], AR3[:, 0:8192]]
        bcB = AR1[0:1, 20480:24576]
        bcC = AR1[32:33, 20480:24576]
        dtstg = [AR6[0:64, 0:256], AR6[0:64, 256:512]]
        dtt = [AR6[:, 512:1024], AR6[:, 1024:1536]]
        dt_c = [[AR4[:, (p * 2 + u) * T:(p * 2 + u + 1) * T]
                 for u in range(NU)] for p in range(2)]
        xc_c = [[AR4[:, 1024 + (p * 2 + u) * T: 1024 + (p * 2 + u + 1) * T]
                 for u in range(NU)] for p in range(2)]
        sz_c = [[AR4[:, 2048 + (p * 2 + u) * T: 2048 + (p * 2 + u + 1) * T]
                 for u in range(NU)] for p in range(2)]
        vb = AR4[:, 3072:3072 + T]
        yb = AR4[:, 3584:3584 + T]
        t1b = AR4[:, 3840:3840 + T]
        mixb = [AR5[:, 0:T], AR5[:, T:2 * T]]

        m_dA_free = [None, None]
        m_u_free = [None, None]
        m_h_free = None
        m_mix_last = None
        pend_st = []
        pend_t1 = []

        def emit_t1(ent):
            qi, qu, qpar, qycol, qmpar, qd, qt0, qymm = ent
            m_t1 = s.step("vector", lambda e, i=qi, u=qu, par=qpar, ycol=qycol: e.scalar_tensor_tensor(
                t1b[:, :], xc_c[par][u][:, :], smallw[:, 24 + i:25 + i],
                ypzb[ycol][:, 0:256],
                ALU.mult, ALU.add), waits=[qymm, m_smw])
            buf_free[("ypz", qycol)] = m_t1
            m_mx = s.step("vector", lambda e, mpar=qmpar, par=qpar, u=qu: e.tensor_tensor(
                mixb[mpar][:, :], t1b[:, :], sz_c[par][u][:, :], ALU.mult),
                waits=[m_t1, buf_free.get(("mix", qmpar))])
            pend_st.append((qd, qu, qt0, qmpar, m_mx))
            if qu == NU - 1:
                buf_free[("chunk", qpar)] = m_mx

        chunk_list = []
        for d in range(2):
            for ci in range(NCH):
                chunk_list.append(
                    (d, ci, ci * T if d == 0 else L - (ci + 1) * T, ci % 2))

        def emit_bcast(k):
            d, ci, t0, par = chunk_list[k]
            def bcs(e, sem, d=d, t0=t0, par=par):
                e.dma_start(out=bcB[0:1, :],
                            in_=s_dbc[d, 64:80, t0:t0 + T]).then_inc(sem, 16)
                e.dma_start(out=bcC[0:1, :],
                            in_=s_dbc[d, 80:96, t0:t0 + T]).then_inc(sem, 16)
                e.dma_start(out=dtstg[par][0:64, :],
                            in_=s_dbc[d, 0:64, t0:t0 + T]).then_inc(sem, 16)
            m_bs = s.step("sync", bcs,
                          waits=[buf_free.get("bcstg"), m_ar_d[d],
                                 buf_free.get(("dtstg", par))] + barB, dma=3)
            def dtmm(e, d=d, par=par):
                for u in range(NU):
                    inst = e.matmul(
                        pdt[:, u * 256:(u + 1) * 256],
                        dtw_sb[0:64, d * DSL + u * 128: d * DSL + (u + 1) * 128],
                        dtstg[par][0:64, :], start=True, stop=True)
                return inst
            m_dtm = s.step("tensor", dtmm,
                           waits=[m_bs, m_smw, buf_free.get("pdt")]
                           + (barB if k == 0 else []))
            buf_free[("dtstg", par)] = m_dtm
            def dtexp(e, d=d, par=par):
                for u in range(NU):
                    i2 = d * NU + u
                    inst = e.activation(
                        dtt[par][:, u * 256:(u + 1) * 256],
                        pdt[:, u * 256:(u + 1) * 256], AF.Exp,
                        bias=smallw[:, 20 + i2:21 + i2])
                return inst
            m_dte = s.step("scalar", dtexp,
                           waits=[m_dtm, buf_free.get(("dtt", par))])
            buf_free["pdt"] = m_dte
            def dtln(e, par=par):
                for u in range(NU):
                    inst = e.activation(
                        dt_c[par][u][:, :], dtt[par][:, u * 256:(u + 1) * 256],
                        AF.Ln, bias=cst[:, 1:2])
                return inst
            m_dtc = s.step("scalar", dtln,
                           waits=[m_dte, buf_free.get(("chunk", par)), m_cst])
            buf_free[("dtt", par)] = m_dtc
            evs = []
            m_mm_bc = None
            for g in range(16):
                bank = pz[g % 2]
                def bmm(e, g=g, bank=bank):
                    c0 = g * 512
                    if c0 < 4096:
                        src, orow = bcB[0:1, c0:c0 + 512], onesrow
                    else:
                        src = bcC[0:1, c0 - 4096:c0 - 3584]
                        orow = onesrow32
                    return e.matmul(bank, orow, src, start=True, stop=True)
                m_mm_bc = s.step("tensor", bmm,
                                 waits=[m_bs, m_ones,
                                        bank_free.get(("bc", g % 2))]
                                 + (barB if k == 0 and g == 0 else []))
                m_e = s.step("scalar", lambda e, g=g, par=par, bank=bank: e.activation(
                    BCrep[par][:, g * 512:(g + 1) * 512], bank,
                    AF.Identity),
                    waits=[m_mm_bc]
                    + ([buf_free.get(("BC", par))] if g == 0 else []))
                bank_free[("bc", g % 2)] = m_e
                evs.append(m_e)
            buf_free["bcstg"] = m_mm_bc
            return evs[7], evs[15], m_dtc

        nxt_ev = emit_bcast(0)
        it = 0
        for k, (d, ci, t0, par) in enumerate(chunk_list):
                m_evB, m_evC, m_dtc = nxt_ev

                def loadc(e, sem, d=d, t0=t0, par=par):
                    for u in range(NU):
                        r0, r1 = u * 128, (u + 1) * 128
                        e.dma_start(out=xc_c[par][u][:, :],
                                    in_=s_xc[d, r0:r1, t0:t0 + T]).then_inc(sem, 16)
                        e.dma_start(out=sz_c[par][u][:, :],
                                    in_=s_sz[d, r0:r1, t0:t0 + T]).then_inc(sem, 16)
                m_ld = s.step("sync", loadc,
                              waits=[buf_free.get(("chunk", par))] + barB,
                              dma=2 * NU)

                m_dAs = []
                for u in range(NU):
                    i = d * NU + u
                    p2 = u
                    dA = dAb[p2]
                    def dAacts(e, i=i, u=u, par=par, dA=dA):
                        for n in range(NST):
                            inst = e.activation(
                                dA[:, n * T:(n + 1) * T], dt_c[par][u][:, :], AF.Exp,
                                scale=Atile[:, i * NST + n: i * NST + n + 1])
                        return inst
                    m_dAs.append(s.step("scalar", dAacts,
                                  waits=[m_ld, m_dtc, m_dA_free[p2], m_an] + barB))
                if k + 1 < len(chunk_list):
                    nxt_ev = emit_bcast(k + 1)

                for u in range(NU):
                    i = d * NU + u
                    p2 = u
                    dA, ubuf, hbuf = dAb[p2], ubb[p2], hbb
                    m_dA = m_dAs[u]
                    m_v = s.step("vector", lambda e, u=u, par=par: e.tensor_tensor(
                        vb[:, :], dt_c[par][u][:, :], xc_c[par][u][:, :], ALU.mult),
                        waits=[m_ld, m_dtc, buf_free.get("vb")] + barB)
                    m_u = s.step("vector", lambda e, par=par, ubuf=ubuf: e.tensor_tensor(
                        ubuf[:].rearrange("p (n t) -> p n t", n=NST), rep16(vb, T),
                        BCrep[par][:, 0:4096].rearrange("p (n t) -> p n t", n=NST),
                        ALU.mult),
                        waits=[m_v, m_evB, m_u_free[p2]])
                    buf_free["vb"] = m_u
                    if pend_st:
                        pd, pu, pt0, pmpar, pmx = pend_st.pop(0)
                        m_st = s.step("sync", lambda e, sem, d=pd, u=pu, t0=pt0,
                                      mpar=pmpar: e.dma_start(
                            out=s_mix[d, u * 128:(u + 1) * 128, t0:t0 + T],
                            in_=mixb[mpar][:, :]).then_inc(sem, 16),
                            waits=[pmx], dma=1)
                        buf_free[("mix", pmpar)] = m_st
                        m_mix_last = m_st
                    def scans(e, d=d, i=i, dA=dA, ubuf=ubuf, hbuf=hbuf):
                        for n in range(NST):
                            hv = hbuf[:, n * T:(n + 1) * T]
                            av = dA[:, n * T:(n + 1) * T]
                            uv = ubuf[:, n * T:(n + 1) * T]
                            if d == 1:
                                hv = hv[:, ::-1]; av = av[:, ::-1]; uv = uv[:, ::-1]
                            inst = e.tensor_tensor_scan(
                                hv, av, uv, carry[:, i * NST + n: i * NST + n + 1],
                                ALU.mult, ALU.add)
                        return inst
                    m_sc = s.step("vector", scans,
                                  waits=[m_dA, m_u, m_h_free, m_cz] + barB)
                    m_u_free[p2] = m_sc
                    ccol = T - 1 if d == 0 else 0
                    m_cy = s.step("vector", lambda e, i=i, ccol=ccol, hbuf=hbuf: e.tensor_copy(
                        carry[:, i * NST:(i + 1) * NST],
                        hbuf[:].rearrange("p (n t) -> p n t", n=NST)[:, :, ccol]),
                        waits=[m_sc])
                    m_hc = s.step("vector", lambda e, par=par, dA=dA, hbuf=hbuf: e.tensor_tensor(
                        dA[:, :], hbuf[:, :], BCrep[par][:, 4096:8192], ALU.mult),
                        waits=[m_sc, m_evC, m_cy])
                    m_h_free = m_hc
                    buf_free[("BC", par)] = m_hc
                    ycol = it % 2
                    def ymm(e, dA=dA, ycol=ycol):
                        for n in range(NST):
                            inst = e.matmul(ypzb[ycol][:, 0:256], ident[:, :],
                                            dA[:, n * T:(n + 1) * T],
                                            start=(n == 0), stop=(n == NST - 1))
                        return inst
                    m_ymm = s.step("tensor", ymm,
                                   waits=[m_hc, m_id, buf_free.get(("ypz", it % 2))])
                    m_h_free = m_hc
                    m_dA_free[p2] = m_ymm
                    if pend_t1:
                        emit_t1(pend_t1.pop(0))
                    pend_t1.append((i, u, par, ycol, it % 2, d, t0, m_ymm))
                    it += 1
        while pend_t1:
            emit_t1(pend_t1.pop(0))
        while pend_st:
            pd, pu, pt0, pmpar, pmx = pend_st.pop(0)
            m_st = s.step("sync", lambda e, sem, d=pd, u=pu, t0=pt0,
                          mpar=pmpar: e.dma_start(
                out=s_mix[d, u * 128:(u + 1) * 128, t0:t0 + T],
                in_=mixb[mpar][:, :]).then_inc(sem, 16),
                waits=[pmx], dma=1)
            buf_free[("mix", pmpar)] = m_st
            m_mix_last = m_st

        barC0 = _barrier(s)
        Msb = AR4[:, 0:4096]
        owT = AR1[:, 10240:12288]
        owstg = AR1[:, 6144:8192]
        pjstg = AR1[:, 8192:10240]
        mixc = AR5[:, 0:2048]
        attc = AR1[:, 0:4096]
        ptst = [AR1[:, 4096:5120], AR1[:, 5120:6144]]

        m_M_last = None
        for d in range(2):
            def ldow(e, sem, d=d):
                for u in range(NU):
                    e.dma_start(out=owstg[:, u * 1024:(u + 1) * 1024],
                                in_=ow[d, u * 128:(u + 1) * 128, :]).then_inc(sem, 16)
            m_ow = s.step("sync", ldow, waits=[buf_free.get("owstg")] + barC0, dma=NU)
            m_trs = None
            for u in range(NU):
                for mt in range(NKT):
                    bank = pz[(u * NKT + mt) % 2]
                    m_tr = s.step("tensor", lambda e, u=u, mt=mt, bank=bank: e.transpose(
                        bank[:, 0:128],
                        owstg[:, u * 1024 + mt * 128: u * 1024 + (mt + 1) * 128],
                        ident[:, :]),
                        waits=[m_ow, m_id, bank_free.get(id(bank))] + barC0)
                    m_ev = s.step("scalar", lambda e, u=u, mt=mt, bank=bank: e.copy(
                        owT[:, mt * 256 + u * 128: mt * 256 + (u + 1) * 128],
                        bank[:, 0:128]),
                        waits=[m_tr, buf_free.get("owT")] + barC0)
                    bank_free[id(bank)] = m_ev
                    m_trs = m_ev
            buf_free["owstg"] = m_trs
            for oc in range(4):
                def ldpj(e, sem, d=d, oc=oc):
                    for mt in range(NKT):
                        e.dma_start(
                            out=pjstg[:, mt * 256:(mt + 1) * 256],
                            in_=proj_w[d * 1024 + mt * 128: d * 1024 + (mt + 1) * 128,
                                       oc * 256:(oc + 1) * 256]).then_inc(sem, 16)
                m_pj = s.step("sync", ldpj, waits=[buf_free.get("pjstg")] + barC0, dma=NKT)
                m_mm_l = None
                for u in range(NU):
                    bank = pz[2 + u % 2]
                    def mmM(e, u=u, bank=bank):
                        for mt in range(NKT):
                            inst = e.matmul(
                                bank[:, 0:256],
                                owT[:, mt * 256 + u * 128: mt * 256 + (u + 1) * 128],
                                pjstg[:, mt * 256:(mt + 1) * 256],
                                start=(mt == 0), stop=(mt == NKT - 1))
                        return inst
                    m_m = s.step("tensor", mmM,
                                 waits=[m_pj, m_trs, bank_free.get(id(bank))])
                    m_mm_l = m_m
                    m_e = s.step("scalar", lambda e, d=d, u=u, oc=oc, bank=bank: e.copy(
                        Msb[:, (d * NU + u) * 1024 + oc * 256:
                               (d * NU + u) * 1024 + (oc + 1) * 256],
                        bank[:, 0:256]), waits=[m_m, buf_free.get("Msb")] + barC0)
                    bank_free[id(bank)] = m_e
                    m_M_last = m_e
                buf_free["pjstg"] = m_mm_l

        qstg = [AR2[:, 0:4096], AR3[:, 0:4096]]
        screp = [AR2[:, 4096:8192], AR3[:, 4096:8192]]
        wdq = AR1[:, 12288:16384]
        dq_units = []
        for (qsrc, ssrc, wdst, nkt, ncols) in (
                (fc1_q, fc1_s, s_w1, NKT, FF), (fc2_q, fc2_s, s_w2, NOF, DM)):
            for kt in range(nkt):
                dq_units.append((qsrc, ssrc, wdst, kt, ncols))
        dq_dma = {}

        def emit_dq_dma(j):
            qsrc, ssrc, wdst, kt, ncols = dq_units[j]
            par = j % 2
            m_q = s.step("gpsimd", lambda e, sem, qsrc=qsrc, kt=kt, par=par, ncols=ncols: e.dma_start(
                out=qstg[par][:, 0:ncols],
                in_=qsrc[kt * 128:(kt + 1) * 128, :]).then_inc(sem, 16),
                waits=[buf_free.get(("qstg", par))] + barC0, dma=1)
            m_sp = s.step("scalar", lambda e, sem, ssrc=ssrc, kt=kt, par=par, ncols=ncols: e.dma_start(
                out=screp[par][0:1, 0:ncols],
                in_=ssrc[kt:kt + 1, :]).then_inc(sem, 16),
                waits=[buf_free.get(("screp", par))] + barC0, dma=1)
            for k in [1, 2, 4, 8, 16, 32, 64]:
                m_sp = s.step("scalar", lambda e, sem, k=k, par=par, ncols=ncols: e.dma_start(
                    out=screp[par][k:2 * k, 0:ncols],
                    in_=screp[par][0:k, 0:ncols]).then_inc(sem, 16), dma=1)
            dq_dma[j] = (m_q, m_sp)

        def emit_dq_comp(j):
            qsrc, ssrc, wdst, kt, ncols = dq_units[j]
            par = j % 2
            m_q, m_sp = dq_dma[j]
            m_dq = s.step("vector", lambda e, par=par, ncols=ncols: e.scalar_tensor_tensor(
                wdq[:, 0:ncols], qstg[par][:, 0:ncols], 128.0,
                screp[par][:, 0:ncols], ALU.subtract, ALU.mult),
                waits=[m_q, m_sp, buf_free.get("wdq")] + barC0)
            buf_free[("qstg", par)] = m_dq
            buf_free[("screp", par)] = m_dq
            m_sw = s.step("scalar", lambda e, sem, wdst=wdst, kt=kt, ncols=ncols: e.dma_start(
                out=wdst[kt * 128:(kt + 1) * 128, :],
                in_=wdq[:, 0:ncols]).then_inc(sem, 16), waits=[m_dq], dma=1)
            buf_free["wdq"] = m_sw
            return m_sw

        emit_dq_dma(0)
        emit_dq_dma(1)
        dq_j = 0
        m_w_done = None

        m_pt_last = None
        for g in range(NGC):
            t0 = g * CW
            def ldmix(e, sem, t0=t0):
                for d in range(2):
                    for u in range(NU):
                        e.dma_start(
                            out=mixc[:, (d * NU + u) * 512:(d * NU + u + 1) * 512],
                            in_=s_mix[d, u * 128:(u + 1) * 128, t0:t0 + CW]).then_inc(sem, 16)
            m_mxl = s.step("sync", ldmix,
                           waits=[buf_free.get("mixc"), m_mix_last, m_M_last] + barC0,
                           dma=2 * NU)
            m_oev = []
            for o in range(NKT):
                bank = pz[o % 2]
                def mmP(e, o=o, bank=bank):
                    for i in range(2 * NU):
                        inst = e.matmul(
                            bank[:, :],
                            Msb[:, i * 1024 + o * 128: i * 1024 + (o + 1) * 128],
                            mixc[:, i * 512:(i + 1) * 512],
                            start=(i == 0), stop=(i == 2 * NU - 1))
                    return inst
                m_m = s.step("tensor", mmP,
                             waits=[m_mxl, m_M_last, bank_free.get(id(bank))])
                m_e = s.step("vector", lambda e, o=o, bank=bank: e.tensor_copy(
                    attc[:, o * 512:(o + 1) * 512], bank[:, :]),
                    waits=[m_m, buf_free.get("attc")])
                bank_free[id(bank)] = m_e
                m_oev.append(m_e)
            buf_free["mixc"] = m_m
            m_tr_l = None
            for ts in range(4):
                tpar = ts % 2
                ptb = ptst[tpar]
                m_ads = []
                for o in range(NKT):
                    bank = pz[2 + (ts * NKT + o) % 2]
                    m_tr = s.step("tensor", lambda e, o=o, ts=ts, bank=bank: e.transpose(
                        bank[:, 0:128],
                        attc[:, o * 512 + ts * 128: o * 512 + (ts + 1) * 128],
                        ident[:, :]),
                        waits=m_oev + [m_id, bank_free.get(id(bank))])
                    m_tr_l = m_tr
                    m_cp = s.step("vector", lambda e, o=o, ptb=ptb, bank=bank: e.tensor_copy(
                        ptb[:, o * 128:(o + 1) * 128], bank[:, 0:128]),
                        waits=[m_tr, buf_free.get(("ptb", tpar))] + barC0)
                    bank_free[id(bank)] = m_cp
                    m_ads.append(m_cp)
                m_stp = s.step("gpsimd", lambda e, sem, t0=t0, ts=ts, ptb=ptb: e.dma_start(
                    out=s_part[t0 + ts * 128: t0 + (ts + 1) * 128, :],
                    in_=ptb[:, :]).then_inc(sem, 16), waits=m_ads, dma=1)
                buf_free[("ptb", tpar)] = m_stp
                m_pt_last = m_stp
            buf_free["attc"] = m_tr_l
            for _ in range(2):
                if dq_j < len(dq_units):
                    m_w_done = emit_dq_comp(dq_j)
                    if dq_j + 2 < len(dq_units):
                        emit_dq_dma(dq_j + 2)
                    dq_j += 1

        barC = _barrier(s)
        m_rs = s.step("gpsimd", lambda e: e.collective_compute(
            "ReduceScatter", ALU.add, replica_groups=[list(range(NCORES))],
            ins=[s_part[:]], outs=[s_attn[:]]), waits=[m_pt_last])

        fxt = [AR1[:, 0:1024], AR1[:, 1024:2048]]
        fxsq = AR1[:, 2048:3072]
        fxn2 = AR1[:, 3072:4096]
        fhtm = [AR6[:, 0:1024], AR6[:, 1024:2048]]
        fhstg = AR1[:, 4096:5120]
        grep = AR1[:, 5120:6144]
        brep = AR1[:, 6144:7168]
        pbrep = AR7[:, 0:1024]

        def ldgb(e, sem):
            e.dma_start(out=grep[0:1, :], in_=ln_g.unsqueeze(0)).then_inc(sem, 16)
            e.dma_start(out=brep[0:1, :], in_=ln_b.unsqueeze(0)).then_inc(sem, 16)
            e.dma_start(out=pbrep[0:1, :], in_=proj_b.unsqueeze(0)).then_inc(sem, 16)
        m_db2 = s.step("sync", ldgb, waits=barC, dma=3)
        for k in [1, 2, 4, 8, 16, 32, 64]:
            def dbl(e, sem, k=k):
                e.dma_start(out=grep[k:2 * k, :], in_=grep[0:k, :]).then_inc(sem, 16)
                e.dma_start(out=brep[k:2 * k, :], in_=brep[0:k, :]).then_inc(sem, 16)
                e.dma_start(out=pbrep[k:2 * k, :], in_=pbrep[0:k, :]).then_inc(sem, 16)
            m_db2 = s.step("sync", dbl, waits=[m_db2], dma=3)

        tr2 = 0
        m_sth = None
        m_hT_chunk = {}
        for t in range(TSL // 128):
            par = t % 2
            m_dma = s.step("sync", lambda e, sem, t=t, par=par: e.dma_start(
                out=fxt[par][:, :], in_=xs[t * 128:(t + 1) * 128, :]).then_inc(sem, 16),
                waits=[buf_free.get(("fxt", par))] + barC, dma=1)
            m_s1 = s.step("vector", lambda e, par=par: e.reduce_sum(
                st[:, 0:1], fxt[par][:, :], AX.X), waits=[m_dma] + barC)
            m_s2 = s.step("scalar", lambda e, par=par: e.activation(
                fxsq[:, :], fxt[par][:, :], AF.Square, accum_out=st[:, 1:2]),
                waits=[m_dma] + barC)
            m_v1 = s.step("vector", lambda e: e.tensor_scalar(
                st[:, 2:3], st[:, 0:1], 1.0 / DM, None, ALU.mult), waits=[m_s1, m_s2])
            m_v1b = s.step("vector", lambda e: e.tensor_scalar(
                st[:, 3:4], st[:, 1:2], 1.0 / DM, None, ALU.mult), waits=[m_v1])
            m_v2 = s.step("vector", lambda e: e.tensor_tensor(
                st[:, 4:5], st[:, 2:3], st[:, 2:3], ALU.mult), waits=[m_v1b])
            m_vr = s.step("vector", lambda e: e.tensor_tensor(
                st[:, 5:6], st[:, 3:4], st[:, 4:5], ALU.subtract), waits=[m_v2])
            m_sq = s.step("scalar", lambda e: e.activation(
                st[:, 6:7], st[:, 5:6], AF.Sqrt, bias=cst[:, 0:1]), waits=[m_vr])
            m_rc = s.step("vector", lambda e: e.reciprocal(
                st[:, 7:8], st[:, 6:7]), waits=[m_sq])
            m_xn = s.step("vector", lambda e, par=par: e.tensor_scalar(
                fxsq[:, :], fxt[par][:, :], st[:, 2:3], st[:, 7:8],
                ALU.subtract, ALU.mult), waits=[m_rc, m_s2])
            m_rt = s.step("vector", lambda e, par=par: e.tensor_tensor(
                fhtm[par][:, :], fxt[par][:, :], fxsq[:, :], ALU.add),
                waits=[m_xn, buf_free.get(("fhtm", par))])
            buf_free[("fxt", par)] = m_rt
            m_str = s.step("scalar", lambda e, sem, t=t, par=par: e.dma_start(
                out=s_res[t * 128:(t + 1) * 128, :],
                in_=fhtm[par][:, :]).then_inc(sem, 16), waits=[m_rt], dma=1)
            m_t1s = s.step("vector", lambda e: e.reduce_sum(
                st[:, 8:9], fxsq[:, :], AX.X), waits=[m_xn])
            m_t2s = s.step("scalar", lambda e: e.activation(
                fxn2[:, :], fxsq[:, :], AF.Square, accum_out=st[:, 9:10]), waits=[m_xn])
            m_u1 = s.step("vector", lambda e: e.tensor_scalar(
                st[:, 10:11], st[:, 8:9], 1.0 / DM, None, ALU.mult), waits=[m_t1s, m_t2s])
            m_u1b = s.step("vector", lambda e: e.tensor_scalar(
                st[:, 11:12], st[:, 9:10], 1.0 / DM, None, ALU.mult), waits=[m_u1])
            m_u2 = s.step("vector", lambda e: e.tensor_tensor(
                st[:, 12:13], st[:, 10:11], st[:, 10:11], ALU.mult), waits=[m_u1b])
            m_ur = s.step("vector", lambda e: e.tensor_tensor(
                st[:, 13:14], st[:, 11:12], st[:, 12:13], ALU.subtract), waits=[m_u2])
            m_sq2 = s.step("scalar", lambda e: e.activation(
                st[:, 14:15], st[:, 13:14], AF.Sqrt, bias=cst[:, 0:1]), waits=[m_ur])
            m_rc2 = s.step("vector", lambda e: e.reciprocal(
                st[:, 15:16], st[:, 14:15]), waits=[m_sq2])
            m_xn2 = s.step("vector", lambda e: e.tensor_scalar(
                fxn2[:, :], fxsq[:, :], st[:, 10:11], st[:, 15:16],
                ALU.subtract, ALU.mult), waits=[m_rc2])
            m_h1 = s.step("vector", lambda e: e.tensor_tensor(
                fxsq[:, :], fxn2[:, :], grep[:, :], ALU.mult), waits=[m_xn2, m_db2])
            m_h2 = s.step("vector", lambda e, par=par: e.tensor_tensor(
                fhtm[par][:, :], fxsq[:, :], brep[:, :], ALU.add),
                waits=[m_h1, m_str])
            m_evs = []
            for dsub in range(NKT):
                bank = pz[tr2 % 2]
                m_tr = s.step("tensor", lambda e, par=par, dsub=dsub, bank=bank: e.transpose(
                    bank[:, 0:128], fhtm[par][:, dsub * 128:(dsub + 1) * 128],
                    ident[:, :]), waits=[m_h2, bank_free.get(id(bank))] + barC)
                m_ev = s.step("scalar", lambda e, dsub=dsub, bank=bank: e.copy(
                    fhstg[:, dsub * 128:(dsub + 1) * 128], bank[:, 0:128]),
                    waits=[m_tr, buf_free.get("fhstg")])
                bank_free[id(bank)] = m_ev
                m_evs.append(m_ev)
                tr2 += 1
            buf_free[("fhtm", par)] = m_evs[-1]

            def sthT(e, sem, t=t):
                for dsub in range(NKT):
                    e.dma_start(out=s_hT[dsub * 128:(dsub + 1) * 128, t * 128:(t + 1) * 128],
                                in_=fhstg[:, dsub * 128:(dsub + 1) * 128]).then_inc(sem, 16)
            m_sth = s.step("scalar", sthT, waits=m_evs, dma=NKT)
            buf_free["fhstg"] = m_sth
            m_hT_chunk[t // 4] = m_sth
        m_hT_done = m_sth

        syb = AR1[:, 0:NOF * 512]
        hTc = [AR1[:, 16384:20480], AR1[:, 20480:24576]]
        w1str = [AR2[:, 0:2048], AR2[:, 2048:4096]]
        w2str = [AR2[:, 4096:8192], AR3[:, 4096:8192]]
        ffnc = AR3[:, 0:4096]
        fatt = AR4[:, 0:1024]
        fres = AR4[:, 1024:2048]
        fout = [AR4[:, 2048:3072], AR4[:, 3072:4096]]

        for g in range(TSL // 512):
            t0 = g * 512
            par = g % 2
            def ldh(e, sem, t0=t0, par=par):
                for kt in range(NKT):
                    e.dma_start(out=hTc[par][:, kt * 512:(kt + 1) * 512],
                                in_=s_hT[kt * 128:(kt + 1) * 128, t0:t0 + 512]).then_inc(sem, 16)
            m_h = s.step("sync", ldh,
                         waits=[buf_free.get(("hTc", par)), m_hT_chunk[g]], dma=NKT)
            m_sy = []
            m_m1_l = None
            for o in range(NOF):
                wpar = o % 2
                def ldw1(e, sem, o=o, wpar=wpar):
                    for kt in range(NKT):
                        e.dma_start(out=w1str[wpar][:, kt * 128:(kt + 1) * 128],
                                    in_=s_w1[kt * 128:(kt + 1) * 128,
                                             o * 128:(o + 1) * 128]).then_inc(sem, 16)
                m_w = s.step("sync", ldw1,
                             waits=[buf_free.get(("w1str", wpar)), m_w_done], dma=NKT)
                bank = pz[o % 2]
                def mmf1(e, o=o, wpar=wpar, par=par, bank=bank):
                    for kt in range(NKT):
                        inst = e.matmul(bank[:, :], w1str[wpar][:, kt * 128:(kt + 1) * 128],
                                        hTc[par][:, kt * 512:(kt + 1) * 512],
                                        start=(kt == 0), stop=(kt == NKT - 1))
                    return inst
                m_m = s.step("tensor", mmf1,
                             waits=[m_w, m_h, bank_free.get(id(bank))] + barC)
                m_m1_l = m_m
                buf_free[("w1str", wpar)] = m_m
                m_e = s.step("scalar", lambda e, o=o, bank=bank: e.activation(
                    syb[:, o * 512:(o + 1) * 512], bank[:, :], AF.Silu,
                    bias=f1bc[:, o:o + 1]), waits=[m_m, buf_free.get("syb")])
                bank_free[id(bank)] = m_e
                m_sy.append(m_e)
            buf_free[("hTc", par)] = m_m1_l
            m_f2 = []
            m_m2_l = None
            for o in range(NKT):
                wpar = o % 2
                def ldw2(e, sem, o=o, wpar=wpar):
                    for kt in range(NOF):
                        e.dma_start(out=w2str[wpar][:, kt * 128:(kt + 1) * 128],
                                    in_=s_w2[kt * 128:(kt + 1) * 128,
                                             o * 128:(o + 1) * 128]).then_inc(sem, 16)
                m_w = s.step("sync", ldw2,
                             waits=[buf_free.get(("w2str", wpar)), m_w_done], dma=NOF)
                bank = pz[2 + o % 2]
                def mmf2(e, o=o, wpar=wpar, bank=bank):
                    for kt in range(NOF):
                        inst = e.matmul(bank[:, :], w2str[wpar][:, kt * 128:(kt + 1) * 128],
                                        syb[:, kt * 512:(kt + 1) * 512],
                                        start=(kt == 0), stop=(kt == NOF - 1))
                    return inst
                m_m = s.step("tensor", mmf2,
                             waits=[m_w, bank_free.get(id(bank))] + m_sy)
                m_m2_l = m_m
                buf_free[("w2str", wpar)] = m_m
                m_e = s.step("vector", lambda e, o=o, bank=bank: e.tensor_scalar(
                    ffnc[:, o * 512:(o + 1) * 512], bank[:, :],
                    f2bc[:, o:o + 1], 0.5, ALU.add, ALU.mult),
                    waits=[m_m, buf_free.get("ffnc")] + barC)
                bank_free[id(bank)] = m_e
                m_f2.append(m_e)
            buf_free["syb"] = m_m2_l
            m_tr_l2 = None
            for ts in range(4):
                tpar = ts % 2
                m_lat = s.step("sync", lambda e, sem, t0=t0, ts=ts: e.dma_start(
                    out=fatt[:, :],
                    in_=s_attn[t0 + ts * 128: t0 + (ts + 1) * 128, :]).then_inc(sem, 16),
                    waits=[buf_free.get("fatt"), m_rs], dma=1)
                m_lre = s.step("sync", lambda e, sem, t0=t0, ts=ts: e.dma_start(
                    out=fres[:, :],
                    in_=s_res[t0 + ts * 128: t0 + (ts + 1) * 128, :]).then_inc(sem, 16),
                    waits=[buf_free.get("fres")], dma=1)
                m_ads = []
                for o in range(NKT):
                    bank = pz[(ts * NKT + o) % 2]
                    m_tr = s.step("tensor", lambda e, o=o, ts=ts, bank=bank: e.transpose(
                        bank[:, 0:128],
                        ffnc[:, o * 512 + ts * 128: o * 512 + (ts + 1) * 128],
                        ident[:, :]),
                        waits=m_f2 + [bank_free.get(id(bank))])
                    m_tr_l2 = m_tr
                    m_a1 = s.step("vector", lambda e, o=o, tpar=tpar, bank=bank: e.tensor_tensor(
                        fout[tpar][:, o * 128:(o + 1) * 128], bank[:, 0:128],
                        fatt[:, o * 128:(o + 1) * 128], ALU.add),
                        waits=[m_tr, m_lat, buf_free.get(("fout", tpar))])
                    bank_free[id(bank)] = m_a1
                    m_ads.append(m_a1)
                m_a2 = s.step("vector", lambda e, tpar=tpar: e.tensor_tensor(
                    fout[tpar][:, :], fout[tpar][:, :], fres[:, :], ALU.add),
                    waits=m_ads + [m_lre])
                m_a3 = s.step("vector", lambda e, tpar=tpar: e.tensor_tensor(
                    fout[tpar][:, :], fout[tpar][:, :], pbrep[:, :], ALU.add),
                    waits=[m_a2, m_db2])
                buf_free["fatt"] = m_a3
                buf_free["fres"] = m_a3
                m_so = s.step("sync", lambda e, sem, t0=t0, ts=ts, tpar=tpar: e.dma_start(
                    out=o_y[t0 + ts * 128: t0 + (ts + 1) * 128, :],
                    in_=fout[tpar][:, :]).then_inc(sem, 16), waits=[m_a3], dma=1)
                buf_free[("fout", tpar)] = m_so
            buf_free["ffnc"] = m_tr_l2

        s.finish(block)
    return nc


def kernel(**inputs):
    xfull = np.ascontiguousarray(np.asarray(inputs["L0_attn_x"]).reshape(L, DM), np.float32)
    in_w = np.asarray(inputs["L0_attn_in_w"], np.float32)
    shared = {
        "x": xfull,
        "proj_w": np.ascontiguousarray(inputs["L0_attn_proj_w"], np.float32),
        "proj_b": np.ascontiguousarray(inputs["L0_attn_proj_b"], np.float32),
        "ln_g": np.ascontiguousarray(inputs["L0_ffn_ln_gamma"], np.float32),
        "ln_b": np.ascontiguousarray(inputs["L0_ffn_ln_beta"], np.float32),
        "fc1_q": np.ascontiguousarray(inputs["L0_ffn_fc1_q"], np.int32),
        "fc1_s": np.ascontiguousarray(inputs["L0_ffn_fc1_scales"], np.float32),
        "fc1_b": np.ascontiguousarray(inputs["L0_ffn_fc1_bias"], np.float32),
        "fc2_q": np.ascontiguousarray(inputs["L0_ffn_fc2_q"], np.int32),
        "fc2_s": np.ascontiguousarray(inputs["L0_ffn_fc2_scales"], np.float32),
        "fc2_b": np.ascontiguousarray(inputs["L0_ffn_fc2_bias"], np.float32),
    }
    nc = build_fused()
    in_maps = []
    for r in range(NCORES):
        d0, d1 = r * DSL, (r + 1) * DSL
        m = dict(shared)
        m["inw"] = np.ascontiguousarray(
            np.concatenate([in_w[:, :, d0:d1], in_w[:, :, DI + d0:DI + d1]], axis=2))
        m["conv_w"] = np.ascontiguousarray(inputs["L0_attn_conv_w"][:, d0:d1], np.float32)
        m["conv_b"] = np.ascontiguousarray(inputs["L0_attn_conv_b"][:, d0:d1], np.float32)
        m["xp_w"] = np.ascontiguousarray(inputs["L0_attn_xp_w"][:, d0:d1], np.float32)
        m["dt_w"] = np.ascontiguousarray(inputs["L0_attn_dt_w"][:, :, d0:d1], np.float32)
        m["dt_b"] = np.ascontiguousarray(inputs["L0_attn_dt_b"][:, d0:d1], np.float32)
        m["i_alog"] = np.ascontiguousarray(inputs["L0_attn_A_log"][:, d0:d1], np.float32)
        m["i_d"] = np.ascontiguousarray(inputs["L0_attn_D"][:, d0:d1], np.float32)
        m["ow"] = np.ascontiguousarray(inputs["L0_attn_out_w"][:, d0:d1], np.float32)
        m["xs"] = np.ascontiguousarray(xfull[r * TSL:(r + 1) * TSL])
        in_maps.append(m)
    res = run_bass_kernel_spmd(nc, in_maps, core_ids=list(range(NCORES)))
    y = np.concatenate([r["o_y"] for r in res.results], axis=0)
    return y.reshape(1, L, DM)
